# version 17
# baseline (speedup 1.0000x reference)
import contextlib
import numpy as np

import concourse.bass as bass
import concourse.mybir as mybir
from concourse.bass_utils import run_bass_kernel_spmd

F32 = mybir.dt.float32
BF16 = mybir.dt.bfloat16
U8 = mybir.dt.uint8
AF = mybir.ActivationFunctionType
ALU = mybir.AluOpType
AX = mybir.AxisListType

P = 128
T = 2048
DM = 2048
OF = 6144
IFF = 8192
TT = T // P
DC = DM // P
FC = 16
NQG = 4
NCORES = 8

ENGS = ("tensor", "vector", "scalar", "gpsimd", "sync")


def _build(phases=3.0, dbg=False):
    nc = bass.Bass("TRN2", target_bir_lowering=False)

    dx = nc.dram_tensor("x", [T, DM], F32, kind="ExternalInput")
    dwqkv = nc.dram_tensor("wqkv", [DM, OF], F32, kind="ExternalInput")
    dbqkv = nc.dram_tensor("bqkv", [1, OF], F32, kind="ExternalInput")
    dwout = nc.dram_tensor("wout", [DM, DM], F32, kind="ExternalInput")
    dbout = nc.dram_tensor("bout", [1, DM], F32, kind="ExternalInput")
    dwg = nc.dram_tensor("wg_t", [DM, IFF], U8, kind="ExternalInput")
    dwu = nc.dram_tensor("wu_t", [DM, IFF], U8, kind="ExternalInput")
    dwd = nc.dram_tensor("wd_t", [IFF, DM], U8, kind="ExternalInput")
    dgs = nc.dram_tensor("gs", [P, 64], F32, kind="ExternalInput")
    dus = nc.dram_tensor("us", [P, 64], F32, kind="ExternalInput")
    dds = nc.dram_tensor("ds_rep", [P, DM], F32, kind="ExternalInput")
    dcos = nc.dram_tensor("cos_t", [TT, P, 16], F32, kind="ExternalInput")
    dsin = nc.dram_tensor("sin_t", [TT, P, 16], F32, kind="ExternalInput")
    dident = nc.dram_tensor("ident", [P, P], F32, kind="ExternalInput")
    dmask = nc.dram_tensor("maskd", [P, P], F32, kind="ExternalInput")
    _skind = dict(kind="ExternalOutput") if dbg else {}
    q_scr = nc.dram_tensor("q_scr", [FC, P, T], BF16, **_skind)
    k_scr = nc.dram_tensor("k_scr", [FC, P, T], BF16, **_skind)
    v_scr = nc.dram_tensor("v_scr", [FC, TT, P, 130], BF16, **_skind)
    x1_scr = nc.dram_tensor("x1_scr", [T, DM], F32, **_skind)
    rb_scr = nc.dram_tensor("rb_scr", [2, 1024], BF16)
    if dbg:
        dxn2_dbg = nc.dram_tensor("xn2_dbg", [P, DC, 512], BF16, kind="ExternalOutput")
        dhsr_dbg = nc.dram_tensor("hsr_dbg", [P, 512], F32, kind="ExternalOutput")
        dh_dbg = nc.dram_tensor("h_dbg", [P, 64, 512], BF16, kind="ExternalOutput")
    dout = nc.dram_tensor("out", [T, DM], F32, kind="ExternalOutput")

    ops = {e: [] for e in ENGS}
    waited = {e: {} for e in ENGS}
    stack = contextlib.ExitStack()

    class Ctr:
        __slots__ = ("sem", "step", "n")

        def __init__(self, name, step):
            self.sem = stack.enter_context(nc.semaphore(name))
            self.step = step
            self.n = 0

    def wait(e, ctr, val=None):
        v = ctr.n if val is None else val
        if v <= 0 or waited[e].get(ctr, -1) >= v:
            return
        waited[e][ctr] = v
        sem = ctr.sem
        ops[e].append(lambda eng, sem=sem, v=v: eng.wait_ge(sem, v))

    def rwait(e, rel):
        if rel is not None:
            wait(e, rel[0], rel[1])

    def emit(e, fn, inc=None):
        if e == "vector":
            wait("vector", ev, ev.n)
            if inc is None:
                inc = ev
        if inc is None:
            ops[e].append(fn)
            return None
        inc.n += inc.step
        sem, st = inc.sem, inc.step
        ops[e].append(lambda eng, fn=fn, sem=sem, st=st: fn(eng).then_inc(sem, st))
        return inc.n

    def dma(e, out_ap, in_ap, ctr):
        return emit(e, lambda eng, o=out_ap, i=in_ap: eng.dma_start(o, i), ctr)

    def flush():
        with nc.Block() as blk:
            for e in ENGS:
                lst = ops[e]
                if not lst:
                    continue

                def runner(eng, lst=lst):
                    for f in lst:
                        f(eng)

                getattr(blk, e)(runner)
                ops[e] = []

    def barrier(ctrs):
        for e in ENGS:
            for c in ctrs:
                wait(e, c)

    mm = Ctr("mm", 1)
    ev = Ctr("ev", 1)
    es = Ctr("es", 1)
    pre = Ctr("pre", 16)
    preg = Ctr("preg", 16)
    ldx = [Ctr(f"ldx{i}", 16) for i in range(2)]
    ldw = [Ctr(f"ldw{i}", 16) for i in range(2)]
    stqk = [Ctr(f"stqk{i}", 16) for i in range(4)]
    stv = [Ctr(f"stv{i}", 16) for i in range(2)]
    ldq = [Ctr(f"ldq{i}", 16) for i in range(3)]
    ldk = [Ctr(f"ldk{i}", 16) for i in range(3)]
    ldv = [Ctr(f"ldv{i}", 16) for i in range(3)]
    ldwo = [Ctr(f"ldwo{i}", 16) for i in range(2)]
    ldxr = [Ctr(f"ldxr{i}", 16) for i in range(2)]
    sth2 = [Ctr(f"sth2{i}", 16) for i in range(2)]
    stx1 = [Ctr(f"stx1{i}", 16) for i in range(2)]
    ldx1 = [Ctr(f"ldx1{i}", 16) for i in range(2)]
    ldgu = [Ctr(f"ldgu{i}", 16) for i in range(2)]
    ldwd = [Ctr(f"ldwd{i}", 16) for i in range(2)]
    ldx1b = [Ctr(f"ldx1b{i}", 16) for i in range(2)]
    stout = [Ctr(f"stout{i}", 16) for i in range(2)]
    strb = [Ctr(f"strb{i}", 16) for i in range(2)]
    ldrb = [Ctr(f"ldrb{i}", 16) for i in range(2)]

    ones_sb = stack.enter_context(nc.sbuf_tensor("ones_sb", [P, P], BF16))
    identb = stack.enter_context(nc.sbuf_tensor("identb", [P, P], BF16))
    identf = stack.enter_context(nc.sbuf_tensor("identf", [P, P], F32))
    maskb = stack.enter_context(nc.sbuf_tensor("maskb", [P, P], BF16))
    bqkv_sb = stack.enter_context(nc.sbuf_tensor("bqkv_sb", [1, OF], BF16))
    bout_sb = stack.enter_context(nc.sbuf_tensor("bout_sb", [1, DM], BF16))
    cos_sb = stack.enter_context(nc.sbuf_tensor("cos_sb", [P, TT, 16], F32))
    sin_sb = stack.enter_context(nc.sbuf_tensor("sin_sb", [P, TT, 16], F32))
    gs_sb = stack.enter_context(nc.sbuf_tensor("gs_sb", [P, 64], F32))
    us_sb = stack.enter_context(nc.sbuf_tensor("us_sb", [P, 64], F32))
    ds_sb = stack.enter_context(nc.sbuf_tensor("ds_sb", [P, DM], F32))
    consts = stack.enter_context(nc.sbuf_tensor("consts", [P, 8], F32))
    C_INVD = consts[:, 0:1]
    C_EPS = consts[:, 1:2]
    C_NEG1 = consts[:, 2:3]
    C_8TH = consts[:, 3:4]
    C_NEG4 = consts[:, 4:5]
    C_NEG128 = consts[:, 5:6]
    stats = stack.enter_context(nc.sbuf_tensor("stats", [P, 32], F32))
    x1buf = stack.enter_context(nc.sbuf_tensor("x1buf", [P, 2, DM], F32))
    sq2 = stack.enter_context(nc.sbuf_tensor("sq2", [P, DM], F32))
    xn2tok = stack.enter_context(nc.sbuf_tensor("xn2tok", [P, 2, DM], BF16))

    emit("vector", lambda e: e.memset(ones_sb[:, :], 1.0), ev)
    for col, val in ((0, 1.0 / DM), (1, 1e-5), (2, -1.0), (3, 0.125),
                     (4, -4.0), (5, -128.0)):
        emit("vector", lambda e, c=col, v=val: e.memset(consts[:, c:c + 1], v), ev)
    EV_CONST = ev.n
    dma("gpsimd", identb[:, :], dident[:, :], preg)
    dma("gpsimd", maskb[:, :], dmask[:, :], preg)
    dma("gpsimd", bqkv_sb[:, :], dbqkv[:, :], preg)
    dma("gpsimd", bout_sb[:, :], dbout[:, :], preg)
    dma("sync", identf[:, :], dident[:, :], pre)
    dma("sync", cos_sb[:, :, :], dcos[:, :, :].transpose([1, 0, 2]), pre)
    dma("sync", sin_sb[:, :, :], dsin[:, :, :].transpose([1, 0, 2]), pre)
    dma("sync", gs_sb[:, :], dgs[:, :], pre)
    dma("sync", us_sb[:, :], dus[:, :], pre)
    dma("sync", ds_sb[:, :], dds[:, :], pre)
    PRE_TOTAL = pre.n
    PREG_TOTAL = preg.n
    for e in ("tensor", "vector", "scalar"):
        wait(e, pre, PRE_TOTAL)
        wait(e, preg, PREG_TOTAL)
        wait(e, ev, EV_CONST)

    def emit_ln(xb, sq, par, xn_out):
        b = 16 * par
        S1 = stats[:, b + 0:b + 1]
        S2 = stats[:, b + 1:b + 2]
        M = stats[:, b + 2:b + 3]
        MSQ = stats[:, b + 3:b + 4]
        MM2 = stats[:, b + 4:b + 5]
        VAR = stats[:, b + 5:b + 6]
        STD = stats[:, b + 6:b + 7]
        ISTD = stats[:, b + 7:b + 8]
        NMI = stats[:, b + 8:b + 9]
        emit("vector", lambda e: e.reduce_sum(S1, xb, axis=AX.X), ev)
        sq_done = emit("scalar", lambda e: e.activation(sq, xb, AF.Square), es)
        wait("vector", es, sq_done)
        emit("vector", lambda e: e.reduce_sum(S2, sq, axis=AX.X), ev)
        emit("vector", lambda e: e.tensor_single_scalar(M, S1, C_INVD, ALU.mult), ev)
        emit("vector", lambda e: e.tensor_single_scalar(MSQ, S2, C_INVD, ALU.mult), ev)
        emit("vector", lambda e: e.tensor_mul(MM2, M, M), ev)
        v_done = emit("vector", lambda e: e.tensor_sub(VAR, MSQ, MM2), ev)
        wait("scalar", ev, v_done)
        std_done = emit("scalar", lambda e: e.activation(STD, VAR, AF.Sqrt, bias=C_EPS), es)
        wait("vector", es, std_done)
        emit("vector", lambda e: e.reciprocal(ISTD, STD), ev)
        r_done = emit("vector", lambda e: e.scalar_tensor_tensor(
            NMI, M, C_NEG1, ISTD, ALU.mult, ALU.mult), ev)
        wait("scalar", ev, r_done)
        xn_done = emit("scalar", lambda e: e.activation(
            xn_out, xb, AF.Identity, bias=NMI, scale=ISTD), es)
        return xn_done

    x1_rel = [None, None]
    ln_state = {}
    xn2tok_rel = [None, None]

    def ln2_stats(tg, ts, st_wait=None):
        par = ts % 2
        tt = tg * 4 + ts
        rwait("sync", x1_rel[par])
        if st_wait is not None:
            wait("sync", stx1[0], st_wait[0])
            wait("sync", stx1[1], st_wait[1])
        lv = dma("sync", x1buf[:, par, :], x1_scr[tt * P:(tt + 1) * P, :],
                 ldx1[par])
        wait("vector", ldx1[par], lv)
        wait("scalar", ldx1[par], lv)
        rwait("scalar", xn2tok_rel[par])
        ln_state[(tg, ts)] = emit_ln(
            x1buf[:, par, :], sq2[:, :], par, xn2tok[:, par, :])
        x1_rel[par] = (es, ln_state[(tg, ts)])

    with contextlib.ExitStack() as sa:
        xnf = sa.enter_context(nc.sbuf_tensor("xnf", [P, DC, T], BF16))
        xbuf = sa.enter_context(nc.sbuf_tensor("xbuf", [P, 2, DM], F32))
        sqscr = sa.enter_context(nc.sbuf_tensor("sqscr", [P, DM], F32))
        xntok = sa.enter_context(nc.sbuf_tensor("xntok", [P, 2, DM], BF16))
        wbuf = sa.enter_context(nc.sbuf_tensor("wbuf", [P, 2, DC, 512], BF16))
        ropest = sa.enter_context(nc.sbuf_tensor("ropest", [P, 4, 512], BF16))
        scrA = sa.enter_context(nc.sbuf_tensor("scrA", [P, 8, 16], F32))
        scrB = sa.enter_context(nc.sbuf_tensor("scrB", [P, 8, 16], F32))
        qk2 = sa.enter_context(nc.sbuf_tensor("qk2", [P, 4, P], BF16))
        vstage = sa.enter_context(nc.sbuf_tensor("vstage", [P, 2, 4, 130], BF16))
        pq = [sa.enter_context(nc.psum_tensor(f"pq{i}", [P, 512], F32)) for i in range(4)]
        psT = [sa.enter_context(nc.psum_tensor(f"psT{i}", [P, 1024], BF16)) for i in range(2)]

        for sl in range(2):
            emit("vector", lambda e, s=sl: e.memset(vstage[:, s, :, :], 1.0), ev)

        def load_wgrp(og, slot, rel):
            rwait("gpsimd", rel)
            for dcc in range(DC):
                dma("gpsimd", wbuf[:, slot, dcc, :],
                    dwqkv[dcc * P:(dcc + 1) * P, og * 512:(og + 1) * 512], ldw[slot])
            return ldw[slot].n

        wld = [0, 0]
        wrel = [None, None]
        wld[0] = load_wgrp(0, 0, None)
        wld[1] = load_wgrp(1, 1, None)
        pq_rel = [None, None, None, None]
        qk2_rel = [None] * 4
        ropest_rel = [None, None, None, None]
        vstage_rel = [None, None]
        xbuf_rel = [None, None]
        psT_rel = [None, None]
        xntok_rel = [None, None]
        xnf_cp = [0] * TT
        pend = [None]
        jidx = [0]

        def emit_ln_tile(tt):
            par = tt % 2
            rwait("sync", xbuf_rel[par])
            lv = dma("sync", xbuf[:, par, :], dx[tt * P:(tt + 1) * P, :], ldx[par])
            wait("vector", ldx[par], lv)
            wait("scalar", ldx[par], lv)
            rwait("scalar", xntok_rel[par])
            xn_done = emit_ln(xbuf[:, par, :], sqscr[:, :], par, xntok[:, par, :])
            xbuf_rel[par] = (es, xn_done)
            return xn_done

        def emit_ln_tp(tt, xn_done):
            par = tt % 2
            wait("tensor", es, xn_done)
            for g in range(2):
                sp = g % 2
                rwait("tensor", psT_rel[sp])
                mv = 0
                for c in range(8):
                    dc = g * 8 + c
                    mv = emit("tensor", lambda e, d=dc, c=c, s=sp, p=par: e.transpose(
                        psT[s][:, c * P:(c + 1) * P],
                        xntok[:, p, d * P:(d + 1) * P], identb[:, :]), mm)
                wait("vector", mm, mv)
                dst = xnf[:, g * 8:(g + 1) * 8, tt * P:(tt + 1) * P]
                cv = emit("vector", lambda e, s=sp, dst=dst: e.tensor_copy(
                    dst, psT[s][:, :].rearrange("p (c f) -> p c f", c=8)), ev)
                psT_rel[sp] = (ev, cv)
            xntok_rel[par] = (mm, mm.n)
            xnf_cp[tt] = ev.n

        def mm_job(og, tt):
            par = jidx[0] % 4
            jidx[0] += 1
            wslot = og % 2
            wait("tensor", ldw[wslot], wld[wslot])
            wait("tensor", ev, xnf_cp[tt])
            for _r in (pq_rel[par] or []):
                rwait("tensor", _r)
            for dcc in range(DC):
                emit("tensor", lambda e, d=dcc, t=tt, w=wslot, p=par: e.matmul(
                    pq[p][:, :], xnf[:, d, t * P:(t + 1) * P], wbuf[:, w, d, :],
                    start=(d == 0), stop=False), None)
            mm_tile = emit("tensor", lambda e, o=og, p=par: e.matmul(
                pq[p][:, :], ones_sb[0:1, :], bqkv_sb[0:1, o * 512:(o + 1) * 512],
                start=False, stop=True), mm)
            if tt == TT - 1:
                wrel[wslot] = (mm, mm_tile)
            if pend[0] is not None:
                pend[0]()
                pend[0] = None
            if og < 8:
                isq = og < 4
                scr = q_scr if isq else k_scr
                fcbase = (og % 4) * 4
                pr = pq[par].rearrange("p (h d) -> p h d", h=8)
                st = ropest[:, par, :].rearrange("p (h d) -> p h d", h=8)
                cosb = cos_sb[:, tt, :][:, None, :].broadcast_to((P, 8, 16))
                sinb = sin_sb[:, tt, :][:, None, :].broadcast_to((P, 8, 16))
                wait("vector", mm, mm_tile)
                rwait("vector", ropest_rel[par])
                emit("vector", lambda e, pr=pr, c=cosb: e.tensor_mul(
                    scrA[:, :, :], pr[:, :, 0:16], c), None)
                emit("vector", lambda e, pr=pr, s=sinb: e.tensor_mul(
                    scrB[:, :, :], pr[:, :, 16:32], s), None)
                emit("vector", lambda e, st=st: e.tensor_sub(
                    st[:, :, 0:16], scrA[:, :, :], scrB[:, :, :]), None)
                emit("vector", lambda e, pr=pr, s=sinb: e.tensor_mul(
                    scrA[:, :, :], pr[:, :, 0:16], s), None)
                emit("vector", lambda e, pr=pr, c=cosb: e.tensor_mul(
                    scrB[:, :, :], pr[:, :, 16:32], c), None)
                emit("vector", lambda e, st=st: e.tensor_add(
                    st[:, :, 16:32], scrA[:, :, :], scrB[:, :, :]), None)
                rope_done = emit("vector", lambda e, st=st, pr=pr: e.tensor_copy(
                    st[:, :, 32:64], pr[:, :, 32:64]), ev)
                sp_cp = None
                pq_rel[par] = [(ev, rope_done)]

                def tp(par=par, tt=tt, scr=scr, fcbase=fcbase,
                       rope_done=rope_done):
                    sp = jidx[0] % 2
                    wait("tensor", ev, rope_done)
                    rwait("tensor", psT_rel[sp])
                    mv = 0
                    for c in range(4):
                        mv = emit("tensor", lambda e, c=c, s=sp, p=par: e.transpose(
                            psT[s][:, c * P:(c + 1) * P],
                            ropest[:, p, c * P:(c + 1) * P], identb[:, :]), mm)
                    wait("vector", mm, mv)
                    for c in range(4):
                        rwait("vector", qk2_rel[c])
                    cv = emit("vector", lambda e, s=sp: e.tensor_copy(
                        qk2[:, :, :], psT[s][:, 0:512].rearrange(
                            "p (c f) -> p c f", c=4)), ev)
                    psT_rel[sp] = (ev, cv)
                    wait("sync", ev, cv)
                    for c in range(4):
                        sv = dma("sync", scr[fcbase + c, :, tt * P:(tt + 1) * P],
                                 qk2[:, c, :], stqk[c])
                        qk2_rel[c] = (stqk[c], sv)
                    ropest_rel[par] = (mm, mm.n)
                pend[0] = tp
            else:
                hpbase = (og - 8) * 4
                vsl = jidx[0] % 2
                pr = pq[par].rearrange("p (h x d) -> p h x d", h=4, x=2)
                dstv = vstage[:, vsl, :, :].rearrange(
                    "p h (x e) -> p h x e", x=2)[:, :, :, 0:64]
                wait("vector", mm, mm_tile)
                rwait("vector", vstage_rel[vsl])
                cv = emit("vector", lambda e, pr=pr, d=dstv: e.tensor_copy(
                    d, pr[:, :, :, 0:64]), ev)
                pq_rel[par] = [(ev, cv)]
                wait("sync", ev, cv)
                last = 0
                for c in range(4):
                    last = dma("sync", v_scr[hpbase + c, tt, :, :],
                               vstage[:, vsl, c, :], stv[vsl])
                vstage_rel[vsl] = (stv[vsl], last)
            if tt == TT - 1 and og + 2 < 12:
                wld[wslot] = load_wgrp(og + 2, wslot, wrel[wslot])

        xn_pend = emit_ln_tile(0)
        for tt in range(TT):
            emit_ln_tp(tt, xn_pend)
            if tt + 1 < TT:
                xn_pend = emit_ln_tile(tt + 1)
            mm_job(0, tt)
            mm_job(1, tt)
        for og in range(2, 12):
            for tt in range(TT):
                mm_job(og, tt)
        if pend[0] is not None:
            pend[0]()
            pend[0] = None
        A_CTRS = [mm, ev, es] + stqk + stv
        flush()

    if phases < 2:
        stack.close()
        return nc
    with contextlib.ExitStack() as sb:
        qbuf = sb.enter_context(nc.sbuf_tensor("qbuf", [P, 3, 512], BF16))
        kbuf = sb.enter_context(nc.sbuf_tensor("kbuf", [P, 3, T], BF16))
        vbuf = sb.enter_context(nc.sbuf_tensor("vbuf", [P, 3, TT, 130], BF16))
        ebAB = sb.enter_context(nc.sbuf_tensor("ebAB", [P, 4, 1024], BF16))
        rdbuf = sb.enter_context(nc.sbuf_tensor("rdbuf", [P, 2, 512], BF16))
        repsb = sb.enter_context(nc.sbuf_tensor("repsb", [P, 2, 512], BF16))
        stg2 = sb.enter_context(nc.sbuf_tensor("stg2", [P, 2, 512], BF16))
        oc = sb.enter_context(nc.sbuf_tensor("oc", [P, 2, 2, 512], F32))
        attnout = sb.enter_context(nc.sbuf_tensor("attnout", [P, 2, FC, 512], BF16))
        wobuf = sb.enter_context(nc.sbuf_tensor("wobuf", [P, 2, DC, 512], BF16))
        xresb = sb.enter_context(nc.sbuf_tensor("xresb", [P, 2, 512], F32))
        o1st = sb.enter_context(nc.sbuf_tensor("o1st", [P, 2, 512], F32))
        sAB = [sb.enter_context(nc.psum_tensor(f"sAB{i}", [P, 1024], F32)) for i in range(2)]
        oA = sb.enter_context(nc.psum_tensor("oA", [P, 512], F32))
        oB = sb.enter_context(nc.psum_tensor("oB", [P, 512], F32))
        pw = [sb.enter_context(nc.psum_tensor(f"pw{i}", [P, 512], F32)) for i in range(2)]

        barrier(A_CTRS)

        q_rel = [None, None, None]
        k_rel = [None, None, None]
        v_rel = [None, None, None]
        kld = [0, 0, 0]
        vld = [0, 0, 0]
        qld = [0, 0, 0]
        sAB_rel = [None, None]
        eb_rel = [None, None, None, None]
        o_rel = [None]
        rb_rel = [None, None]
        rd_rel = [None]
        oc_rel = [None, None]
        att_pend = [None]
        stg2_rel = [None, None]
        wout_psum_rel = [None, None]
        wo_rel = [None, None]
        wo_pref = [0, 0]
        xres_rel = [None, None]
        o1st_rel = [None, None]
        units_done = {-2: 0, -1: 0}
        ux1_done = {}
        wout_units = []
        wo_ld = [0, 0]

        def load_job(g):
            jqg, jfc = g // FC, g % FC
            jnkt = 4 * (jqg + 1)
            s = g % 3
            rwait("sync", q_rel[s])
            qld[s] = dma("sync", qbuf[:, s, :],
                         q_scr[jfc, :, jqg * 512:(jqg + 1) * 512], ldq[s])
            rwait("sync", k_rel[s])
            kld[s] = dma("sync", kbuf[:, s, 0:jnkt * P],
                         k_scr[jfc, :, 0:jnkt * P], ldk[s])
            rwait("sync", v_rel[s])
            h1 = (jnkt + 1) // 2
            dma("sync", vbuf[:, s, 0:h1, :],
                v_scr[jfc, 0:h1, :, :].transpose([1, 0, 2]), ldv[s])
            vld[s] = dma("sync", vbuf[:, s, h1:jnkt, :],
                         v_scr[jfc, h1:jnkt, :, :].transpose([1, 0, 2]), ldv[s])

        load_job(0)
        load_job(1)
        for qg in range(NQG):
            nkt = 4 * (qg + 1)
            for fc in range(FC):
                g = qg * FC + fc
                qsl = g % 3
                if g + 2 < NQG * FC:
                    load_job(g + 2)
                qv = qld[qsl]
                qflat = qbuf[:, qsl, :]
                av_q = []
                sm = {}
                for kt in range(nkt):
                    ks = qsl
                    ssl = kt % 2
                    diag = kt >= 4 * qg
                    off = P * (kt - 4 * qg) if diag else 0
                    wait("tensor", ldk[ks], kld[ks])
                    wait("tensor", ldq[qsl], qv)
                    rwait("tensor", sAB_rel[ssl])
                    kap = kbuf[:, ks, kt * P:(kt + 1) * P]
                    if diag:
                        emit("tensor", lambda e, k=kap, s=ssl, o=off, q=qflat: e.matmul(
                            sAB[s][:, o:512], k[0:64, :], q[0:64, o:512],
                            start=True, stop=False, skip_group_check=True), None)
                        emit("tensor", lambda e, s=ssl, o=off: e.matmul(
                            sAB[s][:, o:o + P], identb[:, :], maskb[:, :],
                            start=False, stop=True, skip_group_check=True), None)
                        emit("tensor", lambda e, k=kap, s=ssl, o=off, q=qflat: e.matmul(
                            sAB[s][:, 512 + o:1024], k[64:128, :],
                            q[64:128, o:512],
                            start=True, stop=False, skip_group_check=True), None)
                        sm[kt] = emit("tensor", lambda e, s=ssl, o=off: e.matmul(
                            sAB[s][:, 512 + o:512 + o + P], identb[:, :], maskb[:, :],
                            start=False, stop=True, skip_group_check=True), mm)
                    else:
                        emit("tensor", lambda e, k=kap, s=ssl, q=qflat: e.matmul(
                            sAB[s][:, 0:512], k[0:64, :], q[0:64, :],
                            start=True, stop=True, skip_group_check=True), None)
                        sm[kt] = emit("tensor", lambda e, k=kap, s=ssl, q=qflat: e.matmul(
                            sAB[s][:, 512:1024], k[64:128, :], q[64:128, :],
                            start=True, stop=True, skip_group_check=True), mm)
                    if kt % 2 == 0:
                        continue
                    for k2 in (kt - 1, kt):
                        s2 = k2 % 2
                        d2 = k2 >= 4 * qg
                        o2 = P * (k2 - 4 * qg) if d2 else 0
                        e4 = k2 % 4
                        wait("scalar", mm, sm[k2])
                        rwait("scalar", eb_rel[e4])
                        if d2:
                            emit("scalar", lambda e, s=s2, b=e4, o=o2: e.activation(
                                ebAB[:, b, o:512], sAB[s][:, o:512], AF.Exp,
                                bias=C_NEG4, scale=C_8TH), es)
                            exAB = emit("scalar", lambda e, s=s2, b=e4, o=o2: e.activation(
                                ebAB[:, b, 512 + o:1024], sAB[s][:, 512 + o:1024],
                                AF.Exp, bias=C_NEG4, scale=C_8TH), es)
                        else:
                            exAB = emit("scalar", lambda e, s=s2, b=e4: e.activation(
                                ebAB[:, b, 0:1024], sAB[s][:, 0:1024], AF.Exp,
                                bias=C_NEG4, scale=C_8TH), es)
                        sAB_rel[s2] = (es, exAB)

                        def av_emit(kt=k2, off=o2, exAB=exAB, vs=qsl,
                                    first=(k2 == 0), last=(k2 == nkt - 1)):
                            ebs = kt % 4
                            wait("tensor", es, exAB)
                            wait("tensor", ldv[vs], vld[vs])
                            if first:
                                rwait("tensor", o_rel[0])
                            emit("tensor", lambda e, kt=kt, vs=vs, ebs=ebs, off=off,
                                 first=first, last=last: e.matmul(
                                oA[0:65, off:512], vbuf[:, vs, kt, 0:65],
                                ebAB[:, ebs, off:512],
                                start=first, stop=last, skip_group_check=True), None)
                            r2 = emit("tensor", lambda e, kt=kt, vs=vs, ebs=ebs, off=off,
                                      first=first, last=last: e.matmul(
                                oB[0:65, off:512], vbuf[:, vs, kt, 65:130],
                                ebAB[:, ebs, 512 + off:1024],
                                start=first, stop=last, skip_group_check=True), mm)
                            eb_rel[ebs] = (mm, r2)
                            return r2
                        av_q.append(av_emit)
                    while len(av_q) > 2:
                        av_q.pop(0)()
                    if kt == 1 and att_pend[0] is not None:
                        att_pend[0]()
                        att_pend[0] = None
                    if kt == 3 and wout_units and fc >= 4:
                        wout_units.pop(0)()
                        if fc in (6, 8, 10, 12) and wout_units:
                            wout_units.pop(0)()
                while av_q:
                    av_q.pop(0)()
                av_last = mm.n
                q_rel[qsl] = (mm, av_last)
                k_rel[qsl] = (mm, av_last)
                v_rel[qsl] = (mm, av_last)
                ocs = fc % 2
                wait("scalar", mm, av_last)
                rwait("scalar", oc_rel[ocs])
                emit("scalar", lambda e, s=ocs: e.activation(
                    oc[0:65, s, 0, :], oA[0:65, :], AF.Identity), None)
                oc_done = emit("scalar", lambda e, s=ocs: e.activation(
                    oc[0:65, s, 1, :], oB[0:65, :], AF.Identity), es)
                o_rel[0] = (es, oc_done)

                def attnorm(fc=fc, ocs=ocs, oc_done=oc_done, aslot=qg % 2,
                            slot_free=units_done[qg - 2]):
                    rsl = fc % 2
                    wait("vector", es, oc_done)
                    rwait("vector", rb_rel[rsl])
                    rwait("vector", rd_rel[0])

                    def _recip(e, dst, src):
                        with nc.allow_low_precision(
                                reason="softmax denominator reciprocal stored "
                                       "bf16 as a broadcast operand"):
                            return e.reciprocal(dst, src)
                    emit("vector", lambda e, s=ocs: _recip(
                        e, rdbuf[64:65, 0, :], oc[64:65, s, 0, :]), ev)
                    rb = emit("vector", lambda e, s=ocs: _recip(
                        e, rdbuf[64:65, 1, :], oc[64:65, s, 1, :]), ev)
                    wait("gpsimd", ev, rb)
                    rwait("gpsimd", rb_rel[rsl])
                    sv1 = dma("gpsimd", rb_scr[rsl:rsl + 1, :],
                              rdbuf[64:65, 0:2, :].rearrange("p a b -> p (a b)"),
                              strb[rsl])
                    rd_rel[0] = (strb[rsl], sv1)
                    wait("gpsimd", strb[rsl], sv1)
                    bv = dma("gpsimd", repsb[0:64, :, :],
                             rb_scr[rsl:rsl + 1, :].broadcast_to((64, 1024)),
                             ldrb[rsl])
                    rb_rel[rsl] = (ldrb[rsl], bv)
                    wait("vector", ldrb[rsl], bv)
                    wait("vector", mm, slot_free)
                    emit("vector", lambda e, f=fc, a=aslot, s=ocs: e.tensor_mul(
                        attnout[0:64, a, f, :], oc[0:64, s, 0, :],
                        repsb[0:64, 0, :]), ev)
                    h2sl = fc % 2
                    rwait("vector", stg2_rel[h2sl])
                    mub = emit("vector", lambda e, s=h2sl, c=ocs: e.tensor_mul(
                        stg2[0:64, s, :], oc[0:64, c, 1, :], repsb[0:64, 1, :]), ev)
                    oc_rel[ocs] = (ev, mub)
                    wait("sync", ev, mub)
                    wait("sync", mm, slot_free)
                    sv = dma("sync", attnout[64:128, aslot, fc, :],
                             stg2[0:64, h2sl, :], sth2[h2sl])
                    stg2_rel[h2sl] = (sth2[h2sl], sv)
                att_pend[0] = attnorm
            for og_p in range(2):
                rwait("gpsimd", wo_rel[og_p])
                for dcc in range(DC):
                    dma("gpsimd", wobuf[:, og_p, dcc, :],
                        dwout[dcc * P:(dcc + 1) * P,
                              og_p * 512:(og_p + 1) * 512], ldwo[og_p])
                wo_pref[og_p] = ldwo[og_p].n
            if att_pend[0] is not None:
                att_pend[0]()
                att_pend[0] = None
            att_ev = ev.n
            att_h2 = [sth2[0].n, sth2[1].n]

            def wout_unit(qg=qg, u=None, att_ev=att_ev, att_h2=att_h2,
                          aslot=qg % 2):
                og, ts = u // 4, u % 4
                wsl = og % 2
                if og < 2:
                    wov = wo_pref[wsl]
                else:
                    wov = wo_ld[wsl]
                psl = ts % 2
                pwt = pw[psl][:, :]
                wait("tensor", ldwo[wsl], wov)
                wait("tensor", ev, att_ev)
                wait("tensor", sth2[0], att_h2[0])
                wait("tensor", sth2[1], att_h2[1])
                rwait("tensor", wout_psum_rel[psl])
                xsl = u % 2
                rwait("gpsimd", xres_rel[xsl])
                xr = dma("gpsimd", xresb[:, xsl, :],
                         dx[qg * 512 + ts * P: qg * 512 + (ts + 1) * P,
                            og * 512:(og + 1) * 512], ldxr[xsl])
                for dcc in range(DC):
                    emit("tensor", lambda e, d=dcc, t=ts, w=wsl, ps=pwt, a=aslot:
                         e.matmul(ps, attnout[:, a, d, t * P:(t + 1) * P],
                                  wobuf[:, w, d, :],
                                  start=(d == 0), stop=False), None)
                wo_mm = emit("tensor", lambda e, o=og, ps=pwt: e.matmul(
                    ps, ones_sb[0:1, :], bout_sb[0:1, o * 512:(o + 1) * 512],
                    start=False, stop=True), mm)
                osl = u % 2
                wait("vector", mm, wo_mm)
                wait("vector", ldxr[xsl], xr)
                rwait("vector", o1st_rel[osl])
                o1v = emit("vector", lambda e, ps=pwt, x=xsl, s=osl: e.tensor_add(
                    o1st[:, s, :], ps, xresb[:, x, :]), ev)
                wout_psum_rel[psl] = (ev, o1v)
                xres_rel[xsl] = (ev, o1v)
                wait("gpsimd", ev, o1v)
                sv = dma("gpsimd",
                         x1_scr[qg * 512 + ts * P: qg * 512 + (ts + 1) * P,
                                og * 512:(og + 1) * 512],
                         o1st[:, osl, :], stx1[osl])
                o1st_rel[osl] = (stx1[osl], sv)
                if u == 3 or u == 7:
                    og2 = 2 if u == 3 else 3
                    wsl2 = og2 % 2
                    wo_rel[wsl2] = (mm, mm.n)
                    rwait("gpsimd", wo_rel[wsl2])
                    for dcc in range(DC):
                        dma("gpsimd", wobuf[:, wsl2, dcc, :],
                            dwout[dcc * P:(dcc + 1) * P, og2 * 512:(og2 + 1) * 512],
                            ldwo[wsl2])
                    wo_ld[wsl2] = ldwo[wsl2].n
                if u == 11:
                    wo_rel[0] = (mm, mm.n)
                if u == 15:
                    wo_rel[1] = (mm, mm.n)
                    units_done[qg] = mm.n
                    ux1_done[qg] = (stx1[0].n, stx1[1].n)

            wout_units = [
                (lambda u=u, f=wout_unit: f(u=u)) for u in range(16)]
            if qg == NQG - 1:
                while wout_units:
                    wout_units.pop(0)()
        ln2_stats(0, 0, st_wait=ux1_done[0])
        ln2_stats(0, 1, st_wait=ux1_done[0])
        B_CTRS = [mm, ev, es] + stx1 + sth2
        flush()

    if phases < 2.1:
        stack.close()
        return nc
    with contextlib.ExitStack() as sc:
        xn2f = sc.enter_context(nc.sbuf_tensor("xn2f", [P, DC, 512], BF16))
        wgub = sc.enter_context(nc.sbuf_tensor("wgub", [P, 2, DC, 512], BF16))
        wdb = wgub
        h_sb = sc.enter_context(nc.sbuf_tensor("h_sb", [P, 64, 512], BF16))
        hsr = sc.enter_context(nc.sbuf_tensor("hsr", [P, 512], F32))
        hs2r = sc.enter_context(nc.sbuf_tensor("hs2r", [P, 512], F32))
        hs2c = sc.enter_context(nc.sbuf_tensor("hs2c", [P, 4], F32))
        gl = sc.enter_context(nc.sbuf_tensor("gl", [P, 2, 512], F32))
        sil = sc.enter_context(nc.sbuf_tensor("sil", [P, 4, 512], F32))
        usb = sc.enter_context(nc.sbuf_tensor("usb", [P, 2, 512], F32))
        x1b = sc.enter_context(nc.sbuf_tensor("x1b", [P, 2, 512], F32))
        outst = sc.enter_context(nc.sbuf_tensor("outst", [P, 2, 512], F32))
        pg = [sc.enter_context(nc.psum_tensor(f"pg{i}", [P, 512], F32)) for i in range(2)]
        pu = [sc.enter_context(nc.psum_tensor(f"pu{i}", [P, 512], F32)) for i in range(2)]
        psh = sc.enter_context(nc.psum_tensor("psh", [P, 512], F32))
        ptp = [sc.enter_context(nc.psum_tensor(f"ptp{i}", [P, 1024], BF16)) for i in range(2)]
        pht = sc.enter_context(nc.psum_tensor("pht", [P, 512], F32))

        barrier(B_CTRS)

        psT_rel2 = [None, None]
        gu_rel = [None, None]
        gu_ld = [0, 0]
        wd_rel = [None, None]
        psum_rel = [None, None, None, None]
        gl_rel = [None, None]
        sil_rel = [None, None, None, None]
        usb_rel = [None, None]
        h_pe_done = 0
        pht_rel = None
        outst_rel = [None, None]
        x1b_rel = [None, None]
        hs2_es_prev = [0]

        guf_prev = [0]
        lnhs = {}

        def ln2_tp(tg, ts):
            par = ts % 2
            wait("vector", mm, guf_prev[0])
            wait("tensor", es, ln_state[(tg, ts)])
            for g in range(2):
                sp = g % 2
                rwait("tensor", psT_rel2[sp])
                mv = 0
                for c in range(8):
                    dcc = g * 8 + c
                    mv = emit("tensor", lambda e, d=dcc, c=c, s=sp, p=par: e.transpose(
                        ptp[s][:, c * P:(c + 1) * P],
                        xn2tok[:, p, d * P:(d + 1) * P], identb[:, :]), mm)
                wait("vector", mm, mv)
                dst = xn2f[:, g * 8:(g + 1) * 8, ts * P:(ts + 1) * P]
                cv = emit("vector", lambda e, s=sp, dst=dst: e.tensor_copy(
                    dst, ptp[s][:, :].rearrange("p (c f) -> p c f", c=8)), ev)
                psT_rel2[sp] = (ev, cv)
            xn2tok_rel[par] = (mm, mm.n)

        def emit_hs(tg):
            xn2_ev = ev.n
            wait("tensor", ev, xn2_ev)
            wait("tensor", es, hs2_es_prev[0])
            for dcc in range(DC):
                hs_mm = emit("tensor", lambda e, d=dcc: e.matmul(
                    psh[:, :], ones_sb[:, :], xn2f[:, d, :],
                    start=(d == 0), stop=(d == DC - 1)),
                    mm if dcc == DC - 1 else None)
            wait("scalar", mm, hs_mm)
            wait("scalar", ev, ev.n)
            hs_es = emit("scalar", lambda e: e.activation(
                hsr[:, :], psh[:, :], AF.Identity, scale=C_NEG128), es)
            lnhs[tg] = (xn2_ev, hs_es)

        ln2_tp(0, 0)
        ln2_stats(0, 2)
        ln2_tp(0, 1)
        ln2_stats(0, 3)
        ln2_tp(0, 2)
        ln2_tp(0, 3)
        emit_hs(0)
        gu_prefetched = [False]
        for tg in range(NQG):
            xn2_ev, hs_es = lnhs[tg]
            hs2_es_prev[0] = hs_es
            wait("vector", mm, h_pe_done)

            def load_gu(kind, ib, slot, rel):
                rwait("gpsimd", rel)
                src = dwg if kind == 0 else dwu
                for dcc in range(DC):
                    dma("gpsimd", wgub[:, slot, dcc, :],
                        src[dcc * P:(dcc + 1) * P, ib * 512:(ib + 1) * 512], ldgu[slot])
                return ldgu[slot].n

            seq = [(k, ib) for ib in range(16) for k in (0, 1)]
            sil_done = [0, 0, 0, 0]
            if not gu_prefetched[0]:
                gu_ld[0] = load_gu(*seq[0], 0, gu_rel[0])
            gu_prefetched[0] = False
            gu_ld[1] = load_gu(*seq[1], 1, gu_rel[1])
            gu_use = [0, 0]
            for si, (kind, ib) in enumerate(seq):
                slot = si % 2
                wait("tensor", ldgu[slot], gu_ld[slot])
                for icl in range(4):
                    ic = ib * 4 + icl
                    ppar = icl % 2
                    pslot = ppar if kind == 0 else 2 + ppar
                    ps = pg[ppar] if kind == 0 else pu[ppar]
                    rwait("tensor", psum_rel[pslot])
                    for dcc in range(DC):
                        mmv = emit("tensor", lambda e, d=dcc, i=icl, s=slot, ps=ps: e.matmul(
                            ps[:, :], wgub[:, s, d, i * P:(i + 1) * P], xn2f[:, d, :],
                            start=(d == 0), stop=(d == DC - 1)),
                            mm if dcc == DC - 1 else None)
                    if kind == 0:
                        wait("vector", mm, mmv)
                        wait("vector", es, hs_es)
                        rwait("vector", gl_rel[ppar])
                        gv = emit("vector", lambda e, ps=ps, p=ppar: e.tensor_add(
                            gl[:, p, :], ps[:, :], hsr[:, :]), ev)
                        psum_rel[pslot] = (ev, gv)
                        wait("scalar", ev, gv)
                        rwait("scalar", sil_rel[icl])
                        sv = emit("scalar", lambda e, p=ppar, il=icl, ic=ic: e.activation(
                            sil[:, il, :], gl[:, p, :], AF.Silu,
                            scale=gs_sb[:, ic:ic + 1]), es)
                        gl_rel[ppar] = (es, sv)
                        sil_done[icl] = sv
                    else:
                        wait("vector", mm, mmv)
                        rwait("vector", usb_rel[ppar])
                        uv = emit("vector", lambda e, ps=ps, p=ppar: e.tensor_add(
                            usb[:, p, :], ps[:, :], hsr[:, :]), ev)
                        psum_rel[pslot] = (ev, uv)
                        wait("vector", es, sil_done[icl])
                        hv = emit("vector", lambda e, p=ppar, il=icl, ic=ic: e.scalar_tensor_tensor(
                            h_sb[:, ic, :], sil[:, il, :], us_sb[:, ic:ic + 1],
                            usb[:, p, :], ALU.mult, ALU.mult), ev)
                        sil_rel[icl] = (ev, hv)
                        usb_rel[ppar] = (ev, hv)
                gu_use[slot] = mm.n
                if si + 2 < len(seq):
                    gu_rel[slot] = (mm, mm.n)
                    gu_ld[slot] = load_gu(*seq[si + 2], slot, gu_rel[slot])
            h_ev = ev.n
            guf_mm = mm.n
            guf_prev[0] = guf_mm
            if dbg and tg == 0:
                wait("sync", ev, h_ev)
                wait("sync", es, hs_es)
                dma("sync", dxn2_dbg[:, :, :], xn2f[:, :, :], pre)
                dma("sync", dhsr_dbg[:, :], hsr[:, :], pre)
                dma("sync", dh_dbg[:, :, :], h_sb[:, :, :], pre)
                wait("vector", pre, pre.n)
                wait("tensor", pre, pre.n)
                wait("scalar", pre, pre.n)
            wait("tensor", ev, h_ev)
            wait("tensor", es, hs_es)
            for ic in range(64):
                hs2_mm = emit("tensor", lambda e, i=ic: e.matmul(
                    psh[:, :], ones_sb[:, :], h_sb[:, i, :],
                    start=(i == 0), stop=(i == 63)),
                    mm if ic == 63 else None)
            wait("scalar", mm, hs2_mm)
            wait("scalar", ev, ev.n)
            hs2_es = emit("scalar", lambda e: e.activation(
                hs2r[:, :], psh[:, :], AF.Identity, scale=C_NEG128), es)
            hs2_es_prev[0] = hs2_es
            wait("tensor", es, hs2_es)
            rwait("tensor", pht_rel)
            mv = 0
            for ts in range(4):
                mv = emit("tensor", lambda e, t=ts: e.transpose(
                    pht[:, t * P:(t + 1) * P], hs2r[:, t * P:(t + 1) * P],
                    identf[:, :]), mm)
            wait("vector", mm, mv)
            cpv = emit("vector", lambda e: e.tensor_copy(
                hs2c[:, 0:4],
                bass.AP(pht.tensor if hasattr(pht, 'tensor') else pht, 0,
                        [[512, P], [P, 4], [1, 1]])), ev)
            pht_rel = (ev, cpv)
            hs2c_ev = ev.n
            if tg + 1 < NQG:
                ln2_stats(tg + 1, 0)
                ln2_stats(tg + 1, 1)
            for hg in range(4):
                for q4 in range(4):
                    wsl = (hg * 4 + q4) % 2
                    rwait("gpsimd", wd_rel[wsl])
                    wait("gpsimd", mm, gu_use[wsl])
                    for icl in range(16):
                        i = q4 * 16 + icl
                        dma("gpsimd", wdb[:, wsl, icl, :],
                            dwd[i * P:(i + 1) * P, hg * 512:(hg + 1) * 512], ldwd[wsl])
                    wv = ldwd[wsl].n
                    wait("tensor", ldwd[wsl], wv)
                    q4last = 0
                    for ts in range(4):
                        pd = (pg + pu)[ts]
                        if q4 == 0:
                            rwait("tensor", psum_rel[ts])
                        for icl in range(16):
                            i = q4 * 16 + icl
                            q4last = emit("tensor",
                                lambda e, i=i, il=icl, w=wsl, t=ts, pd=pd: e.matmul(
                                    pd[:, :], h_sb[:, i, t * P:(t + 1) * P],
                                    wdb[:, w, il, :],
                                    start=(i == 0), stop=(i == 63)),
                                mm if (ts == 3 and icl == 15) else None)
                    wd_rel[wsl] = (mm, q4last)
                    if hg == 3 and q4 == 3 and tg + 1 < NQG:
                        gu_ld[0] = load_gu(0, 0, 0, wd_rel[0])
                        gu_prefetched[0] = True
                if hg == 3 and tg + 1 < NQG:
                    emit_hs(tg + 1)
                dn_mm = mm.n
                for ts in range(4):
                    pd = (pg + pu)[ts]
                    osl = ts % 2
                    xsl = ts % 2
                    tt = tg * 4 + ts
                    rwait("sync", x1b_rel[xsl])
                    xv = dma("sync", x1b[:, xsl, :],
                             x1_scr[tt * P:(tt + 1) * P, hg * 512:(hg + 1) * 512],
                             ldx1b[xsl])
                    wait("vector", mm, dn_mm)
                    wait("vector", ev, hs2c_ev)
                    rwait("vector", outst_rel[osl])
                    emit("vector", lambda e, pd=pd, t=ts, hg=hg, o=osl: e.scalar_tensor_tensor(
                        outst[:, o, :], pd[:, :], hs2c[:, t:t + 1],
                        ds_sb[:, hg * 512:(hg + 1) * 512], ALU.add, ALU.mult), None)
                    wait("vector", ldx1b[xsl], xv)
                    ov = emit("vector", lambda e, o=osl, x=xsl: e.tensor_add(
                        outst[:, o, :], outst[:, o, :], x1b[:, x, :]), ev)
                    x1b_rel[xsl] = (ev, ov)
                    psum_rel[ts] = (ev, ov)
                    wait("sync", ev, ov)
                    sv = dma("sync",
                             dout[tt * P:(tt + 1) * P, hg * 512:(hg + 1) * 512],
                             outst[:, osl, :], stout[osl])
                    outst_rel[osl] = (stout[osl], sv)
                if tg + 1 < NQG:
                    if hg == 0:
                        ln2_tp(tg + 1, 0)
                        ln2_stats(tg + 1, 2)
                    elif hg == 1:
                        ln2_tp(tg + 1, 1)
                        ln2_stats(tg + 1, 3)
                        ln2_tp(tg + 1, 2)
                    elif hg == 2:
                        ln2_tp(tg + 1, 3)
            h_pe_done = mm.n
            gu_rel = [(mm, h_pe_done), (mm, h_pe_done)]

        wait("sync", stout[0], stout[0].n)
        wait("sync", stout[1], stout[1].n)
        wait("gpsimd", stout[0], stout[0].n)
        wait("gpsimd", stout[1], stout[1].n)
        flush()

    stack.close()
    return nc


_CACHE = {}


def kernel(L0_attn_x, L0_attn_Wqkv, L0_attn_b_qkv, L0_attn_Wout, L0_attn_b_out,
           L0_ffn_gate_w, L0_ffn_gate_scale, L0_ffn_up_w, L0_ffn_up_scale,
           L0_ffn_down_w, L0_ffn_down_scale):
    x = np.ascontiguousarray(np.asarray(L0_attn_x, dtype=np.float32))
    wqkv = np.ascontiguousarray(np.asarray(L0_attn_Wqkv, dtype=np.float32))
    bqkv = np.ascontiguousarray(np.asarray(L0_attn_b_qkv, dtype=np.float32)).reshape(1, OF)
    wout = np.ascontiguousarray(np.asarray(L0_attn_Wout, dtype=np.float32))
    bout = np.ascontiguousarray(np.asarray(L0_attn_b_out, dtype=np.float32)).reshape(1, DM)
    wg_t = np.ascontiguousarray(np.asarray(L0_ffn_gate_w).T).astype(np.uint8)
    wu_t = np.ascontiguousarray(np.asarray(L0_ffn_up_w).T).astype(np.uint8)
    wd_t = np.ascontiguousarray(np.asarray(L0_ffn_down_w).T).astype(np.uint8)
    gs = np.ascontiguousarray(
        np.asarray(L0_ffn_gate_scale, dtype=np.float32).reshape(64, P).T)
    us = np.ascontiguousarray(
        np.asarray(L0_ffn_up_scale, dtype=np.float32).reshape(64, P).T)
    ds_rep = np.ascontiguousarray(
        np.tile(np.asarray(L0_ffn_down_scale, dtype=np.float32).reshape(1, DM), (P, 1)))
    inv = (1.0 / 10000.0 ** (np.arange(0, 32, 2, dtype=np.float32) / np.float32(32.0)))
    tpos = np.arange(T, dtype=np.float32)
    fr = np.outer(tpos, inv).astype(np.float32)
    cos_t = np.cos(fr).astype(np.float32).reshape(TT, P, 16)
    sin_t = np.sin(fr).astype(np.float32).reshape(TT, P, 16)
    ident = np.eye(P, dtype=np.float32)
    kk = np.arange(P)[:, None]
    qq = np.arange(P)[None, :]
    maskd = np.where(kk > qq, np.float32(-10000.0), np.float32(0.0)).astype(np.float32)

    if "nc" not in _CACHE:
        _CACHE["nc"] = _build()
    nc = _CACHE["nc"]

    shared = dict(wqkv=wqkv, bqkv=bqkv, wout=wout, bout=bout,
                  wg_t=wg_t, wu_t=wu_t, wd_t=wd_t, gs=gs, us=us, ds_rep=ds_rep,
                  cos_t=cos_t, sin_t=sin_t, ident=ident, maskd=maskd)
    in_maps = [dict(x=np.ascontiguousarray(x[b]), **shared) for b in range(NCORES)]
    res = run_bass_kernel_spmd(nc, in_maps, core_ids=list(range(NCORES)))
    return np.stack([np.asarray(res.results[b]["out"]) for b in range(NCORES)], axis=0)


# version 18
# speedup vs baseline: 1.0346x; 1.0346x over previous
import contextlib
import numpy as np

import concourse.bass as bass
import concourse.mybir as mybir
from concourse.bass_utils import run_bass_kernel_spmd

F32 = mybir.dt.float32
BF16 = mybir.dt.bfloat16
U8 = mybir.dt.uint8
AF = mybir.ActivationFunctionType
ALU = mybir.AluOpType
AX = mybir.AxisListType

P = 128
T = 2048
DM = 2048
OF = 6144
IFF = 8192
TT = T // P
DC = DM // P
FC = 16
NQG = 4
NCORES = 8

ENGS = ("tensor", "vector", "scalar", "gpsimd", "sync")


def _build(phases=3.0, dbg=False):
    nc = bass.Bass("TRN2", target_bir_lowering=False)

    dx = nc.dram_tensor("x", [T, DM], F32, kind="ExternalInput")
    dwqkv = nc.dram_tensor("wqkv", [DM, OF], F32, kind="ExternalInput")
    dbqkv = nc.dram_tensor("bqkv", [1, OF], F32, kind="ExternalInput")
    dwout = nc.dram_tensor("wout", [DM, DM], F32, kind="ExternalInput")
    dbout = nc.dram_tensor("bout", [1, DM], F32, kind="ExternalInput")
    dwg = nc.dram_tensor("wg_t", [DM, IFF], U8, kind="ExternalInput")
    dwu = nc.dram_tensor("wu_t", [DM, IFF], U8, kind="ExternalInput")
    dwd = nc.dram_tensor("wd_t", [IFF, DM], U8, kind="ExternalInput")
    dgs = nc.dram_tensor("gs", [P, 64], F32, kind="ExternalInput")
    dus = nc.dram_tensor("us", [P, 64], F32, kind="ExternalInput")
    dds = nc.dram_tensor("ds_rep", [P, DM], F32, kind="ExternalInput")
    dcos = nc.dram_tensor("cos_t", [TT, P, 16], F32, kind="ExternalInput")
    dsin = nc.dram_tensor("sin_t", [TT, P, 16], F32, kind="ExternalInput")
    dident = nc.dram_tensor("ident", [P, P], F32, kind="ExternalInput")
    dmask = nc.dram_tensor("maskd", [P, P], F32, kind="ExternalInput")
    _skind = dict(kind="ExternalOutput") if dbg else {}
    q_scr = nc.dram_tensor("q_scr", [FC, P, T], BF16, **_skind)
    k_scr = nc.dram_tensor("k_scr", [FC, P, T], BF16, **_skind)
    v_scr = nc.dram_tensor("v_scr", [FC, TT, P, 130], BF16, **_skind)
    x1_scr = nc.dram_tensor("x1_scr", [T, DM], F32, **_skind)
    rb_scr = nc.dram_tensor("rb_scr", [2, 1024], BF16)
    if dbg:
        dxn2_dbg = nc.dram_tensor("xn2_dbg", [P, DC, 512], BF16, kind="ExternalOutput")
        dhsr_dbg = nc.dram_tensor("hsr_dbg", [P, 512], F32, kind="ExternalOutput")
        dh_dbg = nc.dram_tensor("h_dbg", [P, 64, 512], BF16, kind="ExternalOutput")
    dout = nc.dram_tensor("out", [T, DM], F32, kind="ExternalOutput")

    ops = {e: [] for e in ENGS}
    waited = {e: {} for e in ENGS}
    stack = contextlib.ExitStack()

    class Ctr:
        __slots__ = ("sem", "step", "n")

        def __init__(self, name, step):
            self.sem = stack.enter_context(nc.semaphore(name))
            self.step = step
            self.n = 0

    def wait(e, ctr, val=None):
        v = ctr.n if val is None else val
        if v <= 0 or waited[e].get(ctr, -1) >= v:
            return
        waited[e][ctr] = v
        sem = ctr.sem
        ops[e].append(lambda eng, sem=sem, v=v: eng.wait_ge(sem, v))

    def rwait(e, rel):
        if rel is not None:
            wait(e, rel[0], rel[1])

    def emit(e, fn, inc=None):
        if e == "vector":
            wait("vector", ev, ev.n)
            if inc is None:
                inc = ev
        if inc is None:
            ops[e].append(fn)
            return None
        inc.n += inc.step
        sem, st = inc.sem, inc.step
        ops[e].append(lambda eng, fn=fn, sem=sem, st=st: fn(eng).then_inc(sem, st))
        return inc.n

    def dma(e, out_ap, in_ap, ctr):
        return emit(e, lambda eng, o=out_ap, i=in_ap: eng.dma_start(o, i), ctr)

    def flush():
        with nc.Block() as blk:
            for e in ENGS:
                lst = ops[e]
                if not lst:
                    continue

                def runner(eng, lst=lst):
                    for f in lst:
                        f(eng)

                getattr(blk, e)(runner)
                ops[e] = []

    def barrier(ctrs):
        for e in ENGS:
            for c in ctrs:
                wait(e, c)

    mm = Ctr("mm", 1)
    ev = Ctr("ev", 1)
    es = Ctr("es", 1)
    pre = Ctr("pre", 16)
    preg = Ctr("preg", 16)
    ldx = [Ctr(f"ldx{i}", 16) for i in range(2)]
    ldw = [Ctr(f"ldw{i}", 16) for i in range(2)]
    stqk = [Ctr(f"stqk{i}", 16) for i in range(4)]
    stv = [Ctr(f"stv{i}", 16) for i in range(2)]
    ldq = [Ctr(f"ldq{i}", 16) for i in range(3)]
    ldk = [Ctr(f"ldk{i}", 16) for i in range(3)]
    ldv = [Ctr(f"ldv{i}", 16) for i in range(3)]
    ldwo = [Ctr(f"ldwo{i}", 16) for i in range(2)]
    ldxr = [Ctr(f"ldxr{i}", 16) for i in range(2)]
    sth2 = [Ctr(f"sth2{i}", 16) for i in range(2)]
    stx1 = [Ctr(f"stx1{i}", 16) for i in range(2)]
    ldx1 = [Ctr(f"ldx1{i}", 16) for i in range(2)]
    ldgu = [Ctr(f"ldgu{i}", 16) for i in range(2)]
    ldwd = [Ctr(f"ldwd{i}", 16) for i in range(2)]
    ldx1b = [Ctr(f"ldx1b{i}", 16) for i in range(2)]
    stout = [Ctr(f"stout{i}", 16) for i in range(2)]
    strb = [Ctr(f"strb{i}", 16) for i in range(2)]
    ldrb = [Ctr(f"ldrb{i}", 16) for i in range(2)]

    ones_sb = stack.enter_context(nc.sbuf_tensor("ones_sb", [P, P], BF16))
    identb = stack.enter_context(nc.sbuf_tensor("identb", [P, P], BF16))
    identf = stack.enter_context(nc.sbuf_tensor("identf", [P, P], F32))
    maskb = stack.enter_context(nc.sbuf_tensor("maskb", [P, P], BF16))
    bqkv_sb = stack.enter_context(nc.sbuf_tensor("bqkv_sb", [1, OF], BF16))
    bout_sb = stack.enter_context(nc.sbuf_tensor("bout_sb", [1, DM], BF16))
    cos_sb = stack.enter_context(nc.sbuf_tensor("cos_sb", [P, TT, 16], F32))
    sin_sb = stack.enter_context(nc.sbuf_tensor("sin_sb", [P, TT, 16], F32))
    gs_sb = stack.enter_context(nc.sbuf_tensor("gs_sb", [P, 64], F32))
    us_sb = stack.enter_context(nc.sbuf_tensor("us_sb", [P, 64], F32))
    ds_sb = stack.enter_context(nc.sbuf_tensor("ds_sb", [P, DM], F32))
    consts = stack.enter_context(nc.sbuf_tensor("consts", [P, 8], F32))
    C_INVD = consts[:, 0:1]
    C_EPS = consts[:, 1:2]
    C_NEG1 = consts[:, 2:3]
    C_8TH = consts[:, 3:4]
    C_NEG4 = consts[:, 4:5]
    C_NEG128 = consts[:, 5:6]
    stats = stack.enter_context(nc.sbuf_tensor("stats", [P, 32], F32))
    x1buf = stack.enter_context(nc.sbuf_tensor("x1buf", [P, 2, DM], F32))
    sq2 = stack.enter_context(nc.sbuf_tensor("sq2", [P, DM], F32))
    xn2tok = stack.enter_context(nc.sbuf_tensor("xn2tok", [P, 2, DM], BF16))

    emit("vector", lambda e: e.memset(ones_sb[:, :], 1.0), ev)
    for col, val in ((0, 1.0 / DM), (1, 1e-5), (2, -1.0), (3, 0.125),
                     (4, -4.0), (5, -128.0)):
        emit("vector", lambda e, c=col, v=val: e.memset(consts[:, c:c + 1], v), ev)
    EV_CONST = ev.n
    dma("gpsimd", identb[:, :], dident[:, :], preg)
    dma("gpsimd", maskb[:, :], dmask[:, :], preg)
    dma("gpsimd", bqkv_sb[:, :], dbqkv[:, :], preg)
    dma("gpsimd", bout_sb[:, :], dbout[:, :], preg)
    dma("sync", identf[:, :], dident[:, :], pre)
    dma("sync", cos_sb[:, :, :], dcos[:, :, :].transpose([1, 0, 2]), pre)
    dma("sync", sin_sb[:, :, :], dsin[:, :, :].transpose([1, 0, 2]), pre)
    dma("sync", gs_sb[:, :], dgs[:, :], pre)
    dma("sync", us_sb[:, :], dus[:, :], pre)
    dma("sync", ds_sb[:, :], dds[:, :], pre)
    PRE_TOTAL = pre.n
    PREG_TOTAL = preg.n
    for e in ("tensor", "vector", "scalar"):
        wait(e, pre, PRE_TOTAL)
        wait(e, preg, PREG_TOTAL)
        wait(e, ev, EV_CONST)

    def emit_ln(xb, sq, par, xn_out):
        b = 16 * par
        S1 = stats[:, b + 0:b + 1]
        S2 = stats[:, b + 1:b + 2]
        M = stats[:, b + 2:b + 3]
        MSQ = stats[:, b + 3:b + 4]
        MM2 = stats[:, b + 4:b + 5]
        VAR = stats[:, b + 5:b + 6]
        STD = stats[:, b + 6:b + 7]
        ISTD = stats[:, b + 7:b + 8]
        NMI = stats[:, b + 8:b + 9]
        emit("vector", lambda e: e.reduce_sum(S1, xb, axis=AX.X), ev)
        sq_done = emit("scalar", lambda e: e.activation(sq, xb, AF.Square), es)
        wait("vector", es, sq_done)
        emit("vector", lambda e: e.reduce_sum(S2, sq, axis=AX.X), ev)
        emit("vector", lambda e: e.tensor_single_scalar(M, S1, C_INVD, ALU.mult), ev)
        emit("vector", lambda e: e.tensor_single_scalar(MSQ, S2, C_INVD, ALU.mult), ev)
        emit("vector", lambda e: e.tensor_mul(MM2, M, M), ev)
        v_done = emit("vector", lambda e: e.tensor_sub(VAR, MSQ, MM2), ev)
        wait("scalar", ev, v_done)
        std_done = emit("scalar", lambda e: e.activation(STD, VAR, AF.Sqrt, bias=C_EPS), es)
        wait("vector", es, std_done)
        emit("vector", lambda e: e.reciprocal(ISTD, STD), ev)
        r_done = emit("vector", lambda e: e.scalar_tensor_tensor(
            NMI, M, C_NEG1, ISTD, ALU.mult, ALU.mult), ev)
        wait("scalar", ev, r_done)
        xn_done = emit("scalar", lambda e: e.activation(
            xn_out, xb, AF.Identity, bias=NMI, scale=ISTD), es)
        return xn_done

    x1_rel = [None, None]
    ln_state = {}
    xn2tok_rel = [None, None]

    def ln2_stats(tg, ts, st_wait=None):
        par = ts % 2
        tt = tg * 4 + ts
        rwait("sync", x1_rel[par])
        if st_wait is not None:
            wait("sync", stx1[0], st_wait[0])
            wait("sync", stx1[1], st_wait[1])
        lv = dma("sync", x1buf[:, par, :], x1_scr[tt * P:(tt + 1) * P, :],
                 ldx1[par])
        wait("vector", ldx1[par], lv)
        wait("scalar", ldx1[par], lv)
        rwait("scalar", xn2tok_rel[par])
        ln_state[(tg, ts)] = emit_ln(
            x1buf[:, par, :], sq2[:, :], par, xn2tok[:, par, :])
        x1_rel[par] = (es, ln_state[(tg, ts)])

    with contextlib.ExitStack() as sa:
        xnf = sa.enter_context(nc.sbuf_tensor("xnf", [P, DC, T], BF16))
        xbuf = sa.enter_context(nc.sbuf_tensor("xbuf", [P, 2, DM], F32))
        sqscr = sa.enter_context(nc.sbuf_tensor("sqscr", [P, DM], F32))
        xntok = sa.enter_context(nc.sbuf_tensor("xntok", [P, 2, DM], BF16))
        wbuf = sa.enter_context(nc.sbuf_tensor("wbuf", [P, 2, DC, 512], BF16))
        ropest = sa.enter_context(nc.sbuf_tensor("ropest", [P, 4, 512], BF16))
        scrA = sa.enter_context(nc.sbuf_tensor("scrA", [P, 8, 16], F32))
        scrB = sa.enter_context(nc.sbuf_tensor("scrB", [P, 8, 16], F32))
        qk2 = sa.enter_context(nc.sbuf_tensor("qk2", [P, 4, P], BF16))
        vstage = sa.enter_context(nc.sbuf_tensor("vstage", [P, 2, 4, 130], BF16))
        pq = [sa.enter_context(nc.psum_tensor(f"pq{i}", [P, 512], F32)) for i in range(4)]
        psT = [sa.enter_context(nc.psum_tensor(f"psT{i}", [P, 1024], BF16)) for i in range(2)]

        for sl in range(2):
            emit("vector", lambda e, s=sl: e.memset(vstage[:, s, :, :], 1.0), ev)

        def load_wgrp(og, slot, rel):
            rwait("gpsimd", rel)
            for dcc in range(DC):
                dma("gpsimd", wbuf[:, slot, dcc, :],
                    dwqkv[dcc * P:(dcc + 1) * P, og * 512:(og + 1) * 512], ldw[slot])
            return ldw[slot].n

        wld = [0, 0]
        wrel = [None, None]
        wld[0] = load_wgrp(0, 0, None)
        wld[1] = load_wgrp(1, 1, None)
        pq_rel = [None, None, None, None]
        qk2_rel = [None] * 4
        ropest_rel = [None, None, None, None]
        vstage_rel = [None, None]
        xbuf_rel = [None, None]
        psT_rel = [None, None]
        xntok_rel = [None, None]
        xnf_cp = [0] * TT
        pend = [None]
        jidx = [0]

        def emit_ln_tile(tt):
            par = tt % 2
            rwait("sync", xbuf_rel[par])
            lv = dma("sync", xbuf[:, par, :], dx[tt * P:(tt + 1) * P, :], ldx[par])
            wait("vector", ldx[par], lv)
            wait("scalar", ldx[par], lv)
            rwait("scalar", xntok_rel[par])
            xn_done = emit_ln(xbuf[:, par, :], sqscr[:, :], par, xntok[:, par, :])
            xbuf_rel[par] = (es, xn_done)
            return xn_done

        def emit_ln_tp(tt, xn_done):
            par = tt % 2
            wait("tensor", es, xn_done)
            for g in range(2):
                sp = g % 2
                rwait("tensor", psT_rel[sp])
                mv = 0
                for c in range(8):
                    dc = g * 8 + c
                    mv = emit("tensor", lambda e, d=dc, c=c, s=sp, p=par: e.transpose(
                        psT[s][:, c * P:(c + 1) * P],
                        xntok[:, p, d * P:(d + 1) * P], identb[:, :]), mm)
                wait("vector", mm, mv)
                dst = xnf[:, g * 8:(g + 1) * 8, tt * P:(tt + 1) * P]
                cv = emit("vector", lambda e, s=sp, dst=dst: e.tensor_copy(
                    dst, psT[s][:, :].rearrange("p (c f) -> p c f", c=8)), ev)
                psT_rel[sp] = (ev, cv)
            xntok_rel[par] = (mm, mm.n)
            xnf_cp[tt] = ev.n

        def mm_job(og, tt):
            par = jidx[0] % 4
            jidx[0] += 1
            wslot = og % 2
            wait("tensor", ldw[wslot], wld[wslot])
            wait("tensor", ev, xnf_cp[tt])
            for _r in (pq_rel[par] or []):
                rwait("tensor", _r)
            for dcc in range(DC):
                emit("tensor", lambda e, d=dcc, t=tt, w=wslot, p=par: e.matmul(
                    pq[p][:, :], xnf[:, d, t * P:(t + 1) * P], wbuf[:, w, d, :],
                    start=(d == 0), stop=False), None)
            mm_tile = emit("tensor", lambda e, o=og, p=par: e.matmul(
                pq[p][:, :], ones_sb[0:1, :], bqkv_sb[0:1, o * 512:(o + 1) * 512],
                start=False, stop=True), mm)
            if tt == TT - 1:
                wrel[wslot] = (mm, mm_tile)
            if pend[0] is not None:
                pend[0]()
                pend[0] = None
            if og < 8:
                isq = og < 4
                scr = q_scr if isq else k_scr
                fcbase = (og % 4) * 4
                pr = pq[par].rearrange("p (h d) -> p h d", h=8)
                st = ropest[:, par, :].rearrange("p (h d) -> p h d", h=8)
                cosb = cos_sb[:, tt, :][:, None, :].broadcast_to((P, 8, 16))
                sinb = sin_sb[:, tt, :][:, None, :].broadcast_to((P, 8, 16))
                wait("vector", mm, mm_tile)
                rwait("vector", ropest_rel[par])
                emit("vector", lambda e, pr=pr, c=cosb: e.tensor_mul(
                    scrA[:, :, :], pr[:, :, 0:16], c), None)
                emit("vector", lambda e, pr=pr, s=sinb: e.tensor_mul(
                    scrB[:, :, :], pr[:, :, 16:32], s), None)
                emit("vector", lambda e, st=st: e.tensor_sub(
                    st[:, :, 0:16], scrA[:, :, :], scrB[:, :, :]), None)
                emit("vector", lambda e, pr=pr, s=sinb: e.tensor_mul(
                    scrA[:, :, :], pr[:, :, 0:16], s), None)
                emit("vector", lambda e, pr=pr, c=cosb: e.tensor_mul(
                    scrB[:, :, :], pr[:, :, 16:32], c), None)
                emit("vector", lambda e, st=st: e.tensor_add(
                    st[:, :, 16:32], scrA[:, :, :], scrB[:, :, :]), None)
                rope_done = emit("vector", lambda e, st=st, pr=pr: e.tensor_copy(
                    st[:, :, 32:64], pr[:, :, 32:64]), ev)
                sp_cp = None
                pq_rel[par] = [(ev, rope_done)]

                def tp(par=par, tt=tt, scr=scr, fcbase=fcbase,
                       rope_done=rope_done):
                    sp = jidx[0] % 2
                    wait("tensor", ev, rope_done)
                    rwait("tensor", psT_rel[sp])
                    mv = 0
                    for c in range(4):
                        mv = emit("tensor", lambda e, c=c, s=sp, p=par: e.transpose(
                            psT[s][:, c * P:(c + 1) * P],
                            ropest[:, p, c * P:(c + 1) * P], identb[:, :]), mm)
                    wait("vector", mm, mv)
                    for c in range(4):
                        rwait("vector", qk2_rel[c])
                    cv = emit("vector", lambda e, s=sp: e.tensor_copy(
                        qk2[:, :, :], psT[s][:, 0:512].rearrange(
                            "p (c f) -> p c f", c=4)), ev)
                    psT_rel[sp] = (ev, cv)
                    wait("sync", ev, cv)
                    for c in range(4):
                        sv = dma("sync", scr[fcbase + c, :, tt * P:(tt + 1) * P],
                                 qk2[:, c, :], stqk[c])
                        qk2_rel[c] = (stqk[c], sv)
                    ropest_rel[par] = (mm, mm.n)
                pend[0] = tp
            else:
                hpbase = (og - 8) * 4
                vsl = jidx[0] % 2
                pr = pq[par].rearrange("p (h x d) -> p h x d", h=4, x=2)
                dstv = vstage[:, vsl, :, :].rearrange(
                    "p h (x e) -> p h x e", x=2)[:, :, :, 0:64]
                wait("vector", mm, mm_tile)
                rwait("vector", vstage_rel[vsl])
                cv = emit("vector", lambda e, pr=pr, d=dstv: e.tensor_copy(
                    d, pr[:, :, :, 0:64]), ev)
                pq_rel[par] = [(ev, cv)]
                wait("sync", ev, cv)
                last = 0
                for c in range(4):
                    last = dma("sync", v_scr[hpbase + c, tt, :, :],
                               vstage[:, vsl, c, :], stv[vsl])
                vstage_rel[vsl] = (stv[vsl], last)
            if tt == TT - 1 and og + 2 < 12:
                wld[wslot] = load_wgrp(og + 2, wslot, wrel[wslot])

        xn_pend = emit_ln_tile(0)
        for tt in range(TT):
            emit_ln_tp(tt, xn_pend)
            if tt + 1 < TT:
                xn_pend = emit_ln_tile(tt + 1)
            mm_job(0, tt)
            mm_job(1, tt)
        for og in range(2, 12):
            for tt in range(TT):
                mm_job(og, tt)
        if pend[0] is not None:
            pend[0]()
            pend[0] = None
        A_CTRS = [mm, ev, es] + stqk + stv
        flush()

    if phases < 2:
        stack.close()
        return nc
    with contextlib.ExitStack() as sb:
        qbuf = sb.enter_context(nc.sbuf_tensor("qbuf", [P, 3, 512], BF16))
        kbuf = sb.enter_context(nc.sbuf_tensor("kbuf", [P, 3, T], BF16))
        vbuf = sb.enter_context(nc.sbuf_tensor("vbuf", [P, 3, TT, 130], BF16))
        ebAB = sb.enter_context(nc.sbuf_tensor("ebAB", [P, 4, 1024], BF16))
        rdbuf = sb.enter_context(nc.sbuf_tensor("rdbuf", [P, 2, 512], BF16))
        repsb = sb.enter_context(nc.sbuf_tensor("repsb", [P, 2, 512], BF16))
        stg2 = sb.enter_context(nc.sbuf_tensor("stg2", [P, 2, 512], BF16))
        oc = sb.enter_context(nc.sbuf_tensor("oc", [P, 2, 2, 512], F32))
        attnout = sb.enter_context(nc.sbuf_tensor("attnout", [P, 2, FC, 512], BF16))
        wobuf = sb.enter_context(nc.sbuf_tensor("wobuf", [P, 2, DC, 512], BF16))
        xresb = sb.enter_context(nc.sbuf_tensor("xresb", [P, 2, 512], F32))
        o1st = sb.enter_context(nc.sbuf_tensor("o1st", [P, 2, 512], F32))
        sAB = [sb.enter_context(nc.psum_tensor(f"sAB{i}", [P, 1024], F32)) for i in range(2)]
        oA = sb.enter_context(nc.psum_tensor("oA", [P, 512], F32))
        oB = sb.enter_context(nc.psum_tensor("oB", [P, 512], F32))
        pw = [sb.enter_context(nc.psum_tensor(f"pw{i}", [P, 512], F32)) for i in range(2)]

        barrier(A_CTRS)

        q_rel = [None, None, None]
        k_rel = [None, None, None]
        v_rel = [None, None, None]
        kld = [0, 0, 0]
        vld = [0, 0, 0]
        qld = [0, 0, 0]
        sAB_rel = [None, None]
        eb_rel = [None, None, None, None]
        o_rel = [None]
        rb_rel = [None, None]
        rd_rel = [None]
        oc_rel = [None, None]
        att_pend = [None]
        stg2_rel = [None, None]
        wout_psum_rel = [None, None]
        wo_rel = [None, None]
        wo_pref = [0, 0]
        xres_rel = [None, None]
        o1st_rel = [None, None]
        units_done = {-2: 0, -1: 0}
        ux1_done = {}
        wout_units = []
        wo_ld = [0, 0]

        def load_job(g):
            jqg, jfc = g // FC, g % FC
            jnkt = 4 * (jqg + 1)
            s = g % 3
            rwait("sync", q_rel[s])
            qld[s] = dma("sync", qbuf[:, s, :],
                         q_scr[jfc, :, jqg * 512:(jqg + 1) * 512], ldq[s])
            rwait("sync", k_rel[s])
            kld[s] = dma("sync", kbuf[:, s, 0:jnkt * P],
                         k_scr[jfc, :, 0:jnkt * P], ldk[s])
            rwait("sync", v_rel[s])
            h1 = (jnkt + 1) // 2
            dma("sync", vbuf[:, s, 0:h1, :],
                v_scr[jfc, 0:h1, :, :].transpose([1, 0, 2]), ldv[s])
            vld[s] = dma("sync", vbuf[:, s, h1:jnkt, :],
                         v_scr[jfc, h1:jnkt, :, :].transpose([1, 0, 2]), ldv[s])

        load_job(0)
        load_job(1)
        for qg in range(NQG):
            nkt = 4 * (qg + 1)
            for fc in range(FC):
                g = qg * FC + fc
                qsl = g % 3
                if g + 2 < NQG * FC:
                    load_job(g + 2)
                qv = qld[qsl]
                qflat = qbuf[:, qsl, :]
                av_q = []
                sm = {}
                for kt in range(nkt):
                    ks = qsl
                    ssl = kt % 2
                    diag = kt >= 4 * qg
                    off = P * (kt - 4 * qg) if diag else 0
                    wait("tensor", ldk[ks], kld[ks])
                    wait("tensor", ldq[qsl], qv)
                    rwait("tensor", sAB_rel[ssl])
                    kap = kbuf[:, ks, kt * P:(kt + 1) * P]
                    if diag:
                        emit("tensor", lambda e, k=kap, s=ssl, o=off, q=qflat: e.matmul(
                            sAB[s][:, o:512], k[0:64, :], q[0:64, o:512],
                            start=True, stop=False, skip_group_check=True), None)
                        emit("tensor", lambda e, s=ssl, o=off: e.matmul(
                            sAB[s][:, o:o + P], identb[:, :], maskb[:, :],
                            start=False, stop=True, skip_group_check=True), None)
                        emit("tensor", lambda e, k=kap, s=ssl, o=off, q=qflat: e.matmul(
                            sAB[s][:, 512 + o:1024], k[64:128, :],
                            q[64:128, o:512],
                            start=True, stop=False, skip_group_check=True), None)
                        sm[kt] = emit("tensor", lambda e, s=ssl, o=off: e.matmul(
                            sAB[s][:, 512 + o:512 + o + P], identb[:, :], maskb[:, :],
                            start=False, stop=True, skip_group_check=True), mm)
                    else:
                        emit("tensor", lambda e, k=kap, s=ssl, q=qflat: e.matmul(
                            sAB[s][:, 0:512], k[0:64, :], q[0:64, :],
                            start=True, stop=True, skip_group_check=True), None)
                        sm[kt] = emit("tensor", lambda e, k=kap, s=ssl, q=qflat: e.matmul(
                            sAB[s][:, 512:1024], k[64:128, :], q[64:128, :],
                            start=True, stop=True, skip_group_check=True), mm)
                    if kt % 2 == 0:
                        continue
                    for k2 in (kt - 1, kt):
                        s2 = k2 % 2
                        d2 = k2 >= 4 * qg
                        o2 = P * (k2 - 4 * qg) if d2 else 0
                        e4 = k2 % 4
                        wait("scalar", mm, sm[k2])
                        rwait("scalar", eb_rel[e4])
                        if d2:
                            emit("scalar", lambda e, s=s2, b=e4, o=o2: e.activation(
                                ebAB[:, b, o:512], sAB[s][:, o:512], AF.Exp,
                                bias=C_NEG4, scale=C_8TH), es)
                            exAB = emit("scalar", lambda e, s=s2, b=e4, o=o2: e.activation(
                                ebAB[:, b, 512 + o:1024], sAB[s][:, 512 + o:1024],
                                AF.Exp, bias=C_NEG4, scale=C_8TH), es)
                        else:
                            exAB = emit("scalar", lambda e, s=s2, b=e4: e.activation(
                                ebAB[:, b, 0:1024], sAB[s][:, 0:1024], AF.Exp,
                                bias=C_NEG4, scale=C_8TH), es)
                        sAB_rel[s2] = (es, exAB)

                        def av_emit(kt=k2, off=o2, exAB=exAB, vs=qsl,
                                    first=(k2 == 0), last=(k2 == nkt - 1)):
                            ebs = kt % 4
                            wait("tensor", es, exAB)
                            wait("tensor", ldv[vs], vld[vs])
                            if first:
                                rwait("tensor", o_rel[0])
                            emit("tensor", lambda e, kt=kt, vs=vs, ebs=ebs, off=off,
                                 first=first, last=last: e.matmul(
                                oA[0:65, off:512], vbuf[:, vs, kt, 0:65],
                                ebAB[:, ebs, off:512],
                                start=first, stop=last, skip_group_check=True), None)
                            r2 = emit("tensor", lambda e, kt=kt, vs=vs, ebs=ebs, off=off,
                                      first=first, last=last: e.matmul(
                                oB[0:65, off:512], vbuf[:, vs, kt, 65:130],
                                ebAB[:, ebs, 512 + off:1024],
                                start=first, stop=last, skip_group_check=True), mm)
                            eb_rel[ebs] = (mm, r2)
                            return r2
                        av_q.append(av_emit)
                    while len(av_q) > 2:
                        av_q.pop(0)()
                    if kt == 1 and att_pend[0] is not None:
                        att_pend[0]()
                        att_pend[0] = None
                    if kt == 3 and wout_units:
                        wout_units.pop(0)()
                while av_q:
                    av_q.pop(0)()
                av_last = mm.n
                q_rel[qsl] = (mm, av_last)
                k_rel[qsl] = (mm, av_last)
                v_rel[qsl] = (mm, av_last)
                ocs = fc % 2
                wait("scalar", mm, av_last)
                rwait("scalar", oc_rel[ocs])
                emit("scalar", lambda e, s=ocs: e.activation(
                    oc[0:65, s, 0, :], oA[0:65, :], AF.Identity), None)
                oc_done = emit("scalar", lambda e, s=ocs: e.activation(
                    oc[0:65, s, 1, :], oB[0:65, :], AF.Identity), es)
                o_rel[0] = (es, oc_done)

                def attnorm(fc=fc, ocs=ocs, oc_done=oc_done, aslot=qg % 2,
                            slot_free=units_done[qg - 2]):
                    rsl = fc % 2
                    wait("vector", es, oc_done)
                    rwait("vector", rb_rel[rsl])
                    rwait("vector", rd_rel[0])

                    def _recip(e, dst, src):
                        with nc.allow_low_precision(
                                reason="softmax denominator reciprocal stored "
                                       "bf16 as a broadcast operand"):
                            return e.reciprocal(dst, src)
                    emit("vector", lambda e, s=ocs: _recip(
                        e, rdbuf[64:65, 0, :], oc[64:65, s, 0, :]), ev)
                    rb = emit("vector", lambda e, s=ocs: _recip(
                        e, rdbuf[64:65, 1, :], oc[64:65, s, 1, :]), ev)
                    wait("gpsimd", ev, rb)
                    rwait("gpsimd", rb_rel[rsl])
                    sv1 = dma("gpsimd", rb_scr[rsl:rsl + 1, :],
                              rdbuf[64:65, 0:2, :].rearrange("p a b -> p (a b)"),
                              strb[rsl])
                    rd_rel[0] = (strb[rsl], sv1)
                    wait("gpsimd", strb[rsl], sv1)
                    bv = dma("gpsimd", repsb[0:64, :, :],
                             rb_scr[rsl:rsl + 1, :].broadcast_to((64, 1024)),
                             ldrb[rsl])
                    rb_rel[rsl] = (ldrb[rsl], bv)
                    wait("vector", ldrb[rsl], bv)
                    wait("vector", mm, slot_free)
                    emit("vector", lambda e, f=fc, a=aslot, s=ocs: e.tensor_mul(
                        attnout[0:64, a, f, :], oc[0:64, s, 0, :],
                        repsb[0:64, 0, :]), ev)
                    h2sl = fc % 2
                    rwait("vector", stg2_rel[h2sl])
                    mub = emit("vector", lambda e, s=h2sl, c=ocs: e.tensor_mul(
                        stg2[0:64, s, :], oc[0:64, c, 1, :], repsb[0:64, 1, :]), ev)
                    oc_rel[ocs] = (ev, mub)
                    wait("sync", ev, mub)
                    wait("sync", mm, slot_free)
                    sv = dma("sync", attnout[64:128, aslot, fc, :],
                             stg2[0:64, h2sl, :], sth2[h2sl])
                    stg2_rel[h2sl] = (sth2[h2sl], sv)
                att_pend[0] = attnorm
            for og_p in range(2):
                rwait("gpsimd", wo_rel[og_p])
                for dcc in range(DC):
                    dma("gpsimd", wobuf[:, og_p, dcc, :],
                        dwout[dcc * P:(dcc + 1) * P,
                              og_p * 512:(og_p + 1) * 512], ldwo[og_p])
                wo_pref[og_p] = ldwo[og_p].n
            if att_pend[0] is not None:
                att_pend[0]()
                att_pend[0] = None
            att_ev = ev.n
            att_h2 = [sth2[0].n, sth2[1].n]

            def wout_unit(qg=qg, u=None, att_ev=att_ev, att_h2=att_h2,
                          aslot=qg % 2):
                og, ts = u // 4, u % 4
                wsl = og % 2
                if og < 2:
                    wov = wo_pref[wsl]
                else:
                    wov = wo_ld[wsl]
                psl = ts % 2
                pwt = pw[psl][:, :]
                wait("tensor", ldwo[wsl], wov)
                wait("tensor", ev, att_ev)
                wait("tensor", sth2[0], att_h2[0])
                wait("tensor", sth2[1], att_h2[1])
                rwait("tensor", wout_psum_rel[psl])
                xsl = u % 2
                rwait("gpsimd", xres_rel[xsl])
                xr = dma("gpsimd", xresb[:, xsl, :],
                         dx[qg * 512 + ts * P: qg * 512 + (ts + 1) * P,
                            og * 512:(og + 1) * 512], ldxr[xsl])
                for dcc in range(DC):
                    emit("tensor", lambda e, d=dcc, t=ts, w=wsl, ps=pwt, a=aslot:
                         e.matmul(ps, attnout[:, a, d, t * P:(t + 1) * P],
                                  wobuf[:, w, d, :],
                                  start=(d == 0), stop=False), None)
                wo_mm = emit("tensor", lambda e, o=og, ps=pwt: e.matmul(
                    ps, ones_sb[0:1, :], bout_sb[0:1, o * 512:(o + 1) * 512],
                    start=False, stop=True), mm)
                osl = u % 2
                wait("vector", mm, wo_mm)
                wait("vector", ldxr[xsl], xr)
                rwait("vector", o1st_rel[osl])
                o1v = emit("vector", lambda e, ps=pwt, x=xsl, s=osl: e.tensor_add(
                    o1st[:, s, :], ps, xresb[:, x, :]), ev)
                wout_psum_rel[psl] = (ev, o1v)
                xres_rel[xsl] = (ev, o1v)
                wait("gpsimd", ev, o1v)
                sv = dma("gpsimd",
                         x1_scr[qg * 512 + ts * P: qg * 512 + (ts + 1) * P,
                                og * 512:(og + 1) * 512],
                         o1st[:, osl, :], stx1[osl])
                o1st_rel[osl] = (stx1[osl], sv)
                if u == 3 or u == 7:
                    og2 = 2 if u == 3 else 3
                    wsl2 = og2 % 2
                    wo_rel[wsl2] = (mm, mm.n)
                    rwait("gpsimd", wo_rel[wsl2])
                    for dcc in range(DC):
                        dma("gpsimd", wobuf[:, wsl2, dcc, :],
                            dwout[dcc * P:(dcc + 1) * P, og2 * 512:(og2 + 1) * 512],
                            ldwo[wsl2])
                    wo_ld[wsl2] = ldwo[wsl2].n
                if u == 11:
                    wo_rel[0] = (mm, mm.n)
                if u == 15:
                    wo_rel[1] = (mm, mm.n)
                    units_done[qg] = mm.n
                    ux1_done[qg] = (stx1[0].n, stx1[1].n)

            wout_units = [
                (lambda u=u, f=wout_unit: f(u=u)) for u in range(16)]
            if qg == NQG - 1:
                while wout_units:
                    wout_units.pop(0)()
        ln2_stats(0, 0, st_wait=ux1_done[0])
        ln2_stats(0, 1, st_wait=ux1_done[0])
        B_CTRS = [mm, ev, es] + stx1 + sth2
        flush()

    if phases < 2.1:
        stack.close()
        return nc
    with contextlib.ExitStack() as sc:
        xn2f = sc.enter_context(nc.sbuf_tensor("xn2f", [P, DC, 512], BF16))
        wgub = sc.enter_context(nc.sbuf_tensor("wgub", [P, 2, DC, 512], BF16))
        wdb = wgub
        h_sb = sc.enter_context(nc.sbuf_tensor("h_sb", [P, 64, 512], BF16))
        hsr = sc.enter_context(nc.sbuf_tensor("hsr", [P, 512], F32))
        hs2r = sc.enter_context(nc.sbuf_tensor("hs2r", [P, 512], F32))
        hs2c = sc.enter_context(nc.sbuf_tensor("hs2c", [P, 4], F32))
        gl = sc.enter_context(nc.sbuf_tensor("gl", [P, 2, 512], F32))
        sil = sc.enter_context(nc.sbuf_tensor("sil", [P, 4, 512], F32))
        usb = sc.enter_context(nc.sbuf_tensor("usb", [P, 2, 512], F32))
        x1b = sc.enter_context(nc.sbuf_tensor("x1b", [P, 2, 512], F32))
        outst = sc.enter_context(nc.sbuf_tensor("outst", [P, 2, 512], F32))
        pg = [sc.enter_context(nc.psum_tensor(f"pg{i}", [P, 512], F32)) for i in range(2)]
        pu = [sc.enter_context(nc.psum_tensor(f"pu{i}", [P, 512], F32)) for i in range(2)]
        psh = sc.enter_context(nc.psum_tensor("psh", [P, 512], F32))
        ptp = [sc.enter_context(nc.psum_tensor(f"ptp{i}", [P, 1024], BF16)) for i in range(2)]
        pht = sc.enter_context(nc.psum_tensor("pht", [P, 512], F32))

        barrier(B_CTRS)

        psT_rel2 = [None, None]
        gu_rel = [None, None]
        gu_ld = [0, 0]
        wd_rel = [None, None]
        psum_rel = [None, None, None, None]
        gl_rel = [None, None]
        sil_rel = [None, None, None, None]
        usb_rel = [None, None]
        h_pe_done = 0
        pht_rel = None
        outst_rel = [None, None]
        x1b_rel = [None, None]
        hs2_es_prev = [0]

        guf_prev = [0]
        lnhs = {}

        def ln2_tp(tg, ts):
            par = ts % 2
            wait("vector", mm, guf_prev[0])
            wait("tensor", es, ln_state[(tg, ts)])
            for g in range(2):
                sp = g % 2
                rwait("tensor", psT_rel2[sp])
                mv = 0
                for c in range(8):
                    dcc = g * 8 + c
                    mv = emit("tensor", lambda e, d=dcc, c=c, s=sp, p=par: e.transpose(
                        ptp[s][:, c * P:(c + 1) * P],
                        xn2tok[:, p, d * P:(d + 1) * P], identb[:, :]), mm)
                wait("vector", mm, mv)
                dst = xn2f[:, g * 8:(g + 1) * 8, ts * P:(ts + 1) * P]
                cv = emit("vector", lambda e, s=sp, dst=dst: e.tensor_copy(
                    dst, ptp[s][:, :].rearrange("p (c f) -> p c f", c=8)), ev)
                psT_rel2[sp] = (ev, cv)
            xn2tok_rel[par] = (mm, mm.n)

        def emit_hs(tg):
            xn2_ev = ev.n
            wait("tensor", ev, xn2_ev)
            wait("tensor", es, hs2_es_prev[0])
            for dcc in range(DC):
                hs_mm = emit("tensor", lambda e, d=dcc: e.matmul(
                    psh[:, :], ones_sb[:, :], xn2f[:, d, :],
                    start=(d == 0), stop=(d == DC - 1)),
                    mm if dcc == DC - 1 else None)
            wait("scalar", mm, hs_mm)
            wait("scalar", ev, ev.n)
            hs_es = emit("scalar", lambda e: e.activation(
                hsr[:, :], psh[:, :], AF.Identity, scale=C_NEG128), es)
            lnhs[tg] = (xn2_ev, hs_es)

        ln2_tp(0, 0)
        ln2_stats(0, 2)
        ln2_tp(0, 1)
        ln2_stats(0, 3)
        ln2_tp(0, 2)
        ln2_tp(0, 3)
        emit_hs(0)
        gu_prefetched = [False]
        for tg in range(NQG):
            xn2_ev, hs_es = lnhs[tg]
            hs2_es_prev[0] = hs_es
            wait("vector", mm, h_pe_done)

            def load_gu(kind, ib, slot, rel):
                rwait("gpsimd", rel)
                src = dwg if kind == 0 else dwu
                for dcc in range(DC):
                    dma("gpsimd", wgub[:, slot, dcc, :],
                        src[dcc * P:(dcc + 1) * P, ib * 512:(ib + 1) * 512], ldgu[slot])
                return ldgu[slot].n

            seq = [(k, ib) for ib in range(16) for k in (0, 1)]
            sil_done = [0, 0, 0, 0]
            if not gu_prefetched[0]:
                gu_ld[0] = load_gu(*seq[0], 0, gu_rel[0])
            gu_prefetched[0] = False
            gu_ld[1] = load_gu(*seq[1], 1, gu_rel[1])
            gu_use = [0, 0]
            for si, (kind, ib) in enumerate(seq):
                slot = si % 2
                wait("tensor", ldgu[slot], gu_ld[slot])
                for icl in range(4):
                    ic = ib * 4 + icl
                    ppar = icl % 2
                    pslot = ppar if kind == 0 else 2 + ppar
                    ps = pg[ppar] if kind == 0 else pu[ppar]
                    rwait("tensor", psum_rel[pslot])
                    for dcc in range(DC):
                        mmv = emit("tensor", lambda e, d=dcc, i=icl, s=slot, ps=ps: e.matmul(
                            ps[:, :], wgub[:, s, d, i * P:(i + 1) * P], xn2f[:, d, :],
                            start=(d == 0), stop=(d == DC - 1)),
                            mm if dcc == DC - 1 else None)
                    if kind == 0:
                        wait("vector", mm, mmv)
                        wait("vector", es, hs_es)
                        rwait("vector", gl_rel[ppar])
                        gv = emit("vector", lambda e, ps=ps, p=ppar: e.tensor_add(
                            gl[:, p, :], ps[:, :], hsr[:, :]), ev)
                        psum_rel[pslot] = (ev, gv)
                        wait("scalar", ev, gv)
                        rwait("scalar", sil_rel[icl])
                        sv = emit("scalar", lambda e, p=ppar, il=icl, ic=ic: e.activation(
                            sil[:, il, :], gl[:, p, :], AF.Silu,
                            scale=gs_sb[:, ic:ic + 1]), es)
                        gl_rel[ppar] = (es, sv)
                        sil_done[icl] = sv
                    else:
                        wait("vector", mm, mmv)
                        rwait("vector", usb_rel[ppar])
                        uv = emit("vector", lambda e, ps=ps, p=ppar: e.tensor_add(
                            usb[:, p, :], ps[:, :], hsr[:, :]), ev)
                        psum_rel[pslot] = (ev, uv)
                        wait("vector", es, sil_done[icl])
                        hv = emit("vector", lambda e, p=ppar, il=icl, ic=ic: e.scalar_tensor_tensor(
                            h_sb[:, ic, :], sil[:, il, :], us_sb[:, ic:ic + 1],
                            usb[:, p, :], ALU.mult, ALU.mult), ev)
                        sil_rel[icl] = (ev, hv)
                        usb_rel[ppar] = (ev, hv)
                gu_use[slot] = mm.n
                if si + 2 < len(seq):
                    gu_rel[slot] = (mm, mm.n)
                    gu_ld[slot] = load_gu(*seq[si + 2], slot, gu_rel[slot])
            h_ev = ev.n
            guf_mm = mm.n
            guf_prev[0] = guf_mm
            if dbg and tg == 0:
                wait("sync", ev, h_ev)
                wait("sync", es, hs_es)
                dma("sync", dxn2_dbg[:, :, :], xn2f[:, :, :], pre)
                dma("sync", dhsr_dbg[:, :], hsr[:, :], pre)
                dma("sync", dh_dbg[:, :, :], h_sb[:, :, :], pre)
                wait("vector", pre, pre.n)
                wait("tensor", pre, pre.n)
                wait("scalar", pre, pre.n)
            wait("tensor", ev, h_ev)
            wait("tensor", es, hs_es)
            for ic in range(64):
                hs2_mm = emit("tensor", lambda e, i=ic: e.matmul(
                    psh[:, :], ones_sb[:, :], h_sb[:, i, :],
                    start=(i == 0), stop=(i == 63)),
                    mm if ic == 63 else None)
            wait("scalar", mm, hs2_mm)
            wait("scalar", ev, ev.n)
            hs2_es = emit("scalar", lambda e: e.activation(
                hs2r[:, :], psh[:, :], AF.Identity, scale=C_NEG128), es)
            hs2_es_prev[0] = hs2_es
            wait("tensor", es, hs2_es)
            rwait("tensor", pht_rel)
            mv = 0
            for ts in range(4):
                mv = emit("tensor", lambda e, t=ts: e.transpose(
                    pht[:, t * P:(t + 1) * P], hs2r[:, t * P:(t + 1) * P],
                    identf[:, :]), mm)
            wait("vector", mm, mv)
            cpv = emit("vector", lambda e: e.tensor_copy(
                hs2c[:, 0:4],
                bass.AP(pht.tensor if hasattr(pht, 'tensor') else pht, 0,
                        [[512, P], [P, 4], [1, 1]])), ev)
            pht_rel = (ev, cpv)
            hs2c_ev = ev.n
            if tg + 1 < NQG:
                ln2_stats(tg + 1, 0)
                ln2_stats(tg + 1, 1)
            for hg in range(4):
                for q4 in range(4):
                    wsl = (hg * 4 + q4) % 2
                    rwait("gpsimd", wd_rel[wsl])
                    wait("gpsimd", mm, gu_use[wsl])
                    for icl in range(16):
                        i = q4 * 16 + icl
                        dma("gpsimd", wdb[:, wsl, icl, :],
                            dwd[i * P:(i + 1) * P, hg * 512:(hg + 1) * 512], ldwd[wsl])
                    wv = ldwd[wsl].n
                    wait("tensor", ldwd[wsl], wv)
                    q4last = 0
                    for ts in range(4):
                        pd = (pg + pu)[ts]
                        if q4 == 0:
                            rwait("tensor", psum_rel[ts])
                        for icl in range(16):
                            i = q4 * 16 + icl
                            q4last = emit("tensor",
                                lambda e, i=i, il=icl, w=wsl, t=ts, pd=pd: e.matmul(
                                    pd[:, :], h_sb[:, i, t * P:(t + 1) * P],
                                    wdb[:, w, il, :],
                                    start=(i == 0), stop=(i == 63)),
                                mm if (ts == 3 and icl == 15) else None)
                    wd_rel[wsl] = (mm, q4last)
                    if hg == 3 and q4 == 3 and tg + 1 < NQG:
                        gu_ld[0] = load_gu(0, 0, 0, wd_rel[0])
                        gu_prefetched[0] = True
                if hg == 3 and tg + 1 < NQG:
                    emit_hs(tg + 1)
                dn_mm = mm.n
                for ts in range(4):
                    pd = (pg + pu)[ts]
                    osl = ts % 2
                    xsl = ts % 2
                    tt = tg * 4 + ts
                    rwait("sync", x1b_rel[xsl])
                    xv = dma("sync", x1b[:, xsl, :],
                             x1_scr[tt * P:(tt + 1) * P, hg * 512:(hg + 1) * 512],
                             ldx1b[xsl])
                    wait("vector", mm, dn_mm)
                    wait("vector", ev, hs2c_ev)
                    rwait("vector", outst_rel[osl])
                    emit("vector", lambda e, pd=pd, t=ts, hg=hg, o=osl: e.scalar_tensor_tensor(
                        outst[:, o, :], pd[:, :], hs2c[:, t:t + 1],
                        ds_sb[:, hg * 512:(hg + 1) * 512], ALU.add, ALU.mult), None)
                    wait("vector", ldx1b[xsl], xv)
                    ov = emit("vector", lambda e, o=osl, x=xsl: e.tensor_add(
                        outst[:, o, :], outst[:, o, :], x1b[:, x, :]), ev)
                    x1b_rel[xsl] = (ev, ov)
                    psum_rel[ts] = (ev, ov)
                    wait("sync", ev, ov)
                    sv = dma("sync",
                             dout[tt * P:(tt + 1) * P, hg * 512:(hg + 1) * 512],
                             outst[:, osl, :], stout[osl])
                    outst_rel[osl] = (stout[osl], sv)
                if tg + 1 < NQG:
                    if hg == 0:
                        ln2_tp(tg + 1, 0)
                        ln2_stats(tg + 1, 2)
                    elif hg == 1:
                        ln2_tp(tg + 1, 1)
                        ln2_stats(tg + 1, 3)
                        ln2_tp(tg + 1, 2)
                    elif hg == 2:
                        ln2_tp(tg + 1, 3)
            h_pe_done = mm.n
            gu_rel = [(mm, h_pe_done), (mm, h_pe_done)]

        wait("sync", stout[0], stout[0].n)
        wait("sync", stout[1], stout[1].n)
        wait("gpsimd", stout[0], stout[0].n)
        wait("gpsimd", stout[1], stout[1].n)
        flush()

    stack.close()
    return nc


_CACHE = {}


def kernel(L0_attn_x, L0_attn_Wqkv, L0_attn_b_qkv, L0_attn_Wout, L0_attn_b_out,
           L0_ffn_gate_w, L0_ffn_gate_scale, L0_ffn_up_w, L0_ffn_up_scale,
           L0_ffn_down_w, L0_ffn_down_scale):
    x = np.ascontiguousarray(np.asarray(L0_attn_x, dtype=np.float32))
    wqkv = np.ascontiguousarray(np.asarray(L0_attn_Wqkv, dtype=np.float32))
    bqkv = np.ascontiguousarray(np.asarray(L0_attn_b_qkv, dtype=np.float32)).reshape(1, OF)
    wout = np.ascontiguousarray(np.asarray(L0_attn_Wout, dtype=np.float32))
    bout = np.ascontiguousarray(np.asarray(L0_attn_b_out, dtype=np.float32)).reshape(1, DM)
    wg_t = np.ascontiguousarray(np.asarray(L0_ffn_gate_w).T).astype(np.uint8)
    wu_t = np.ascontiguousarray(np.asarray(L0_ffn_up_w).T).astype(np.uint8)
    wd_t = np.ascontiguousarray(np.asarray(L0_ffn_down_w).T).astype(np.uint8)
    gs = np.ascontiguousarray(
        np.asarray(L0_ffn_gate_scale, dtype=np.float32).reshape(64, P).T)
    us = np.ascontiguousarray(
        np.asarray(L0_ffn_up_scale, dtype=np.float32).reshape(64, P).T)
    ds_rep = np.ascontiguousarray(
        np.tile(np.asarray(L0_ffn_down_scale, dtype=np.float32).reshape(1, DM), (P, 1)))
    inv = (1.0 / 10000.0 ** (np.arange(0, 32, 2, dtype=np.float32) / np.float32(32.0)))
    tpos = np.arange(T, dtype=np.float32)
    fr = np.outer(tpos, inv).astype(np.float32)
    cos_t = np.cos(fr).astype(np.float32).reshape(TT, P, 16)
    sin_t = np.sin(fr).astype(np.float32).reshape(TT, P, 16)
    ident = np.eye(P, dtype=np.float32)
    kk = np.arange(P)[:, None]
    qq = np.arange(P)[None, :]
    maskd = np.where(kk > qq, np.float32(-10000.0), np.float32(0.0)).astype(np.float32)

    if "nc" not in _CACHE:
        _CACHE["nc"] = _build()
    nc = _CACHE["nc"]

    shared = dict(wqkv=wqkv, bqkv=bqkv, wout=wout, bout=bout,
                  wg_t=wg_t, wu_t=wu_t, wd_t=wd_t, gs=gs, us=us, ds_rep=ds_rep,
                  cos_t=cos_t, sin_t=sin_t, ident=ident, maskd=maskd)
    in_maps = [dict(x=np.ascontiguousarray(x[b]), **shared) for b in range(NCORES)]
    res = run_bass_kernel_spmd(nc, in_maps, core_ids=list(range(NCORES)))
    return np.stack([np.asarray(res.results[b]["out"]) for b in range(NCORES)], axis=0)


# version 19
# speedup vs baseline: 1.0722x; 1.0363x over previous
import contextlib
import numpy as np

import concourse.bass as bass
import concourse.mybir as mybir
from concourse.bass_utils import run_bass_kernel_spmd

F32 = mybir.dt.float32
BF16 = mybir.dt.bfloat16
U8 = mybir.dt.uint8
AF = mybir.ActivationFunctionType
ALU = mybir.AluOpType
AX = mybir.AxisListType

P = 128
T = 2048
DM = 2048
OF = 6144
IFF = 8192
TT = T // P
DC = DM // P
FC = 16
NQG = 4
NCORES = 8

ENGS = ("tensor", "vector", "scalar", "gpsimd", "sync")


def _build(phases=3.0, dbg=False):
    nc = bass.Bass("TRN2", target_bir_lowering=False)

    dx = nc.dram_tensor("x", [T, DM], F32, kind="ExternalInput")
    dwqkv = nc.dram_tensor("wqkv", [DM, OF], F32, kind="ExternalInput")
    dbqkv = nc.dram_tensor("bqkv", [1, OF], F32, kind="ExternalInput")
    dwout = nc.dram_tensor("wout", [DM, DM], F32, kind="ExternalInput")
    dbout = nc.dram_tensor("bout", [1, DM], F32, kind="ExternalInput")
    dwg = nc.dram_tensor("wg_t", [DM, IFF], U8, kind="ExternalInput")
    dwu = nc.dram_tensor("wu_t", [DM, IFF], U8, kind="ExternalInput")
    dwd = nc.dram_tensor("wd_t", [IFF, DM], U8, kind="ExternalInput")
    dgs = nc.dram_tensor("gs", [P, 64], F32, kind="ExternalInput")
    dus = nc.dram_tensor("us", [P, 64], F32, kind="ExternalInput")
    dds = nc.dram_tensor("ds_rep", [P, DM], F32, kind="ExternalInput")
    dcos = nc.dram_tensor("cos_t", [TT, P, 16], F32, kind="ExternalInput")
    dsin = nc.dram_tensor("sin_t", [TT, P, 16], F32, kind="ExternalInput")
    dident = nc.dram_tensor("ident", [P, P], F32, kind="ExternalInput")
    dmask = nc.dram_tensor("maskd", [P, P], F32, kind="ExternalInput")
    _skind = dict(kind="ExternalOutput") if dbg else {}
    q_scr = nc.dram_tensor("q_scr", [FC, P, T], BF16, **_skind)
    k_scr = nc.dram_tensor("k_scr", [FC, P, T], BF16, **_skind)
    v_scr = nc.dram_tensor("v_scr", [FC, TT, P, 130], BF16, **_skind)
    x1_scr = nc.dram_tensor("x1_scr", [T, DM], F32, **_skind)
    rb_scr = nc.dram_tensor("rb_scr", [2, 1024], BF16)
    if dbg:
        dxn2_dbg = nc.dram_tensor("xn2_dbg", [P, DC, 512], BF16, kind="ExternalOutput")
        dhsr_dbg = nc.dram_tensor("hsr_dbg", [P, 512], F32, kind="ExternalOutput")
        dh_dbg = nc.dram_tensor("h_dbg", [P, 64, 512], BF16, kind="ExternalOutput")
    dout = nc.dram_tensor("out", [T, DM], F32, kind="ExternalOutput")

    ops = {e: [] for e in ENGS}
    waited = {e: {} for e in ENGS}
    stack = contextlib.ExitStack()

    class Ctr:
        __slots__ = ("sem", "step", "n")

        def __init__(self, name, step):
            self.sem = stack.enter_context(nc.semaphore(name))
            self.step = step
            self.n = 0

    def wait(e, ctr, val=None):
        v = ctr.n if val is None else val
        if v <= 0 or waited[e].get(ctr, -1) >= v:
            return
        waited[e][ctr] = v
        sem = ctr.sem
        ops[e].append(lambda eng, sem=sem, v=v: eng.wait_ge(sem, v))

    def rwait(e, rel):
        if rel is not None:
            wait(e, rel[0], rel[1])

    def emit(e, fn, inc=None):
        if e == "vector":
            wait("vector", ev, ev.n)
            if inc is None:
                inc = ev
        if inc is None:
            ops[e].append(fn)
            return None
        inc.n += inc.step
        sem, st = inc.sem, inc.step
        ops[e].append(lambda eng, fn=fn, sem=sem, st=st: fn(eng).then_inc(sem, st))
        return inc.n

    def dma(e, out_ap, in_ap, ctr):
        return emit(e, lambda eng, o=out_ap, i=in_ap: eng.dma_start(o, i), ctr)

    def flush():
        with nc.Block() as blk:
            for e in ENGS:
                lst = ops[e]
                if not lst:
                    continue

                def runner(eng, lst=lst):
                    for f in lst:
                        f(eng)

                getattr(blk, e)(runner)
                ops[e] = []

    def barrier(ctrs):
        for e in ENGS:
            for c in ctrs:
                wait(e, c)

    mm = Ctr("mm", 1)
    ev = Ctr("ev", 1)
    es = Ctr("es", 1)
    pre = Ctr("pre", 16)
    preg = Ctr("preg", 16)
    ldx = [Ctr(f"ldx{i}", 16) for i in range(2)]
    ldw = [Ctr(f"ldw{i}", 16) for i in range(2)]
    stqk = [Ctr(f"stqk{i}", 16) for i in range(4)]
    stv = [Ctr(f"stv{i}", 16) for i in range(2)]
    ldq = [Ctr(f"ldq{i}", 16) for i in range(3)]
    ldk = [Ctr(f"ldk{i}", 16) for i in range(3)]
    ldv = [Ctr(f"ldv{i}", 16) for i in range(3)]
    ldwo = [Ctr(f"ldwo{i}", 16) for i in range(2)]
    ldxr = [Ctr(f"ldxr{i}", 16) for i in range(2)]
    sth2 = [Ctr(f"sth2{i}", 16) for i in range(2)]
    stx1 = [Ctr(f"stx1{i}", 16) for i in range(2)]
    ldx1 = [Ctr(f"ldx1{i}", 16) for i in range(2)]
    ldgu = [Ctr(f"ldgu{i}", 16) for i in range(2)]
    ldwd = [Ctr(f"ldwd{i}", 16) for i in range(2)]
    ldx1b = [Ctr(f"ldx1b{i}", 16) for i in range(2)]
    stout = [Ctr(f"stout{i}", 16) for i in range(2)]
    strb = [Ctr(f"strb{i}", 16) for i in range(2)]
    ldrb = [Ctr(f"ldrb{i}", 16) for i in range(2)]

    ones_sb = stack.enter_context(nc.sbuf_tensor("ones_sb", [P, P], BF16))
    identb = stack.enter_context(nc.sbuf_tensor("identb", [P, P], BF16))
    identf = stack.enter_context(nc.sbuf_tensor("identf", [P, P], F32))
    maskb = stack.enter_context(nc.sbuf_tensor("maskb", [P, P], BF16))
    bqkv_sb = stack.enter_context(nc.sbuf_tensor("bqkv_sb", [1, OF], BF16))
    bout_sb = stack.enter_context(nc.sbuf_tensor("bout_sb", [1, DM], BF16))
    cos_sb = stack.enter_context(nc.sbuf_tensor("cos_sb", [P, TT, 16], F32))
    sin_sb = stack.enter_context(nc.sbuf_tensor("sin_sb", [P, TT, 16], F32))
    gs_sb = stack.enter_context(nc.sbuf_tensor("gs_sb", [P, 64], F32))
    us_sb = stack.enter_context(nc.sbuf_tensor("us_sb", [P, 64], F32))
    ds_sb = stack.enter_context(nc.sbuf_tensor("ds_sb", [P, DM], F32))
    consts = stack.enter_context(nc.sbuf_tensor("consts", [P, 8], F32))
    C_INVD = consts[:, 0:1]
    C_EPS = consts[:, 1:2]
    C_NEG1 = consts[:, 2:3]
    C_8TH = consts[:, 3:4]
    C_NEG4 = consts[:, 4:5]
    C_NEG128 = consts[:, 5:6]
    stats = stack.enter_context(nc.sbuf_tensor("stats", [P, 32], F32))
    x1buf = stack.enter_context(nc.sbuf_tensor("x1buf", [P, 2, DM], F32))
    sq2 = stack.enter_context(nc.sbuf_tensor("sq2", [P, DM], F32))
    xn2tok = stack.enter_context(nc.sbuf_tensor("xn2tok", [P, 2, DM], BF16))

    emit("vector", lambda e: e.memset(ones_sb[:, :], 1.0), ev)
    for col, val in ((0, 1.0 / DM), (1, 1e-5), (2, -1.0), (3, 0.125),
                     (4, -4.0), (5, -128.0)):
        emit("vector", lambda e, c=col, v=val: e.memset(consts[:, c:c + 1], v), ev)
    EV_CONST = ev.n
    dma("gpsimd", identb[:, :], dident[:, :], preg)
    dma("gpsimd", maskb[:, :], dmask[:, :], preg)
    dma("gpsimd", bqkv_sb[:, :], dbqkv[:, :], preg)
    dma("gpsimd", bout_sb[:, :], dbout[:, :], preg)
    dma("sync", identf[:, :], dident[:, :], pre)
    dma("sync", cos_sb[:, :, :], dcos[:, :, :].transpose([1, 0, 2]), pre)
    dma("sync", sin_sb[:, :, :], dsin[:, :, :].transpose([1, 0, 2]), pre)
    dma("sync", gs_sb[:, :], dgs[:, :], pre)
    dma("sync", us_sb[:, :], dus[:, :], pre)
    dma("sync", ds_sb[:, :], dds[:, :], pre)
    PRE_TOTAL = pre.n
    PREG_TOTAL = preg.n
    for e in ("tensor", "vector", "scalar"):
        wait(e, pre, PRE_TOTAL)
        wait(e, preg, PREG_TOTAL)
        wait(e, ev, EV_CONST)

    def emit_ln(xb, sq, par, xn_out):
        b = 16 * par
        S1 = stats[:, b + 0:b + 1]
        S2 = stats[:, b + 1:b + 2]
        M = stats[:, b + 2:b + 3]
        MSQ = stats[:, b + 3:b + 4]
        MM2 = stats[:, b + 4:b + 5]
        VAR = stats[:, b + 5:b + 6]
        STD = stats[:, b + 6:b + 7]
        ISTD = stats[:, b + 7:b + 8]
        NMI = stats[:, b + 8:b + 9]
        emit("vector", lambda e: e.reduce_sum(S1, xb, axis=AX.X), ev)
        sq_done = emit("scalar", lambda e: e.activation(sq, xb, AF.Square), es)
        wait("vector", es, sq_done)
        emit("vector", lambda e: e.reduce_sum(S2, sq, axis=AX.X), ev)
        emit("vector", lambda e: e.tensor_single_scalar(M, S1, C_INVD, ALU.mult), ev)
        emit("vector", lambda e: e.tensor_single_scalar(MSQ, S2, C_INVD, ALU.mult), ev)
        emit("vector", lambda e: e.tensor_mul(MM2, M, M), ev)
        v_done = emit("vector", lambda e: e.tensor_sub(VAR, MSQ, MM2), ev)
        wait("scalar", ev, v_done)
        std_done = emit("scalar", lambda e: e.activation(STD, VAR, AF.Sqrt, bias=C_EPS), es)
        wait("vector", es, std_done)
        emit("vector", lambda e: e.reciprocal(ISTD, STD), ev)
        r_done = emit("vector", lambda e: e.scalar_tensor_tensor(
            NMI, M, C_NEG1, ISTD, ALU.mult, ALU.mult), ev)
        wait("scalar", ev, r_done)
        xn_done = emit("scalar", lambda e: e.activation(
            xn_out, xb, AF.Identity, bias=NMI, scale=ISTD), es)
        return xn_done

    x1_rel = [None, None]
    ln_state = {}
    xn2tok_rel = [None, None]

    def ln2_stats(tg, ts, st_wait=None):
        par = ts % 2
        tt = tg * 4 + ts
        rwait("sync", x1_rel[par])
        if st_wait is not None:
            wait("sync", stx1[0], st_wait[0])
            wait("sync", stx1[1], st_wait[1])
        lv = dma("sync", x1buf[:, par, :], x1_scr[tt * P:(tt + 1) * P, :],
                 ldx1[par])
        wait("vector", ldx1[par], lv)
        wait("scalar", ldx1[par], lv)
        rwait("scalar", xn2tok_rel[par])
        ln_state[(tg, ts)] = emit_ln(
            x1buf[:, par, :], sq2[:, :], par, xn2tok[:, par, :])
        x1_rel[par] = (es, ln_state[(tg, ts)])

    with contextlib.ExitStack() as sa:
        xnf = sa.enter_context(nc.sbuf_tensor("xnf", [P, DC, T], BF16))
        xbuf = sa.enter_context(nc.sbuf_tensor("xbuf", [P, 2, DM], F32))
        sqscr = sa.enter_context(nc.sbuf_tensor("sqscr", [P, DM], F32))
        xntok = sa.enter_context(nc.sbuf_tensor("xntok", [P, 2, DM], BF16))
        wbuf = sa.enter_context(nc.sbuf_tensor("wbuf", [P, 2, DC, 512], BF16))
        ropest = sa.enter_context(nc.sbuf_tensor("ropest", [P, 4, 512], BF16))
        scrA = sa.enter_context(nc.sbuf_tensor("scrA", [P, 8, 16], F32))
        scrB = sa.enter_context(nc.sbuf_tensor("scrB", [P, 8, 16], F32))
        qk2 = sa.enter_context(nc.sbuf_tensor("qk2", [P, 4, P], BF16))
        vstage = sa.enter_context(nc.sbuf_tensor("vstage", [P, 2, 4, 130], BF16))
        pq = [sa.enter_context(nc.psum_tensor(f"pq{i}", [P, 512], F32)) for i in range(4)]
        psT = [sa.enter_context(nc.psum_tensor(f"psT{i}", [P, 1024], BF16)) for i in range(2)]

        for sl in range(2):
            emit("vector", lambda e, s=sl: e.memset(vstage[:, s, :, :], 1.0), ev)

        def load_wgrp(og, slot, rel):
            rwait("gpsimd", rel)
            for dcc in range(DC):
                dma("gpsimd", wbuf[:, slot, dcc, :],
                    dwqkv[dcc * P:(dcc + 1) * P, og * 512:(og + 1) * 512], ldw[slot])
            return ldw[slot].n

        wld = [0, 0]
        wrel = [None, None]
        wld[0] = load_wgrp(0, 0, None)
        wld[1] = load_wgrp(1, 1, None)
        pq_rel = [None, None, None, None]
        qk2_rel = [None] * 4
        ropest_rel = [None, None, None, None]
        vstage_rel = [None, None]
        xbuf_rel = [None, None]
        psT_rel = [None, None]
        xntok_rel = [None, None]
        xnf_cp = [0] * TT
        pend = [None]
        jidx = [0]

        def emit_ln_tile(tt):
            par = tt % 2
            rwait("sync", xbuf_rel[par])
            lv = dma("sync", xbuf[:, par, :], dx[tt * P:(tt + 1) * P, :], ldx[par])
            wait("vector", ldx[par], lv)
            wait("scalar", ldx[par], lv)
            rwait("scalar", xntok_rel[par])
            xn_done = emit_ln(xbuf[:, par, :], sqscr[:, :], par, xntok[:, par, :])
            xbuf_rel[par] = (es, xn_done)
            return xn_done

        def emit_ln_tp(tt, xn_done):
            par = tt % 2
            wait("tensor", es, xn_done)
            for g in range(2):
                sp = g % 2
                rwait("tensor", psT_rel[sp])
                mv = 0
                for c in range(8):
                    dc = g * 8 + c
                    mv = emit("tensor", lambda e, d=dc, c=c, s=sp, p=par: e.transpose(
                        psT[s][:, c * P:(c + 1) * P],
                        xntok[:, p, d * P:(d + 1) * P], identb[:, :]), mm)
                wait("vector", mm, mv)
                dst = xnf[:, g * 8:(g + 1) * 8, tt * P:(tt + 1) * P]
                cv = emit("vector", lambda e, s=sp, dst=dst: e.tensor_copy(
                    dst, psT[s][:, :].rearrange("p (c f) -> p c f", c=8)), ev)
                psT_rel[sp] = (ev, cv)
            xntok_rel[par] = (mm, mm.n)
            xnf_cp[tt] = ev.n

        def mm_job(og, tt):
            par = jidx[0] % 4
            jidx[0] += 1
            wslot = og % 2
            wait("tensor", ldw[wslot], wld[wslot])
            wait("tensor", ev, xnf_cp[tt])
            for _r in (pq_rel[par] or []):
                rwait("tensor", _r)
            for dcc in range(DC):
                emit("tensor", lambda e, d=dcc, t=tt, w=wslot, p=par: e.matmul(
                    pq[p][:, :], xnf[:, d, t * P:(t + 1) * P], wbuf[:, w, d, :],
                    start=(d == 0), stop=False), None)
            mm_tile = emit("tensor", lambda e, o=og, p=par: e.matmul(
                pq[p][:, :], ones_sb[0:1, :], bqkv_sb[0:1, o * 512:(o + 1) * 512],
                start=False, stop=True), mm)
            if tt == TT - 1:
                wrel[wslot] = (mm, mm_tile)
            if pend[0] is not None:
                pend[0]()
                pend[0] = None
            if og < 8:
                isq = og < 4
                scr = q_scr if isq else k_scr
                fcbase = (og % 4) * 4
                pr = pq[par].rearrange("p (h d) -> p h d", h=8)
                st = ropest[:, par, :].rearrange("p (h d) -> p h d", h=8)
                cosb = cos_sb[:, tt, :][:, None, :].broadcast_to((P, 8, 16))
                sinb = sin_sb[:, tt, :][:, None, :].broadcast_to((P, 8, 16))
                wait("vector", mm, mm_tile)
                rwait("vector", ropest_rel[par])
                emit("vector", lambda e, pr=pr, c=cosb: e.tensor_mul(
                    scrA[:, :, :], pr[:, :, 0:16], c), None)
                emit("vector", lambda e, pr=pr, s=sinb: e.tensor_mul(
                    scrB[:, :, :], pr[:, :, 16:32], s), None)
                emit("vector", lambda e, st=st: e.tensor_sub(
                    st[:, :, 0:16], scrA[:, :, :], scrB[:, :, :]), None)
                emit("vector", lambda e, pr=pr, s=sinb: e.tensor_mul(
                    scrA[:, :, :], pr[:, :, 0:16], s), None)
                emit("vector", lambda e, pr=pr, c=cosb: e.tensor_mul(
                    scrB[:, :, :], pr[:, :, 16:32], c), None)
                emit("vector", lambda e, st=st: e.tensor_add(
                    st[:, :, 16:32], scrA[:, :, :], scrB[:, :, :]), None)
                rope_done = emit("vector", lambda e, st=st, pr=pr: e.tensor_copy(
                    st[:, :, 32:64], pr[:, :, 32:64]), ev)
                sp_cp = None
                pq_rel[par] = [(ev, rope_done)]

                def tp(par=par, tt=tt, scr=scr, fcbase=fcbase,
                       rope_done=rope_done):
                    sp = jidx[0] % 2
                    wait("tensor", ev, rope_done)
                    rwait("tensor", psT_rel[sp])
                    mv = 0
                    for c in range(4):
                        mv = emit("tensor", lambda e, c=c, s=sp, p=par: e.transpose(
                            psT[s][:, c * P:(c + 1) * P],
                            ropest[:, p, c * P:(c + 1) * P], identb[:, :]), mm)
                    wait("vector", mm, mv)
                    for c in range(4):
                        rwait("vector", qk2_rel[c])
                    cv = emit("vector", lambda e, s=sp: e.tensor_copy(
                        qk2[:, :, :], psT[s][:, 0:512].rearrange(
                            "p (c f) -> p c f", c=4)), ev)
                    psT_rel[sp] = (ev, cv)
                    wait("sync", ev, cv)
                    for c in range(4):
                        sv = dma("sync", scr[fcbase + c, :, tt * P:(tt + 1) * P],
                                 qk2[:, c, :], stqk[c])
                        qk2_rel[c] = (stqk[c], sv)
                    ropest_rel[par] = (mm, mm.n)
                pend[0] = tp
            else:
                hpbase = (og - 8) * 4
                vsl = jidx[0] % 2
                pr = pq[par].rearrange("p (h x d) -> p h x d", h=4, x=2)
                dstv = vstage[:, vsl, :, :].rearrange(
                    "p h (x e) -> p h x e", x=2)[:, :, :, 0:64]
                wait("vector", mm, mm_tile)
                rwait("vector", vstage_rel[vsl])
                cv = emit("vector", lambda e, pr=pr, d=dstv: e.tensor_copy(
                    d, pr[:, :, :, 0:64]), ev)
                pq_rel[par] = [(ev, cv)]
                wait("sync", ev, cv)
                last = 0
                for c in range(4):
                    last = dma("sync", v_scr[hpbase + c, tt, :, :],
                               vstage[:, vsl, c, :], stv[vsl])
                vstage_rel[vsl] = (stv[vsl], last)
            if tt == TT - 1 and og + 2 < 12:
                wld[wslot] = load_wgrp(og + 2, wslot, wrel[wslot])

        xn_pend = emit_ln_tile(0)
        for tt in range(TT):
            emit_ln_tp(tt, xn_pend)
            if tt + 1 < TT:
                xn_pend = emit_ln_tile(tt + 1)
            mm_job(0, tt)
            mm_job(1, tt)
        for og in range(2, 12):
            for tt in range(TT):
                mm_job(og, tt)
        if pend[0] is not None:
            pend[0]()
            pend[0] = None
        A_CTRS = [mm, ev, es] + stqk + stv
        flush()

    if phases < 2:
        stack.close()
        return nc
    with contextlib.ExitStack() as sb:
        qbuf = sb.enter_context(nc.sbuf_tensor("qbuf", [P, 3, 512], BF16))
        kbuf = sb.enter_context(nc.sbuf_tensor("kbuf", [P, 3, T], BF16))
        vbuf = sb.enter_context(nc.sbuf_tensor("vbuf", [P, 3, TT, 130], BF16))
        ebAB = sb.enter_context(nc.sbuf_tensor("ebAB", [P, 4, 1024], BF16))
        rdbuf = sb.enter_context(nc.sbuf_tensor("rdbuf", [P, 2, 512], BF16))
        repsb = sb.enter_context(nc.sbuf_tensor("repsb", [P, 2, 512], BF16))
        stg2 = sb.enter_context(nc.sbuf_tensor("stg2", [P, 2, 512], BF16))
        oc = sb.enter_context(nc.sbuf_tensor("oc", [P, 2, 2, 512], F32))
        attnout = sb.enter_context(nc.sbuf_tensor("attnout", [P, 2, FC, 512], BF16))
        wobuf = sb.enter_context(nc.sbuf_tensor("wobuf", [P, 2, DC, 512], BF16))
        xresb = sb.enter_context(nc.sbuf_tensor("xresb", [P, 2, 512], F32))
        o1st = sb.enter_context(nc.sbuf_tensor("o1st", [P, 2, 512], F32))
        sAB = [sb.enter_context(nc.psum_tensor(f"sAB{i}", [P, 1024], F32)) for i in range(2)]
        oA = sb.enter_context(nc.psum_tensor("oA", [P, 512], F32))
        oB = sb.enter_context(nc.psum_tensor("oB", [P, 512], F32))
        pw = [sb.enter_context(nc.psum_tensor(f"pw{i}", [P, 512], F32)) for i in range(2)]

        barrier(A_CTRS)

        q_rel = [None, None, None]
        k_rel = [None, None, None]
        v_rel = [None, None, None]
        kld = [0, 0, 0]
        vld = [0, 0, 0]
        qld = [0, 0, 0]
        sAB_rel = [None, None]
        eb_rel = [None, None, None, None]
        o_rel = [None]
        rb_rel = [None, None]
        rd_rel = [None]
        oc_rel = [None, None]
        att_pend = [None]
        stg2_rel = [None, None]
        wout_psum_rel = [None, None]
        wo_rel = [None, None]
        wo_pref = [0, 0]
        xres_rel = [None, None]
        o1st_rel = [None, None]
        units_done = {-2: 0, -1: 0}
        ux1_done = {}
        wout_units = []
        wo_ld = [0, 0]

        def load_job(g):
            jqg, jfc = g // FC, g % FC
            jnkt = 4 * (jqg + 1)
            s = g % 3
            rwait("sync", q_rel[s])
            qld[s] = dma("sync", qbuf[:, s, :],
                         q_scr[jfc, :, jqg * 512:(jqg + 1) * 512], ldq[s])
            rwait("sync", k_rel[s])
            kld[s] = dma("sync", kbuf[:, s, 0:jnkt * P],
                         k_scr[jfc, :, 0:jnkt * P], ldk[s])
            rwait("sync", v_rel[s])
            h1 = (jnkt + 1) // 2
            dma("sync", vbuf[:, s, 0:h1, :],
                v_scr[jfc, 0:h1, :, :].transpose([1, 0, 2]), ldv[s])
            vld[s] = dma("sync", vbuf[:, s, h1:jnkt, :],
                         v_scr[jfc, h1:jnkt, :, :].transpose([1, 0, 2]), ldv[s])

        load_job(0)
        load_job(1)
        for qg in range(NQG):
            nkt = 4 * (qg + 1)
            for fc in range(FC):
                g = qg * FC + fc
                qsl = g % 3
                if g + 2 < NQG * FC:
                    load_job(g + 2)
                qv = qld[qsl]
                qflat = qbuf[:, qsl, :]
                av_q = []
                sm = {}
                for kt in range(nkt):
                    ks = qsl
                    ssl = kt % 2
                    diag = kt >= 4 * qg
                    off = P * (kt - 4 * qg) if diag else 0
                    wait("tensor", ldk[ks], kld[ks])
                    wait("tensor", ldq[qsl], qv)
                    rwait("tensor", sAB_rel[ssl])
                    kap = kbuf[:, ks, kt * P:(kt + 1) * P]
                    if diag:
                        emit("tensor", lambda e, k=kap, s=ssl, o=off, q=qflat: e.matmul(
                            sAB[s][:, o:512], k[0:64, :], q[0:64, o:512],
                            start=True, stop=False, skip_group_check=True), None)
                        emit("tensor", lambda e, s=ssl, o=off: e.matmul(
                            sAB[s][:, o:o + P], identb[:, :], maskb[:, :],
                            start=False, stop=True, skip_group_check=True), None)
                        emit("tensor", lambda e, k=kap, s=ssl, o=off, q=qflat: e.matmul(
                            sAB[s][:, 512 + o:1024], k[64:128, :],
                            q[64:128, o:512],
                            start=True, stop=False, skip_group_check=True), None)
                        sm[kt] = emit("tensor", lambda e, s=ssl, o=off: e.matmul(
                            sAB[s][:, 512 + o:512 + o + P], identb[:, :], maskb[:, :],
                            start=False, stop=True, skip_group_check=True), mm)
                    else:
                        emit("tensor", lambda e, k=kap, s=ssl, q=qflat: e.matmul(
                            sAB[s][:, 0:512], k[0:64, :], q[0:64, :],
                            start=True, stop=True, skip_group_check=True), None)
                        sm[kt] = emit("tensor", lambda e, k=kap, s=ssl, q=qflat: e.matmul(
                            sAB[s][:, 512:1024], k[64:128, :], q[64:128, :],
                            start=True, stop=True, skip_group_check=True), mm)
                    if kt % 2 == 0:
                        continue
                    for k2 in (kt - 1, kt):
                        s2 = k2 % 2
                        d2 = k2 >= 4 * qg
                        o2 = P * (k2 - 4 * qg) if d2 else 0
                        e4 = k2 % 4
                        wait("scalar", mm, sm[k2])
                        rwait("scalar", eb_rel[e4])
                        if d2:
                            exA1 = emit("scalar", lambda e, s=s2, b=e4, o=o2: e.activation(
                                ebAB[:, b, o:512], sAB[s][:, o:512], AF.Exp,
                                bias=C_NEG4, scale=C_8TH), es)
                            exAB = emit("scalar", lambda e, s=s2, b=e4, o=o2: e.activation(
                                ebAB[:, b, 512 + o:1024], sAB[s][:, 512 + o:1024],
                                AF.Exp, bias=C_NEG4, scale=C_8TH), es)
                        else:
                            exAB = emit("scalar", lambda e, s=s2, b=e4: e.activation(
                                ebAB[:, b, 0:1024], sAB[s][:, 0:1024], AF.Exp,
                                bias=C_NEG4, scale=C_8TH), es)
                            exA1 = exAB
                        sAB_rel[s2] = (es, exAB)

                        def av_emit(kt=k2, off=o2, exAB=exAB, exA1=exA1, vs=qsl,
                                    first=(k2 == 0), last=(k2 == nkt - 1)):
                            ebs = kt % 4
                            wait("tensor", es, exA1)
                            wait("tensor", ldv[vs], vld[vs])
                            if first:
                                rwait("tensor", o_rel[0])
                            emit("tensor", lambda e, kt=kt, vs=vs, ebs=ebs, off=off,
                                 first=first, last=last: e.matmul(
                                oA[0:65, off:512], vbuf[:, vs, kt, 0:65],
                                ebAB[:, ebs, off:512],
                                start=first, stop=last, skip_group_check=True), None)
                            wait("tensor", es, exAB)
                            r2 = emit("tensor", lambda e, kt=kt, vs=vs, ebs=ebs, off=off,
                                      first=first, last=last: e.matmul(
                                oB[0:65, off:512], vbuf[:, vs, kt, 65:130],
                                ebAB[:, ebs, 512 + off:1024],
                                start=first, stop=last, skip_group_check=True), mm)
                            eb_rel[ebs] = (mm, r2)
                            return r2
                        av_q.append(av_emit)
                    while len(av_q) > 2:
                        av_q.pop(0)()
                    if kt == 1 and att_pend[0] is not None:
                        att_pend[0]()
                        att_pend[0] = None
                    if kt == 3 and wout_units and fc >= 4:
                        wout_units.pop(0)()
                        if fc in (6, 8, 10, 12) and wout_units:
                            wout_units.pop(0)()
                while av_q:
                    av_q.pop(0)()
                av_last = mm.n
                q_rel[qsl] = (mm, av_last)
                k_rel[qsl] = (mm, av_last)
                v_rel[qsl] = (mm, av_last)
                ocs = fc % 2
                wait("scalar", mm, av_last)
                rwait("scalar", oc_rel[ocs])
                emit("scalar", lambda e, s=ocs: e.activation(
                    oc[0:65, s, 0, :], oA[0:65, :], AF.Identity), None)
                oc_done = emit("scalar", lambda e, s=ocs: e.activation(
                    oc[0:65, s, 1, :], oB[0:65, :], AF.Identity), es)
                o_rel[0] = (es, oc_done)

                def attnorm(fc=fc, ocs=ocs, oc_done=oc_done, aslot=qg % 2,
                            slot_free=units_done[qg - 2]):
                    rsl = fc % 2
                    wait("vector", es, oc_done)
                    rwait("vector", rb_rel[rsl])
                    rwait("vector", rd_rel[0])

                    def _recip(e, dst, src):
                        with nc.allow_low_precision(
                                reason="softmax denominator reciprocal stored "
                                       "bf16 as a broadcast operand"):
                            return e.reciprocal(dst, src)
                    emit("vector", lambda e, s=ocs: _recip(
                        e, rdbuf[64:65, 0, :], oc[64:65, s, 0, :]), ev)
                    rb = emit("vector", lambda e, s=ocs: _recip(
                        e, rdbuf[64:65, 1, :], oc[64:65, s, 1, :]), ev)
                    wait("gpsimd", ev, rb)
                    rwait("gpsimd", rb_rel[rsl])
                    sv1 = dma("gpsimd", rb_scr[rsl:rsl + 1, :],
                              rdbuf[64:65, 0:2, :].rearrange("p a b -> p (a b)"),
                              strb[rsl])
                    rd_rel[0] = (strb[rsl], sv1)
                    wait("gpsimd", strb[rsl], sv1)
                    bv = dma("gpsimd", repsb[0:64, :, :],
                             rb_scr[rsl:rsl + 1, :].broadcast_to((64, 1024)),
                             ldrb[rsl])
                    rb_rel[rsl] = (ldrb[rsl], bv)
                    wait("vector", ldrb[rsl], bv)
                    wait("vector", mm, slot_free)
                    emit("vector", lambda e, f=fc, a=aslot, s=ocs: e.tensor_mul(
                        attnout[0:64, a, f, :], oc[0:64, s, 0, :],
                        repsb[0:64, 0, :]), ev)
                    h2sl = fc % 2
                    rwait("vector", stg2_rel[h2sl])
                    mub = emit("vector", lambda e, s=h2sl, c=ocs: e.tensor_mul(
                        stg2[0:64, s, :], oc[0:64, c, 1, :], repsb[0:64, 1, :]), ev)
                    oc_rel[ocs] = (ev, mub)
                    wait("sync", ev, mub)
                    wait("sync", mm, slot_free)
                    sv = dma("sync", attnout[64:128, aslot, fc, :],
                             stg2[0:64, h2sl, :], sth2[h2sl])
                    stg2_rel[h2sl] = (sth2[h2sl], sv)
                att_pend[0] = attnorm
            for og_p in range(2):
                rwait("gpsimd", wo_rel[og_p])
                for dcc in range(DC):
                    dma("gpsimd", wobuf[:, og_p, dcc, :],
                        dwout[dcc * P:(dcc + 1) * P,
                              og_p * 512:(og_p + 1) * 512], ldwo[og_p])
                wo_pref[og_p] = ldwo[og_p].n
            if att_pend[0] is not None:
                att_pend[0]()
                att_pend[0] = None
            att_ev = ev.n
            att_h2 = [sth2[0].n, sth2[1].n]

            def wout_unit(qg=qg, u=None, att_ev=att_ev, att_h2=att_h2,
                          aslot=qg % 2):
                og, ts = u // 4, u % 4
                wsl = og % 2
                if og < 2:
                    wov = wo_pref[wsl]
                else:
                    wov = wo_ld[wsl]
                psl = ts % 2
                pwt = pw[psl][:, :]
                wait("tensor", ldwo[wsl], wov)
                wait("tensor", ev, att_ev)
                wait("tensor", sth2[0], att_h2[0])
                wait("tensor", sth2[1], att_h2[1])
                rwait("tensor", wout_psum_rel[psl])
                xsl = u % 2
                rwait("gpsimd", xres_rel[xsl])
                xr = dma("gpsimd", xresb[:, xsl, :],
                         dx[qg * 512 + ts * P: qg * 512 + (ts + 1) * P,
                            og * 512:(og + 1) * 512], ldxr[xsl])
                for dcc in range(DC):
                    emit("tensor", lambda e, d=dcc, t=ts, w=wsl, ps=pwt, a=aslot:
                         e.matmul(ps, attnout[:, a, d, t * P:(t + 1) * P],
                                  wobuf[:, w, d, :],
                                  start=(d == 0), stop=False), None)
                wo_mm = emit("tensor", lambda e, o=og, ps=pwt: e.matmul(
                    ps, ones_sb[0:1, :], bout_sb[0:1, o * 512:(o + 1) * 512],
                    start=False, stop=True), mm)
                osl = u % 2
                wait("vector", mm, wo_mm)
                wait("vector", ldxr[xsl], xr)
                rwait("vector", o1st_rel[osl])
                o1v = emit("vector", lambda e, ps=pwt, x=xsl, s=osl: e.tensor_add(
                    o1st[:, s, :], ps, xresb[:, x, :]), ev)
                wout_psum_rel[psl] = (ev, o1v)
                xres_rel[xsl] = (ev, o1v)
                wait("gpsimd", ev, o1v)
                sv = dma("gpsimd",
                         x1_scr[qg * 512 + ts * P: qg * 512 + (ts + 1) * P,
                                og * 512:(og + 1) * 512],
                         o1st[:, osl, :], stx1[osl])
                o1st_rel[osl] = (stx1[osl], sv)
                if u == 3 or u == 7:
                    og2 = 2 if u == 3 else 3
                    wsl2 = og2 % 2
                    wo_rel[wsl2] = (mm, mm.n)
                    rwait("gpsimd", wo_rel[wsl2])
                    for dcc in range(DC):
                        dma("gpsimd", wobuf[:, wsl2, dcc, :],
                            dwout[dcc * P:(dcc + 1) * P, og2 * 512:(og2 + 1) * 512],
                            ldwo[wsl2])
                    wo_ld[wsl2] = ldwo[wsl2].n
                if u == 11:
                    wo_rel[0] = (mm, mm.n)
                if u == 15:
                    wo_rel[1] = (mm, mm.n)
                    units_done[qg] = mm.n
                    ux1_done[qg] = (stx1[0].n, stx1[1].n)

            wout_units = [
                (lambda u=u, f=wout_unit: f(u=u)) for u in range(16)]
            if qg == NQG - 1:
                while wout_units:
                    wout_units.pop(0)()
        ln2_stats(0, 0, st_wait=ux1_done[0])
        ln2_stats(0, 1, st_wait=ux1_done[0])
        B_CTRS = [mm, ev, es] + stx1 + sth2
        flush()

    if phases < 2.1:
        stack.close()
        return nc
    with contextlib.ExitStack() as sc:
        xn2f = sc.enter_context(nc.sbuf_tensor("xn2f", [P, DC, 512], BF16))
        wgub = sc.enter_context(nc.sbuf_tensor("wgub", [P, 2, DC, 512], BF16))
        wdb = wgub
        h_sb = sc.enter_context(nc.sbuf_tensor("h_sb", [P, 64, 512], BF16))
        hsr = sc.enter_context(nc.sbuf_tensor("hsr", [P, 512], F32))
        hs2r = sc.enter_context(nc.sbuf_tensor("hs2r", [P, 512], F32))
        hs2c = sc.enter_context(nc.sbuf_tensor("hs2c", [P, 4], F32))
        gl = sc.enter_context(nc.sbuf_tensor("gl", [P, 2, 512], F32))
        sil = sc.enter_context(nc.sbuf_tensor("sil", [P, 4, 512], F32))
        usb = sc.enter_context(nc.sbuf_tensor("usb", [P, 2, 512], F32))
        x1b = sc.enter_context(nc.sbuf_tensor("x1b", [P, 2, 512], F32))
        outst = sc.enter_context(nc.sbuf_tensor("outst", [P, 2, 512], F32))
        pg = [sc.enter_context(nc.psum_tensor(f"pg{i}", [P, 512], F32)) for i in range(2)]
        pu = [sc.enter_context(nc.psum_tensor(f"pu{i}", [P, 512], F32)) for i in range(2)]
        psh = sc.enter_context(nc.psum_tensor("psh", [P, 512], F32))
        ptp = [sc.enter_context(nc.psum_tensor(f"ptp{i}", [P, 1024], BF16)) for i in range(2)]
        pht = sc.enter_context(nc.psum_tensor("pht", [P, 512], F32))

        barrier(B_CTRS)

        psT_rel2 = [None, None]
        gu_rel = [None, None]
        gu_ld = [0, 0]
        wd_rel = [None, None]
        psum_rel = [None, None, None, None]
        gl_rel = [None, None]
        sil_rel = [None, None, None, None]
        usb_rel = [None, None]
        h_pe_done = 0
        pht_rel = None
        outst_rel = [None, None]
        x1b_rel = [None, None]
        hs2_es_prev = [0]

        guf_prev = [0]
        lnhs = {}

        def ln2_tp(tg, ts):
            par = ts % 2
            wait("vector", mm, guf_prev[0])
            wait("tensor", es, ln_state[(tg, ts)])
            for g in range(2):
                sp = g % 2
                rwait("tensor", psT_rel2[sp])
                mv = 0
                for c in range(8):
                    dcc = g * 8 + c
                    mv = emit("tensor", lambda e, d=dcc, c=c, s=sp, p=par: e.transpose(
                        ptp[s][:, c * P:(c + 1) * P],
                        xn2tok[:, p, d * P:(d + 1) * P], identb[:, :]), mm)
                wait("vector", mm, mv)
                dst = xn2f[:, g * 8:(g + 1) * 8, ts * P:(ts + 1) * P]
                cv = emit("vector", lambda e, s=sp, dst=dst: e.tensor_copy(
                    dst, ptp[s][:, :].rearrange("p (c f) -> p c f", c=8)), ev)
                psT_rel2[sp] = (ev, cv)
            xn2tok_rel[par] = (mm, mm.n)

        def emit_hs(tg):
            xn2_ev = ev.n
            wait("tensor", ev, xn2_ev)
            wait("tensor", es, hs2_es_prev[0])
            for dcc in range(DC):
                hs_mm = emit("tensor", lambda e, d=dcc: e.matmul(
                    psh[:, :], ones_sb[:, :], xn2f[:, d, :],
                    start=(d == 0), stop=(d == DC - 1)),
                    mm if dcc == DC - 1 else None)
            wait("scalar", mm, hs_mm)
            wait("scalar", ev, ev.n)
            hs_es = emit("scalar", lambda e: e.activation(
                hsr[:, :], psh[:, :], AF.Identity, scale=C_NEG128), es)
            lnhs[tg] = (xn2_ev, hs_es)

        ln2_tp(0, 0)
        ln2_stats(0, 2)
        ln2_tp(0, 1)
        ln2_stats(0, 3)
        ln2_tp(0, 2)
        ln2_tp(0, 3)
        emit_hs(0)
        gu_prefetched = [False]
        for tg in range(NQG):
            xn2_ev, hs_es = lnhs[tg]
            hs2_es_prev[0] = hs_es
            wait("vector", mm, h_pe_done)

            def load_gu(kind, ib, slot, rel):
                rwait("gpsimd", rel)
                src = dwg if kind == 0 else dwu
                for dcc in range(DC):
                    dma("gpsimd", wgub[:, slot, dcc, :],
                        src[dcc * P:(dcc + 1) * P, ib * 512:(ib + 1) * 512], ldgu[slot])
                return ldgu[slot].n

            seq = [(k, ib) for ib in range(16) for k in (0, 1)]
            sil_done = [0, 0, 0, 0]
            if not gu_prefetched[0]:
                gu_ld[0] = load_gu(*seq[0], 0, gu_rel[0])
            gu_prefetched[0] = False
            gu_ld[1] = load_gu(*seq[1], 1, gu_rel[1])
            gu_use = [0, 0]
            for si, (kind, ib) in enumerate(seq):
                slot = si % 2
                wait("tensor", ldgu[slot], gu_ld[slot])
                for icl in range(4):
                    ic = ib * 4 + icl
                    ppar = icl % 2
                    pslot = ppar if kind == 0 else 2 + ppar
                    ps = pg[ppar] if kind == 0 else pu[ppar]
                    rwait("tensor", psum_rel[pslot])
                    for dcc in range(DC):
                        mmv = emit("tensor", lambda e, d=dcc, i=icl, s=slot, ps=ps: e.matmul(
                            ps[:, :], wgub[:, s, d, i * P:(i + 1) * P], xn2f[:, d, :],
                            start=(d == 0), stop=(d == DC - 1)),
                            mm if dcc == DC - 1 else None)
                    if kind == 0:
                        wait("vector", mm, mmv)
                        wait("vector", es, hs_es)
                        rwait("vector", gl_rel[ppar])
                        gv = emit("vector", lambda e, ps=ps, p=ppar: e.tensor_add(
                            gl[:, p, :], ps[:, :], hsr[:, :]), ev)
                        psum_rel[pslot] = (ev, gv)
                        wait("scalar", ev, gv)
                        rwait("scalar", sil_rel[icl])
                        sv = emit("scalar", lambda e, p=ppar, il=icl, ic=ic: e.activation(
                            sil[:, il, :], gl[:, p, :], AF.Silu,
                            scale=gs_sb[:, ic:ic + 1]), es)
                        gl_rel[ppar] = (es, sv)
                        sil_done[icl] = sv
                    else:
                        wait("vector", mm, mmv)
                        rwait("vector", usb_rel[ppar])
                        uv = emit("vector", lambda e, ps=ps, p=ppar: e.tensor_add(
                            usb[:, p, :], ps[:, :], hsr[:, :]), ev)
                        psum_rel[pslot] = (ev, uv)
                        wait("vector", es, sil_done[icl])
                        hv = emit("vector", lambda e, p=ppar, il=icl, ic=ic: e.scalar_tensor_tensor(
                            h_sb[:, ic, :], sil[:, il, :], us_sb[:, ic:ic + 1],
                            usb[:, p, :], ALU.mult, ALU.mult), ev)
                        sil_rel[icl] = (ev, hv)
                        usb_rel[ppar] = (ev, hv)
                gu_use[slot] = mm.n
                if si + 2 < len(seq):
                    gu_rel[slot] = (mm, mm.n)
                    gu_ld[slot] = load_gu(*seq[si + 2], slot, gu_rel[slot])
            h_ev = ev.n
            guf_mm = mm.n
            guf_prev[0] = guf_mm
            if dbg and tg == 0:
                wait("sync", ev, h_ev)
                wait("sync", es, hs_es)
                dma("sync", dxn2_dbg[:, :, :], xn2f[:, :, :], pre)
                dma("sync", dhsr_dbg[:, :], hsr[:, :], pre)
                dma("sync", dh_dbg[:, :, :], h_sb[:, :, :], pre)
                wait("vector", pre, pre.n)
                wait("tensor", pre, pre.n)
                wait("scalar", pre, pre.n)
            wait("tensor", ev, h_ev)
            wait("tensor", es, hs_es)
            for ic in range(64):
                hs2_mm = emit("tensor", lambda e, i=ic: e.matmul(
                    psh[:, :], ones_sb[:, :], h_sb[:, i, :],
                    start=(i == 0), stop=(i == 63)),
                    mm if ic == 63 else None)
            wait("scalar", mm, hs2_mm)
            wait("scalar", ev, ev.n)
            hs2_es = emit("scalar", lambda e: e.activation(
                hs2r[:, :], psh[:, :], AF.Identity, scale=C_NEG128), es)
            hs2_es_prev[0] = hs2_es
            wait("tensor", es, hs2_es)
            rwait("tensor", pht_rel)
            mv = 0
            for ts in range(4):
                mv = emit("tensor", lambda e, t=ts: e.transpose(
                    pht[:, t * P:(t + 1) * P], hs2r[:, t * P:(t + 1) * P],
                    identf[:, :]), mm)
            wait("vector", mm, mv)
            cpv = emit("vector", lambda e: e.tensor_copy(
                hs2c[:, 0:4],
                bass.AP(pht.tensor if hasattr(pht, 'tensor') else pht, 0,
                        [[512, P], [P, 4], [1, 1]])), ev)
            pht_rel = (ev, cpv)
            hs2c_ev = ev.n
            if tg + 1 < NQG:
                ln2_stats(tg + 1, 0)
                ln2_stats(tg + 1, 1)
            for hg in range(4):
                for q4 in range(4):
                    wsl = (hg * 4 + q4) % 2
                    rwait("gpsimd", wd_rel[wsl])
                    wait("gpsimd", mm, gu_use[wsl])
                    for icl in range(16):
                        i = q4 * 16 + icl
                        dma("gpsimd", wdb[:, wsl, icl, :],
                            dwd[i * P:(i + 1) * P, hg * 512:(hg + 1) * 512], ldwd[wsl])
                    wv = ldwd[wsl].n
                    wait("tensor", ldwd[wsl], wv)
                    q4last = 0
                    for ts in range(4):
                        pd = (pg + pu)[ts]
                        if q4 == 0:
                            rwait("tensor", psum_rel[ts])
                        for icl in range(16):
                            i = q4 * 16 + icl
                            q4last = emit("tensor",
                                lambda e, i=i, il=icl, w=wsl, t=ts, pd=pd: e.matmul(
                                    pd[:, :], h_sb[:, i, t * P:(t + 1) * P],
                                    wdb[:, w, il, :],
                                    start=(i == 0), stop=(i == 63)),
                                mm if (ts == 3 and icl == 15) else None)
                    wd_rel[wsl] = (mm, q4last)
                    if hg == 3 and q4 == 3 and tg + 1 < NQG:
                        gu_ld[0] = load_gu(0, 0, 0, wd_rel[0])
                        gu_prefetched[0] = True
                if hg == 3 and tg + 1 < NQG:
                    emit_hs(tg + 1)
                dn_mm = mm.n
                for ts in range(4):
                    pd = (pg + pu)[ts]
                    osl = ts % 2
                    xsl = ts % 2
                    tt = tg * 4 + ts
                    rwait("sync", x1b_rel[xsl])
                    xv = dma("sync", x1b[:, xsl, :],
                             x1_scr[tt * P:(tt + 1) * P, hg * 512:(hg + 1) * 512],
                             ldx1b[xsl])
                    wait("vector", mm, dn_mm)
                    wait("vector", ev, hs2c_ev)
                    rwait("vector", outst_rel[osl])
                    emit("vector", lambda e, pd=pd, t=ts, hg=hg, o=osl: e.scalar_tensor_tensor(
                        outst[:, o, :], pd[:, :], hs2c[:, t:t + 1],
                        ds_sb[:, hg * 512:(hg + 1) * 512], ALU.add, ALU.mult), None)
                    wait("vector", ldx1b[xsl], xv)
                    ov = emit("vector", lambda e, o=osl, x=xsl: e.tensor_add(
                        outst[:, o, :], outst[:, o, :], x1b[:, x, :]), ev)
                    x1b_rel[xsl] = (ev, ov)
                    psum_rel[ts] = (ev, ov)
                    wait("sync", ev, ov)
                    sv = dma("sync",
                             dout[tt * P:(tt + 1) * P, hg * 512:(hg + 1) * 512],
                             outst[:, osl, :], stout[osl])
                    outst_rel[osl] = (stout[osl], sv)
                if tg + 1 < NQG:
                    if hg == 0:
                        ln2_tp(tg + 1, 0)
                        ln2_stats(tg + 1, 2)
                    elif hg == 1:
                        ln2_tp(tg + 1, 1)
                        ln2_stats(tg + 1, 3)
                        ln2_tp(tg + 1, 2)
                    elif hg == 2:
                        ln2_tp(tg + 1, 3)
            h_pe_done = mm.n
            gu_rel = [(mm, h_pe_done), (mm, h_pe_done)]

        wait("sync", stout[0], stout[0].n)
        wait("sync", stout[1], stout[1].n)
        wait("gpsimd", stout[0], stout[0].n)
        wait("gpsimd", stout[1], stout[1].n)
        flush()

    stack.close()
    return nc


_CACHE = {}


def kernel(L0_attn_x, L0_attn_Wqkv, L0_attn_b_qkv, L0_attn_Wout, L0_attn_b_out,
           L0_ffn_gate_w, L0_ffn_gate_scale, L0_ffn_up_w, L0_ffn_up_scale,
           L0_ffn_down_w, L0_ffn_down_scale):
    x = np.ascontiguousarray(np.asarray(L0_attn_x, dtype=np.float32))
    wqkv = np.ascontiguousarray(np.asarray(L0_attn_Wqkv, dtype=np.float32))
    bqkv = np.ascontiguousarray(np.asarray(L0_attn_b_qkv, dtype=np.float32)).reshape(1, OF)
    wout = np.ascontiguousarray(np.asarray(L0_attn_Wout, dtype=np.float32))
    bout = np.ascontiguousarray(np.asarray(L0_attn_b_out, dtype=np.float32)).reshape(1, DM)
    wg_t = np.ascontiguousarray(np.asarray(L0_ffn_gate_w).T).astype(np.uint8)
    wu_t = np.ascontiguousarray(np.asarray(L0_ffn_up_w).T).astype(np.uint8)
    wd_t = np.ascontiguousarray(np.asarray(L0_ffn_down_w).T).astype(np.uint8)
    gs = np.ascontiguousarray(
        np.asarray(L0_ffn_gate_scale, dtype=np.float32).reshape(64, P).T)
    us = np.ascontiguousarray(
        np.asarray(L0_ffn_up_scale, dtype=np.float32).reshape(64, P).T)
    ds_rep = np.ascontiguousarray(
        np.tile(np.asarray(L0_ffn_down_scale, dtype=np.float32).reshape(1, DM), (P, 1)))
    inv = (1.0 / 10000.0 ** (np.arange(0, 32, 2, dtype=np.float32) / np.float32(32.0)))
    tpos = np.arange(T, dtype=np.float32)
    fr = np.outer(tpos, inv).astype(np.float32)
    cos_t = np.cos(fr).astype(np.float32).reshape(TT, P, 16)
    sin_t = np.sin(fr).astype(np.float32).reshape(TT, P, 16)
    ident = np.eye(P, dtype=np.float32)
    kk = np.arange(P)[:, None]
    qq = np.arange(P)[None, :]
    maskd = np.where(kk > qq, np.float32(-10000.0), np.float32(0.0)).astype(np.float32)

    if "nc" not in _CACHE:
        _CACHE["nc"] = _build()
    nc = _CACHE["nc"]

    shared = dict(wqkv=wqkv, bqkv=bqkv, wout=wout, bout=bout,
                  wg_t=wg_t, wu_t=wu_t, wd_t=wd_t, gs=gs, us=us, ds_rep=ds_rep,
                  cos_t=cos_t, sin_t=sin_t, ident=ident, maskd=maskd)
    in_maps = [dict(x=np.ascontiguousarray(x[b]), **shared) for b in range(NCORES)]
    res = run_bass_kernel_spmd(nc, in_maps, core_ids=list(range(NCORES)))
    return np.stack([np.asarray(res.results[b]["out"]) for b in range(NCORES)], axis=0)


# version 20
# speedup vs baseline: 1.0756x; 1.0032x over previous
import contextlib
import numpy as np

import concourse.bass as bass
import concourse.mybir as mybir
from concourse.bass_utils import run_bass_kernel_spmd

F32 = mybir.dt.float32
BF16 = mybir.dt.bfloat16
U8 = mybir.dt.uint8
AF = mybir.ActivationFunctionType
ALU = mybir.AluOpType
AX = mybir.AxisListType

P = 128
T = 2048
DM = 2048
OF = 6144
IFF = 8192
TT = T // P
DC = DM // P
FC = 16
NQG = 4
NCORES = 8

ENGS = ("tensor", "vector", "scalar", "gpsimd", "sync")


def _build(phases=3.0, dbg=False):
    nc = bass.Bass("TRN2", target_bir_lowering=False)

    dx = nc.dram_tensor("x", [T, DM], F32, kind="ExternalInput")
    dwqkv = nc.dram_tensor("wqkv", [DM, OF], F32, kind="ExternalInput")
    dbqkv = nc.dram_tensor("bqkv", [1, OF], F32, kind="ExternalInput")
    dwout = nc.dram_tensor("wout", [DM, DM], F32, kind="ExternalInput")
    dbout = nc.dram_tensor("bout", [1, DM], F32, kind="ExternalInput")
    dwg = nc.dram_tensor("wg_t", [DM, IFF], U8, kind="ExternalInput")
    dwu = nc.dram_tensor("wu_t", [DM, IFF], U8, kind="ExternalInput")
    dwd = nc.dram_tensor("wd_t", [IFF, DM], U8, kind="ExternalInput")
    dgs = nc.dram_tensor("gs", [P, 64], F32, kind="ExternalInput")
    dus = nc.dram_tensor("us", [P, 64], F32, kind="ExternalInput")
    dds = nc.dram_tensor("ds_rep", [P, DM], F32, kind="ExternalInput")
    dcos = nc.dram_tensor("cos_t", [TT, P, 16], F32, kind="ExternalInput")
    dsin = nc.dram_tensor("sin_t", [TT, P, 16], F32, kind="ExternalInput")
    dident = nc.dram_tensor("ident", [P, P], F32, kind="ExternalInput")
    dmask = nc.dram_tensor("maskd", [P, P], F32, kind="ExternalInput")
    _skind = dict(kind="ExternalOutput") if dbg else {}
    q_scr = nc.dram_tensor("q_scr", [FC, P, T], BF16, **_skind)
    k_scr = nc.dram_tensor("k_scr", [FC, P, T], BF16, **_skind)
    v_scr = nc.dram_tensor("v_scr", [FC, TT, P, 130], BF16, **_skind)
    x1_scr = nc.dram_tensor("x1_scr", [T, DM], F32, **_skind)
    rb_scr = nc.dram_tensor("rb_scr", [2, 1024], BF16)
    if dbg:
        dxn2_dbg = nc.dram_tensor("xn2_dbg", [P, DC, 512], BF16, kind="ExternalOutput")
        dhsr_dbg = nc.dram_tensor("hsr_dbg", [P, 512], F32, kind="ExternalOutput")
        dh_dbg = nc.dram_tensor("h_dbg", [P, 64, 512], BF16, kind="ExternalOutput")
    dout = nc.dram_tensor("out", [T, DM], F32, kind="ExternalOutput")

    ops = {e: [] for e in ENGS}
    waited = {e: {} for e in ENGS}
    stack = contextlib.ExitStack()

    class Ctr:
        __slots__ = ("sem", "step", "n")

        def __init__(self, name, step):
            self.sem = stack.enter_context(nc.semaphore(name))
            self.step = step
            self.n = 0

    def wait(e, ctr, val=None):
        v = ctr.n if val is None else val
        if v <= 0 or waited[e].get(ctr, -1) >= v:
            return
        waited[e][ctr] = v
        sem = ctr.sem
        ops[e].append(lambda eng, sem=sem, v=v: eng.wait_ge(sem, v))

    def rwait(e, rel):
        if rel is not None:
            wait(e, rel[0], rel[1])

    def emit(e, fn, inc=None):
        if e == "vector":
            wait("vector", ev, ev.n)
            if inc is None:
                inc = ev
        if inc is None:
            ops[e].append(fn)
            return None
        inc.n += inc.step
        sem, st = inc.sem, inc.step
        ops[e].append(lambda eng, fn=fn, sem=sem, st=st: fn(eng).then_inc(sem, st))
        return inc.n

    def dma(e, out_ap, in_ap, ctr):
        return emit(e, lambda eng, o=out_ap, i=in_ap: eng.dma_start(o, i), ctr)

    def flush():
        with nc.Block() as blk:
            for e in ENGS:
                lst = ops[e]
                if not lst:
                    continue

                def runner(eng, lst=lst):
                    for f in lst:
                        f(eng)

                getattr(blk, e)(runner)
                ops[e] = []

    def barrier(ctrs):
        for e in ENGS:
            for c in ctrs:
                wait(e, c)

    mm = Ctr("mm", 1)
    ev = Ctr("ev", 1)
    es = Ctr("es", 1)
    pre = Ctr("pre", 16)
    preg = Ctr("preg", 16)
    ldx = [Ctr(f"ldx{i}", 16) for i in range(2)]
    ldw = [Ctr(f"ldw{i}", 16) for i in range(2)]
    stqk = [Ctr(f"stqk{i}", 16) for i in range(4)]
    stv = [Ctr(f"stv{i}", 16) for i in range(2)]
    ldq = [Ctr(f"ldq{i}", 16) for i in range(3)]
    ldk = [Ctr(f"ldk{i}", 16) for i in range(3)]
    ldv = [Ctr(f"ldv{i}", 16) for i in range(3)]
    ldwo = [Ctr(f"ldwo{i}", 16) for i in range(2)]
    ldxr = [Ctr(f"ldxr{i}", 16) for i in range(2)]
    sth2 = [Ctr(f"sth2{i}", 16) for i in range(2)]
    stx1 = [Ctr(f"stx1{i}", 16) for i in range(2)]
    ldx1 = [Ctr(f"ldx1{i}", 16) for i in range(2)]
    ldgu = [Ctr(f"ldgu{i}", 16) for i in range(2)]
    ldwd = [Ctr(f"ldwd{i}", 16) for i in range(2)]
    ldx1b = [Ctr(f"ldx1b{i}", 16) for i in range(2)]
    stout = [Ctr(f"stout{i}", 16) for i in range(2)]
    strb = [Ctr(f"strb{i}", 16) for i in range(2)]
    ldrb = [Ctr(f"ldrb{i}", 16) for i in range(2)]

    ones_sb = stack.enter_context(nc.sbuf_tensor("ones_sb", [P, P], BF16))
    identb = stack.enter_context(nc.sbuf_tensor("identb", [P, P], BF16))
    identf = stack.enter_context(nc.sbuf_tensor("identf", [P, P], F32))
    maskb = stack.enter_context(nc.sbuf_tensor("maskb", [P, P], BF16))
    bqkv_sb = stack.enter_context(nc.sbuf_tensor("bqkv_sb", [1, OF], BF16))
    bout_sb = stack.enter_context(nc.sbuf_tensor("bout_sb", [1, DM], BF16))
    cos_sb = stack.enter_context(nc.sbuf_tensor("cos_sb", [P, TT, 16], F32))
    sin_sb = stack.enter_context(nc.sbuf_tensor("sin_sb", [P, TT, 16], F32))
    gs_sb = stack.enter_context(nc.sbuf_tensor("gs_sb", [P, 64], F32))
    us_sb = stack.enter_context(nc.sbuf_tensor("us_sb", [P, 64], F32))
    ds_sb = stack.enter_context(nc.sbuf_tensor("ds_sb", [P, DM], F32))
    consts = stack.enter_context(nc.sbuf_tensor("consts", [P, 8], F32))
    C_INVD = consts[:, 0:1]
    C_EPS = consts[:, 1:2]
    C_NEG1 = consts[:, 2:3]
    C_8TH = consts[:, 3:4]
    C_NEG4 = consts[:, 4:5]
    C_NEG128 = consts[:, 5:6]
    stats = stack.enter_context(nc.sbuf_tensor("stats", [P, 32], F32))
    x1buf = stack.enter_context(nc.sbuf_tensor("x1buf", [P, 2, DM], F32))
    sq2 = stack.enter_context(nc.sbuf_tensor("sq2", [P, DM], F32))
    xn2tok = stack.enter_context(nc.sbuf_tensor("xn2tok", [P, 2, DM], BF16))

    emit("vector", lambda e: e.memset(ones_sb[:, :], 1.0), ev)
    for col, val in ((0, 1.0 / DM), (1, 1e-5), (2, -1.0), (3, 0.125),
                     (4, -4.0), (5, -128.0)):
        emit("vector", lambda e, c=col, v=val: e.memset(consts[:, c:c + 1], v), ev)
    EV_CONST = ev.n
    dma("gpsimd", identb[:, :], dident[:, :], preg)
    dma("gpsimd", maskb[:, :], dmask[:, :], preg)
    dma("gpsimd", bqkv_sb[:, :], dbqkv[:, :], preg)
    dma("gpsimd", bout_sb[:, :], dbout[:, :], preg)
    dma("sync", identf[:, :], dident[:, :], pre)
    dma("sync", cos_sb[:, :, :], dcos[:, :, :].transpose([1, 0, 2]), pre)
    dma("sync", sin_sb[:, :, :], dsin[:, :, :].transpose([1, 0, 2]), pre)
    dma("sync", gs_sb[:, :], dgs[:, :], pre)
    dma("sync", us_sb[:, :], dus[:, :], pre)
    dma("sync", ds_sb[:, :], dds[:, :], pre)
    PRE_TOTAL = pre.n
    PREG_TOTAL = preg.n
    for e in ("tensor", "vector", "scalar"):
        wait(e, pre, PRE_TOTAL)
        wait(e, preg, PREG_TOTAL)
        wait(e, ev, EV_CONST)

    def emit_ln(xb, sq, par, xn_out):
        b = 16 * par
        S1 = stats[:, b + 0:b + 1]
        S2 = stats[:, b + 1:b + 2]
        M = stats[:, b + 2:b + 3]
        MSQ = stats[:, b + 3:b + 4]
        MM2 = stats[:, b + 4:b + 5]
        VAR = stats[:, b + 5:b + 6]
        STD = stats[:, b + 6:b + 7]
        ISTD = stats[:, b + 7:b + 8]
        NMI = stats[:, b + 8:b + 9]
        emit("vector", lambda e: e.reduce_sum(S1, xb, axis=AX.X), ev)
        sq_done = emit("scalar", lambda e: e.activation(sq, xb, AF.Square), es)
        wait("vector", es, sq_done)
        emit("vector", lambda e: e.reduce_sum(S2, sq, axis=AX.X), ev)
        emit("vector", lambda e: e.tensor_single_scalar(M, S1, C_INVD, ALU.mult), ev)
        emit("vector", lambda e: e.tensor_single_scalar(MSQ, S2, C_INVD, ALU.mult), ev)
        emit("vector", lambda e: e.tensor_mul(MM2, M, M), ev)
        v_done = emit("vector", lambda e: e.tensor_sub(VAR, MSQ, MM2), ev)
        wait("scalar", ev, v_done)
        std_done = emit("scalar", lambda e: e.activation(STD, VAR, AF.Sqrt, bias=C_EPS), es)
        wait("vector", es, std_done)
        emit("vector", lambda e: e.reciprocal(ISTD, STD), ev)
        r_done = emit("vector", lambda e: e.scalar_tensor_tensor(
            NMI, M, C_NEG1, ISTD, ALU.mult, ALU.mult), ev)
        wait("scalar", ev, r_done)
        xn_done = emit("scalar", lambda e: e.activation(
            xn_out, xb, AF.Identity, bias=NMI, scale=ISTD), es)
        return xn_done

    x1_rel = [None, None]
    ln_state = {}
    xn2tok_rel = [None, None]

    def ln2_stats(tg, ts, st_wait=None):
        par = ts % 2
        tt = tg * 4 + ts
        rwait("sync", x1_rel[par])
        if st_wait is not None:
            wait("sync", stx1[0], st_wait[0])
            wait("sync", stx1[1], st_wait[1])
        lv = dma("sync", x1buf[:, par, :], x1_scr[tt * P:(tt + 1) * P, :],
                 ldx1[par])
        wait("vector", ldx1[par], lv)
        wait("scalar", ldx1[par], lv)
        rwait("scalar", xn2tok_rel[par])
        ln_state[(tg, ts)] = emit_ln(
            x1buf[:, par, :], sq2[:, :], par, xn2tok[:, par, :])
        x1_rel[par] = (es, ln_state[(tg, ts)])

    with contextlib.ExitStack() as sa:
        xnf = sa.enter_context(nc.sbuf_tensor("xnf", [P, DC, T], BF16))
        xbuf = sa.enter_context(nc.sbuf_tensor("xbuf", [P, 2, DM], F32))
        sqscr = sa.enter_context(nc.sbuf_tensor("sqscr", [P, DM], F32))
        xntok = sa.enter_context(nc.sbuf_tensor("xntok", [P, 2, DM], BF16))
        wbuf = sa.enter_context(nc.sbuf_tensor("wbuf", [P, 2, DC, 512], BF16))
        ropest = sa.enter_context(nc.sbuf_tensor("ropest", [P, 4, 512], BF16))
        scrA = sa.enter_context(nc.sbuf_tensor("scrA", [P, 8, 16], F32))
        scrB = sa.enter_context(nc.sbuf_tensor("scrB", [P, 8, 16], F32))
        qk2 = sa.enter_context(nc.sbuf_tensor("qk2", [P, 4, P], BF16))
        vstage = sa.enter_context(nc.sbuf_tensor("vstage", [P, 2, 4, 130], BF16))
        pq = [sa.enter_context(nc.psum_tensor(f"pq{i}", [P, 512], F32)) for i in range(4)]
        psT = [sa.enter_context(nc.psum_tensor(f"psT{i}", [P, 1024], BF16)) for i in range(2)]

        for sl in range(2):
            emit("vector", lambda e, s=sl: e.memset(vstage[:, s, :, :], 1.0), ev)

        def load_wgrp(og, slot, rel):
            rwait("gpsimd", rel)
            for dcc in range(DC):
                dma("gpsimd", wbuf[:, slot, dcc, :],
                    dwqkv[dcc * P:(dcc + 1) * P, og * 512:(og + 1) * 512], ldw[slot])
            return ldw[slot].n

        wld = [0, 0]
        wrel = [None, None]
        wld[0] = load_wgrp(0, 0, None)
        wld[1] = load_wgrp(1, 1, None)
        pq_rel = [None, None, None, None]
        qk2_rel = [None] * 4
        ropest_rel = [None, None, None, None]
        vstage_rel = [None, None]
        xbuf_rel = [None, None]
        psT_rel = [None, None]
        xntok_rel = [None, None]
        xnf_cp = [0] * TT
        pend = [None]
        jidx = [0]

        def emit_ln_tile(tt):
            par = tt % 2
            rwait("sync", xbuf_rel[par])
            lv = dma("sync", xbuf[:, par, :], dx[tt * P:(tt + 1) * P, :], ldx[par])
            wait("vector", ldx[par], lv)
            wait("scalar", ldx[par], lv)
            rwait("scalar", xntok_rel[par])
            xn_done = emit_ln(xbuf[:, par, :], sqscr[:, :], par, xntok[:, par, :])
            xbuf_rel[par] = (es, xn_done)
            return xn_done

        def emit_ln_tp(tt, xn_done):
            par = tt % 2
            wait("tensor", es, xn_done)
            for g in range(2):
                sp = g % 2
                rwait("tensor", psT_rel[sp])
                mv = 0
                for c in range(8):
                    dc = g * 8 + c
                    mv = emit("tensor", lambda e, d=dc, c=c, s=sp, p=par: e.transpose(
                        psT[s][:, c * P:(c + 1) * P],
                        xntok[:, p, d * P:(d + 1) * P], identb[:, :]), mm)
                wait("vector", mm, mv)
                dst = xnf[:, g * 8:(g + 1) * 8, tt * P:(tt + 1) * P]
                cv = emit("vector", lambda e, s=sp, dst=dst: e.tensor_copy(
                    dst, psT[s][:, :].rearrange("p (c f) -> p c f", c=8)), ev)
                psT_rel[sp] = (ev, cv)
            xntok_rel[par] = (mm, mm.n)
            xnf_cp[tt] = ev.n

        def mm_job(og, tt):
            par = jidx[0] % 4
            jidx[0] += 1
            wslot = og % 2
            wait("tensor", ldw[wslot], wld[wslot])
            wait("tensor", ev, xnf_cp[tt])
            for _r in (pq_rel[par] or []):
                rwait("tensor", _r)
            for dcc in range(DC):
                emit("tensor", lambda e, d=dcc, t=tt, w=wslot, p=par: e.matmul(
                    pq[p][:, :], xnf[:, d, t * P:(t + 1) * P], wbuf[:, w, d, :],
                    start=(d == 0), stop=False), None)
            mm_tile = emit("tensor", lambda e, o=og, p=par: e.matmul(
                pq[p][:, :], ones_sb[0:1, :], bqkv_sb[0:1, o * 512:(o + 1) * 512],
                start=False, stop=True), mm)
            if tt == TT - 1:
                wrel[wslot] = (mm, mm_tile)
            if pend[0] is not None:
                pend[0]()
                pend[0] = None
            if og < 8:
                isq = og < 4
                scr = q_scr if isq else k_scr
                fcbase = (og % 4) * 4
                pr = pq[par].rearrange("p (h d) -> p h d", h=8)
                st = ropest[:, par, :].rearrange("p (h d) -> p h d", h=8)
                cosb = cos_sb[:, tt, :][:, None, :].broadcast_to((P, 8, 16))
                sinb = sin_sb[:, tt, :][:, None, :].broadcast_to((P, 8, 16))
                wait("vector", mm, mm_tile)
                rwait("vector", ropest_rel[par])
                emit("vector", lambda e, pr=pr, c=cosb: e.tensor_mul(
                    scrA[:, :, :], pr[:, :, 0:16], c), None)
                emit("vector", lambda e, pr=pr, s=sinb: e.tensor_mul(
                    scrB[:, :, :], pr[:, :, 16:32], s), None)
                emit("vector", lambda e, st=st: e.tensor_sub(
                    st[:, :, 0:16], scrA[:, :, :], scrB[:, :, :]), None)
                emit("vector", lambda e, pr=pr, s=sinb: e.tensor_mul(
                    scrA[:, :, :], pr[:, :, 0:16], s), None)
                emit("vector", lambda e, pr=pr, c=cosb: e.tensor_mul(
                    scrB[:, :, :], pr[:, :, 16:32], c), None)
                emit("vector", lambda e, st=st: e.tensor_add(
                    st[:, :, 16:32], scrA[:, :, :], scrB[:, :, :]), None)
                rope_done = emit("vector", lambda e, st=st, pr=pr: e.tensor_copy(
                    st[:, :, 32:64], pr[:, :, 32:64]), ev)
                sp_cp = None
                pq_rel[par] = [(ev, rope_done)]

                def tp(par=par, tt=tt, scr=scr, fcbase=fcbase,
                       rope_done=rope_done):
                    sp = jidx[0] % 2
                    wait("tensor", ev, rope_done)
                    rwait("tensor", psT_rel[sp])
                    mv = 0
                    for c in range(4):
                        mv = emit("tensor", lambda e, c=c, s=sp, p=par: e.transpose(
                            psT[s][:, c * P:(c + 1) * P],
                            ropest[:, p, c * P:(c + 1) * P], identb[:, :]), mm)
                    wait("vector", mm, mv)
                    for c in range(4):
                        rwait("vector", qk2_rel[c])
                    cv = emit("vector", lambda e, s=sp: e.tensor_copy(
                        qk2[:, :, :], psT[s][:, 0:512].rearrange(
                            "p (c f) -> p c f", c=4)), ev)
                    psT_rel[sp] = (ev, cv)
                    wait("scalar", ev, cv)
                    for c in range(4):
                        sv = dma("scalar", scr[fcbase + c, :, tt * P:(tt + 1) * P],
                                 qk2[:, c, :], stqk[c])
                        qk2_rel[c] = (stqk[c], sv)
                    ropest_rel[par] = (mm, mm.n)
                pend[0] = tp
            else:
                hpbase = (og - 8) * 4
                vsl = jidx[0] % 2
                pr = pq[par].rearrange("p (h x d) -> p h x d", h=4, x=2)
                dstv = vstage[:, vsl, :, :].rearrange(
                    "p h (x e) -> p h x e", x=2)[:, :, :, 0:64]
                wait("vector", mm, mm_tile)
                rwait("vector", vstage_rel[vsl])
                cv = emit("vector", lambda e, pr=pr, d=dstv: e.tensor_copy(
                    d, pr[:, :, :, 0:64]), ev)
                pq_rel[par] = [(ev, cv)]
                wait("sync", ev, cv)
                last = 0
                for c in range(4):
                    last = dma("sync", v_scr[hpbase + c, tt, :, :],
                               vstage[:, vsl, c, :], stv[vsl])
                vstage_rel[vsl] = (stv[vsl], last)
            if tt == TT - 1 and og + 2 < 12:
                wld[wslot] = load_wgrp(og + 2, wslot, wrel[wslot])

        xn_pend = emit_ln_tile(0)
        for tt in range(TT):
            emit_ln_tp(tt, xn_pend)
            if tt + 1 < TT:
                xn_pend = emit_ln_tile(tt + 1)
            mm_job(0, tt)
            mm_job(1, tt)
        for og in range(2, 12):
            for tt in range(TT):
                mm_job(og, tt)
        if pend[0] is not None:
            pend[0]()
            pend[0] = None
        A_CTRS = [mm, ev, es] + stqk + stv
        flush()

    if phases < 2:
        stack.close()
        return nc
    with contextlib.ExitStack() as sb:
        qbuf = sb.enter_context(nc.sbuf_tensor("qbuf", [P, 3, 512], BF16))
        kbuf = sb.enter_context(nc.sbuf_tensor("kbuf", [P, 3, T], BF16))
        vbuf = sb.enter_context(nc.sbuf_tensor("vbuf", [P, 3, TT, 130], BF16))
        ebAB = sb.enter_context(nc.sbuf_tensor("ebAB", [P, 4, 1024], BF16))
        rdbuf = sb.enter_context(nc.sbuf_tensor("rdbuf", [P, 2, 512], BF16))
        repsb = sb.enter_context(nc.sbuf_tensor("repsb", [P, 2, 512], BF16))
        stg2 = sb.enter_context(nc.sbuf_tensor("stg2", [P, 2, 512], BF16))
        oc = sb.enter_context(nc.sbuf_tensor("oc", [P, 2, 2, 512], F32))
        attnout = sb.enter_context(nc.sbuf_tensor("attnout", [P, 2, FC, 512], BF16))
        wobuf = sb.enter_context(nc.sbuf_tensor("wobuf", [P, 2, DC, 512], BF16))
        xresb = sb.enter_context(nc.sbuf_tensor("xresb", [P, 2, 512], F32))
        o1st = sb.enter_context(nc.sbuf_tensor("o1st", [P, 2, 512], F32))
        sAB = [sb.enter_context(nc.psum_tensor(f"sAB{i}", [P, 1024], F32)) for i in range(2)]
        oA = sb.enter_context(nc.psum_tensor("oA", [P, 512], F32))
        oB = sb.enter_context(nc.psum_tensor("oB", [P, 512], F32))
        pw = [sb.enter_context(nc.psum_tensor(f"pw{i}", [P, 512], F32)) for i in range(2)]

        barrier(A_CTRS)

        q_rel = [None, None, None]
        k_rel = [None, None, None]
        v_rel = [None, None, None]
        kld = [0, 0, 0]
        vld = [0, 0, 0]
        qld = [0, 0, 0]
        sAB_rel = [None, None]
        eb_rel = [None, None, None, None]
        o_rel = [None]
        rb_rel = [None, None]
        rd_rel = [None]
        oc_rel = [None, None]
        att_pend = [None]
        stg2_rel = [None, None]
        wout_psum_rel = [None, None]
        wo_rel = [None, None]
        wo_pref = [0, 0]
        xres_rel = [None, None]
        o1st_rel = [None, None]
        units_done = {-2: 0, -1: 0}
        ux1_done = {}
        wout_units = []
        wo_ld = [0, 0]

        def load_job(g):
            jqg, jfc = g // FC, g % FC
            jnkt = 4 * (jqg + 1)
            s = g % 3
            rwait("sync", q_rel[s])
            qld[s] = dma("sync", qbuf[:, s, :],
                         q_scr[jfc, :, jqg * 512:(jqg + 1) * 512], ldq[s])
            rwait("sync", k_rel[s])
            kld[s] = dma("sync", kbuf[:, s, 0:jnkt * P],
                         k_scr[jfc, :, 0:jnkt * P], ldk[s])
            rwait("sync", v_rel[s])
            h1 = (jnkt + 1) // 2
            dma("sync", vbuf[:, s, 0:h1, :],
                v_scr[jfc, 0:h1, :, :].transpose([1, 0, 2]), ldv[s])
            vld[s] = dma("sync", vbuf[:, s, h1:jnkt, :],
                         v_scr[jfc, h1:jnkt, :, :].transpose([1, 0, 2]), ldv[s])

        load_job(0)
        load_job(1)
        for qg in range(NQG):
            nkt = 4 * (qg + 1)
            for fc in range(FC):
                g = qg * FC + fc
                qsl = g % 3
                if g + 2 < NQG * FC:
                    load_job(g + 2)
                qv = qld[qsl]
                qflat = qbuf[:, qsl, :]
                av_q = []
                sm = {}
                for kt in range(nkt):
                    ks = qsl
                    ssl = kt % 2
                    diag = kt >= 4 * qg
                    off = P * (kt - 4 * qg) if diag else 0
                    wait("tensor", ldk[ks], kld[ks])
                    wait("tensor", ldq[qsl], qv)
                    rwait("tensor", sAB_rel[ssl])
                    kap = kbuf[:, ks, kt * P:(kt + 1) * P]
                    if diag:
                        emit("tensor", lambda e, k=kap, s=ssl, o=off, q=qflat: e.matmul(
                            sAB[s][:, o:512], k[0:64, :], q[0:64, o:512],
                            start=True, stop=False, skip_group_check=True), None)
                        emit("tensor", lambda e, s=ssl, o=off: e.matmul(
                            sAB[s][:, o:o + P], identb[:, :], maskb[:, :],
                            start=False, stop=True, skip_group_check=True), None)
                        emit("tensor", lambda e, k=kap, s=ssl, o=off, q=qflat: e.matmul(
                            sAB[s][:, 512 + o:1024], k[64:128, :],
                            q[64:128, o:512],
                            start=True, stop=False, skip_group_check=True), None)
                        sm[kt] = emit("tensor", lambda e, s=ssl, o=off: e.matmul(
                            sAB[s][:, 512 + o:512 + o + P], identb[:, :], maskb[:, :],
                            start=False, stop=True, skip_group_check=True), mm)
                    else:
                        emit("tensor", lambda e, k=kap, s=ssl, q=qflat: e.matmul(
                            sAB[s][:, 0:512], k[0:64, :], q[0:64, :],
                            start=True, stop=True, skip_group_check=True), None)
                        sm[kt] = emit("tensor", lambda e, k=kap, s=ssl, q=qflat: e.matmul(
                            sAB[s][:, 512:1024], k[64:128, :], q[64:128, :],
                            start=True, stop=True, skip_group_check=True), mm)
                    if kt % 2 == 0:
                        continue
                    for k2 in (kt - 1, kt):
                        s2 = k2 % 2
                        d2 = k2 >= 4 * qg
                        o2 = P * (k2 - 4 * qg) if d2 else 0
                        e4 = k2 % 4
                        wait("scalar", mm, sm[k2])
                        rwait("scalar", eb_rel[e4])
                        if d2:
                            exA1 = emit("scalar", lambda e, s=s2, b=e4, o=o2: e.activation(
                                ebAB[:, b, o:512], sAB[s][:, o:512], AF.Exp,
                                bias=C_NEG4, scale=C_8TH), es)
                            exAB = emit("scalar", lambda e, s=s2, b=e4, o=o2: e.activation(
                                ebAB[:, b, 512 + o:1024], sAB[s][:, 512 + o:1024],
                                AF.Exp, bias=C_NEG4, scale=C_8TH), es)
                        else:
                            exAB = emit("scalar", lambda e, s=s2, b=e4: e.activation(
                                ebAB[:, b, 0:1024], sAB[s][:, 0:1024], AF.Exp,
                                bias=C_NEG4, scale=C_8TH), es)
                            exA1 = exAB
                        sAB_rel[s2] = (es, exAB)

                        def av_emit(kt=k2, off=o2, exAB=exAB, exA1=exA1, vs=qsl,
                                    first=(k2 == 0), last=(k2 == nkt - 1)):
                            ebs = kt % 4
                            wait("tensor", es, exA1)
                            wait("tensor", ldv[vs], vld[vs])
                            if first:
                                rwait("tensor", o_rel[0])
                            emit("tensor", lambda e, kt=kt, vs=vs, ebs=ebs, off=off,
                                 first=first, last=last: e.matmul(
                                oA[0:65, off:512], vbuf[:, vs, kt, 0:65],
                                ebAB[:, ebs, off:512],
                                start=first, stop=last, skip_group_check=True), None)
                            wait("tensor", es, exAB)
                            r2 = emit("tensor", lambda e, kt=kt, vs=vs, ebs=ebs, off=off,
                                      first=first, last=last: e.matmul(
                                oB[0:65, off:512], vbuf[:, vs, kt, 65:130],
                                ebAB[:, ebs, 512 + off:1024],
                                start=first, stop=last, skip_group_check=True), mm)
                            eb_rel[ebs] = (mm, r2)
                            return r2
                        av_q.append(av_emit)
                    while len(av_q) > 2:
                        av_q.pop(0)()
                    if kt == 1 and att_pend[0] is not None:
                        att_pend[0]()
                        att_pend[0] = None
                    if kt == 3 and wout_units and fc >= 4:
                        wout_units.pop(0)()
                        if fc in (6, 8, 10, 12) and wout_units:
                            wout_units.pop(0)()
                while av_q:
                    av_q.pop(0)()
                av_last = mm.n
                q_rel[qsl] = (mm, av_last)
                k_rel[qsl] = (mm, av_last)
                v_rel[qsl] = (mm, av_last)
                ocs = fc % 2
                wait("scalar", mm, av_last)
                rwait("scalar", oc_rel[ocs])
                emit("scalar", lambda e, s=ocs: e.activation(
                    oc[0:65, s, 0, :], oA[0:65, :], AF.Identity), None)
                oc_done = emit("scalar", lambda e, s=ocs: e.activation(
                    oc[0:65, s, 1, :], oB[0:65, :], AF.Identity), es)
                o_rel[0] = (es, oc_done)

                def attnorm(fc=fc, ocs=ocs, oc_done=oc_done, aslot=qg % 2,
                            slot_free=units_done[qg - 2]):
                    rsl = fc % 2
                    wait("vector", es, oc_done)
                    rwait("vector", rb_rel[rsl])
                    rwait("vector", rd_rel[0])

                    def _recip(e, dst, src):
                        with nc.allow_low_precision(
                                reason="softmax denominator reciprocal stored "
                                       "bf16 as a broadcast operand"):
                            return e.reciprocal(dst, src)
                    emit("vector", lambda e, s=ocs: _recip(
                        e, rdbuf[64:65, 0, :], oc[64:65, s, 0, :]), ev)
                    rb = emit("vector", lambda e, s=ocs: _recip(
                        e, rdbuf[64:65, 1, :], oc[64:65, s, 1, :]), ev)
                    wait("gpsimd", ev, rb)
                    rwait("gpsimd", rb_rel[rsl])
                    sv1 = dma("gpsimd", rb_scr[rsl:rsl + 1, :],
                              rdbuf[64:65, 0:2, :].rearrange("p a b -> p (a b)"),
                              strb[rsl])
                    rd_rel[0] = (strb[rsl], sv1)
                    wait("gpsimd", strb[rsl], sv1)
                    bv = dma("gpsimd", repsb[0:64, :, :],
                             rb_scr[rsl:rsl + 1, :].broadcast_to((64, 1024)),
                             ldrb[rsl])
                    rb_rel[rsl] = (ldrb[rsl], bv)
                    wait("vector", ldrb[rsl], bv)
                    wait("vector", mm, slot_free)
                    emit("vector", lambda e, f=fc, a=aslot, s=ocs: e.tensor_mul(
                        attnout[0:64, a, f, :], oc[0:64, s, 0, :],
                        repsb[0:64, 0, :]), ev)
                    h2sl = fc % 2
                    rwait("vector", stg2_rel[h2sl])
                    mub = emit("vector", lambda e, s=h2sl, c=ocs: e.tensor_mul(
                        stg2[0:64, s, :], oc[0:64, c, 1, :], repsb[0:64, 1, :]), ev)
                    oc_rel[ocs] = (ev, mub)
                    wait("sync", ev, mub)
                    wait("sync", mm, slot_free)
                    sv = dma("sync", attnout[64:128, aslot, fc, :],
                             stg2[0:64, h2sl, :], sth2[h2sl])
                    stg2_rel[h2sl] = (sth2[h2sl], sv)
                att_pend[0] = attnorm
            for og_p in range(2):
                rwait("gpsimd", wo_rel[og_p])
                for dcc in range(DC):
                    dma("gpsimd", wobuf[:, og_p, dcc, :],
                        dwout[dcc * P:(dcc + 1) * P,
                              og_p * 512:(og_p + 1) * 512], ldwo[og_p])
                wo_pref[og_p] = ldwo[og_p].n
            if att_pend[0] is not None:
                att_pend[0]()
                att_pend[0] = None
            att_ev = ev.n
            att_h2 = [sth2[0].n, sth2[1].n]

            def wout_unit(qg=qg, u=None, att_ev=att_ev, att_h2=att_h2,
                          aslot=qg % 2):
                og, ts = u // 4, u % 4
                wsl = og % 2
                if og < 2:
                    wov = wo_pref[wsl]
                else:
                    wov = wo_ld[wsl]
                psl = ts % 2
                pwt = pw[psl][:, :]
                wait("tensor", ldwo[wsl], wov)
                wait("tensor", ev, att_ev)
                wait("tensor", sth2[0], att_h2[0])
                wait("tensor", sth2[1], att_h2[1])
                rwait("tensor", wout_psum_rel[psl])
                xsl = u % 2
                rwait("gpsimd", xres_rel[xsl])
                xr = dma("gpsimd", xresb[:, xsl, :],
                         dx[qg * 512 + ts * P: qg * 512 + (ts + 1) * P,
                            og * 512:(og + 1) * 512], ldxr[xsl])
                for dcc in range(DC):
                    emit("tensor", lambda e, d=dcc, t=ts, w=wsl, ps=pwt, a=aslot:
                         e.matmul(ps, attnout[:, a, d, t * P:(t + 1) * P],
                                  wobuf[:, w, d, :],
                                  start=(d == 0), stop=False), None)
                wo_mm = emit("tensor", lambda e, o=og, ps=pwt: e.matmul(
                    ps, ones_sb[0:1, :], bout_sb[0:1, o * 512:(o + 1) * 512],
                    start=False, stop=True), mm)
                osl = u % 2
                wait("vector", mm, wo_mm)
                wait("vector", ldxr[xsl], xr)
                rwait("vector", o1st_rel[osl])
                o1v = emit("vector", lambda e, ps=pwt, x=xsl, s=osl: e.tensor_add(
                    o1st[:, s, :], ps, xresb[:, x, :]), ev)
                wout_psum_rel[psl] = (ev, o1v)
                xres_rel[xsl] = (ev, o1v)
                wait("gpsimd", ev, o1v)
                sv = dma("gpsimd",
                         x1_scr[qg * 512 + ts * P: qg * 512 + (ts + 1) * P,
                                og * 512:(og + 1) * 512],
                         o1st[:, osl, :], stx1[osl])
                o1st_rel[osl] = (stx1[osl], sv)
                if u == 3 or u == 7:
                    og2 = 2 if u == 3 else 3
                    wsl2 = og2 % 2
                    wo_rel[wsl2] = (mm, mm.n)
                    rwait("gpsimd", wo_rel[wsl2])
                    for dcc in range(DC):
                        dma("gpsimd", wobuf[:, wsl2, dcc, :],
                            dwout[dcc * P:(dcc + 1) * P, og2 * 512:(og2 + 1) * 512],
                            ldwo[wsl2])
                    wo_ld[wsl2] = ldwo[wsl2].n
                if u == 11:
                    wo_rel[0] = (mm, mm.n)
                if u == 15:
                    wo_rel[1] = (mm, mm.n)
                    units_done[qg] = mm.n
                    ux1_done[qg] = (stx1[0].n, stx1[1].n)

            wout_units = [
                (lambda u=u, f=wout_unit: f(u=u)) for u in range(16)]
            if qg == NQG - 1:
                while wout_units:
                    wout_units.pop(0)()
        ln2_stats(0, 0, st_wait=ux1_done[0])
        ln2_stats(0, 1, st_wait=ux1_done[0])
        B_CTRS = [mm, ev, es] + stx1 + sth2
        flush()

    if phases < 2.1:
        stack.close()
        return nc
    with contextlib.ExitStack() as sc:
        xn2f = sc.enter_context(nc.sbuf_tensor("xn2f", [P, DC, 512], BF16))
        wgub = sc.enter_context(nc.sbuf_tensor("wgub", [P, 2, DC, 512], BF16))
        wdb = wgub
        h_sb = sc.enter_context(nc.sbuf_tensor("h_sb", [P, 64, 512], BF16))
        hsr = sc.enter_context(nc.sbuf_tensor("hsr", [P, 512], F32))
        hs2r = sc.enter_context(nc.sbuf_tensor("hs2r", [P, 512], F32))
        hs2c = sc.enter_context(nc.sbuf_tensor("hs2c", [P, 4], F32))
        gl = sc.enter_context(nc.sbuf_tensor("gl", [P, 2, 512], F32))
        sil = sc.enter_context(nc.sbuf_tensor("sil", [P, 4, 512], F32))
        usb = sc.enter_context(nc.sbuf_tensor("usb", [P, 2, 512], F32))
        x1b = sc.enter_context(nc.sbuf_tensor("x1b", [P, 2, 512], F32))
        outst = sc.enter_context(nc.sbuf_tensor("outst", [P, 2, 512], F32))
        pg = [sc.enter_context(nc.psum_tensor(f"pg{i}", [P, 512], F32)) for i in range(2)]
        pu = [sc.enter_context(nc.psum_tensor(f"pu{i}", [P, 512], F32)) for i in range(2)]
        psh = sc.enter_context(nc.psum_tensor("psh", [P, 512], F32))
        ptp = [sc.enter_context(nc.psum_tensor(f"ptp{i}", [P, 1024], BF16)) for i in range(2)]
        pht = sc.enter_context(nc.psum_tensor("pht", [P, 512], F32))

        barrier(B_CTRS)

        psT_rel2 = [None, None]
        gu_rel = [None, None]
        gu_ld = [0, 0]
        wd_rel = [None, None]
        psum_rel = [None, None, None, None]
        gl_rel = [None, None]
        sil_rel = [None, None, None, None]
        usb_rel = [None, None]
        h_pe_done = 0
        pht_rel = None
        outst_rel = [None, None]
        x1b_rel = [None, None]
        hs2_es_prev = [0]

        guf_prev = [0]
        lnhs = {}

        def ln2_tp(tg, ts):
            par = ts % 2
            wait("vector", mm, guf_prev[0])
            wait("tensor", es, ln_state[(tg, ts)])
            for g in range(2):
                sp = g % 2
                rwait("tensor", psT_rel2[sp])
                mv = 0
                for c in range(8):
                    dcc = g * 8 + c
                    mv = emit("tensor", lambda e, d=dcc, c=c, s=sp, p=par: e.transpose(
                        ptp[s][:, c * P:(c + 1) * P],
                        xn2tok[:, p, d * P:(d + 1) * P], identb[:, :]), mm)
                wait("vector", mm, mv)
                dst = xn2f[:, g * 8:(g + 1) * 8, ts * P:(ts + 1) * P]
                cv = emit("vector", lambda e, s=sp, dst=dst: e.tensor_copy(
                    dst, ptp[s][:, :].rearrange("p (c f) -> p c f", c=8)), ev)
                psT_rel2[sp] = (ev, cv)
            xn2tok_rel[par] = (mm, mm.n)

        def emit_hs(tg):
            xn2_ev = ev.n
            wait("tensor", ev, xn2_ev)
            wait("tensor", es, hs2_es_prev[0])
            for dcc in range(DC):
                hs_mm = emit("tensor", lambda e, d=dcc: e.matmul(
                    psh[:, :], ones_sb[:, :], xn2f[:, d, :],
                    start=(d == 0), stop=(d == DC - 1)),
                    mm if dcc == DC - 1 else None)
            wait("scalar", mm, hs_mm)
            wait("scalar", ev, ev.n)
            hs_es = emit("scalar", lambda e: e.activation(
                hsr[:, :], psh[:, :], AF.Identity, scale=C_NEG128), es)
            lnhs[tg] = (xn2_ev, hs_es)

        ln2_tp(0, 0)
        ln2_stats(0, 2)
        ln2_tp(0, 1)
        ln2_stats(0, 3)
        ln2_tp(0, 2)
        ln2_tp(0, 3)
        emit_hs(0)
        gu_prefetched = [False]
        for tg in range(NQG):
            xn2_ev, hs_es = lnhs[tg]
            hs2_es_prev[0] = hs_es
            wait("vector", mm, h_pe_done)

            def load_gu(kind, ib, slot, rel):
                rwait("gpsimd", rel)
                src = dwg if kind == 0 else dwu
                for dcc in range(DC):
                    dma("gpsimd", wgub[:, slot, dcc, :],
                        src[dcc * P:(dcc + 1) * P, ib * 512:(ib + 1) * 512], ldgu[slot])
                return ldgu[slot].n

            seq = [(k, ib) for ib in range(16) for k in (0, 1)]
            sil_done = [0, 0, 0, 0]
            if not gu_prefetched[0]:
                gu_ld[0] = load_gu(*seq[0], 0, gu_rel[0])
            gu_prefetched[0] = False
            gu_ld[1] = load_gu(*seq[1], 1, gu_rel[1])
            gu_use = [0, 0]
            for si, (kind, ib) in enumerate(seq):
                slot = si % 2
                wait("tensor", ldgu[slot], gu_ld[slot])
                for icl in range(4):
                    ic = ib * 4 + icl
                    ppar = icl % 2
                    pslot = ppar if kind == 0 else 2 + ppar
                    ps = pg[ppar] if kind == 0 else pu[ppar]
                    rwait("tensor", psum_rel[pslot])
                    for dcc in range(DC):
                        mmv = emit("tensor", lambda e, d=dcc, i=icl, s=slot, ps=ps: e.matmul(
                            ps[:, :], wgub[:, s, d, i * P:(i + 1) * P], xn2f[:, d, :],
                            start=(d == 0), stop=(d == DC - 1)),
                            mm if dcc == DC - 1 else None)
                    if kind == 0:
                        wait("vector", mm, mmv)
                        wait("vector", es, hs_es)
                        rwait("vector", gl_rel[ppar])
                        gv = emit("vector", lambda e, ps=ps, p=ppar: e.tensor_add(
                            gl[:, p, :], ps[:, :], hsr[:, :]), ev)
                        psum_rel[pslot] = (ev, gv)
                        wait("scalar", ev, gv)
                        rwait("scalar", sil_rel[icl])
                        sv = emit("scalar", lambda e, p=ppar, il=icl, ic=ic: e.activation(
                            sil[:, il, :], gl[:, p, :], AF.Silu,
                            scale=gs_sb[:, ic:ic + 1]), es)
                        gl_rel[ppar] = (es, sv)
                        sil_done[icl] = sv
                    else:
                        wait("vector", mm, mmv)
                        rwait("vector", usb_rel[ppar])
                        uv = emit("vector", lambda e, ps=ps, p=ppar: e.tensor_add(
                            usb[:, p, :], ps[:, :], hsr[:, :]), ev)
                        psum_rel[pslot] = (ev, uv)
                        wait("vector", es, sil_done[icl])
                        hv = emit("vector", lambda e, p=ppar, il=icl, ic=ic: e.scalar_tensor_tensor(
                            h_sb[:, ic, :], sil[:, il, :], us_sb[:, ic:ic + 1],
                            usb[:, p, :], ALU.mult, ALU.mult), ev)
                        sil_rel[icl] = (ev, hv)
                        usb_rel[ppar] = (ev, hv)
                gu_use[slot] = mm.n
                if si + 2 < len(seq):
                    gu_rel[slot] = (mm, mm.n)
                    gu_ld[slot] = load_gu(*seq[si + 2], slot, gu_rel[slot])
            h_ev = ev.n
            guf_mm = mm.n
            guf_prev[0] = guf_mm
            if dbg and tg == 0:
                wait("sync", ev, h_ev)
                wait("sync", es, hs_es)
                dma("sync", dxn2_dbg[:, :, :], xn2f[:, :, :], pre)
                dma("sync", dhsr_dbg[:, :], hsr[:, :], pre)
                dma("sync", dh_dbg[:, :, :], h_sb[:, :, :], pre)
                wait("vector", pre, pre.n)
                wait("tensor", pre, pre.n)
                wait("scalar", pre, pre.n)
            wait("tensor", ev, h_ev)
            wait("tensor", es, hs_es)
            for ic in range(64):
                hs2_mm = emit("tensor", lambda e, i=ic: e.matmul(
                    psh[:, :], ones_sb[:, :], h_sb[:, i, :],
                    start=(i == 0), stop=(i == 63)),
                    mm if ic == 63 else None)
            wait("scalar", mm, hs2_mm)
            wait("scalar", ev, ev.n)
            hs2_es = emit("scalar", lambda e: e.activation(
                hs2r[:, :], psh[:, :], AF.Identity, scale=C_NEG128), es)
            hs2_es_prev[0] = hs2_es
            wait("tensor", es, hs2_es)
            rwait("tensor", pht_rel)
            mv = 0
            for ts in range(4):
                mv = emit("tensor", lambda e, t=ts: e.transpose(
                    pht[:, t * P:(t + 1) * P], hs2r[:, t * P:(t + 1) * P],
                    identf[:, :]), mm)
            wait("vector", mm, mv)
            cpv = emit("vector", lambda e: e.tensor_copy(
                hs2c[:, 0:4],
                bass.AP(pht.tensor if hasattr(pht, 'tensor') else pht, 0,
                        [[512, P], [P, 4], [1, 1]])), ev)
            pht_rel = (ev, cpv)
            hs2c_ev = ev.n
            if tg + 1 < NQG:
                ln2_stats(tg + 1, 0)
                ln2_stats(tg + 1, 1)
            for hg in range(4):
                for q4 in range(4):
                    wsl = (hg * 4 + q4) % 2
                    rwait("gpsimd", wd_rel[wsl])
                    wait("gpsimd", mm, gu_use[wsl])
                    for icl in range(16):
                        i = q4 * 16 + icl
                        dma("gpsimd", wdb[:, wsl, icl, :],
                            dwd[i * P:(i + 1) * P, hg * 512:(hg + 1) * 512], ldwd[wsl])
                    wv = ldwd[wsl].n
                    wait("tensor", ldwd[wsl], wv)
                    q4last = 0
                    for ts in range(4):
                        pd = (pg + pu)[ts]
                        if q4 == 0:
                            rwait("tensor", psum_rel[ts])
                        for icl in range(16):
                            i = q4 * 16 + icl
                            q4last = emit("tensor",
                                lambda e, i=i, il=icl, w=wsl, t=ts, pd=pd: e.matmul(
                                    pd[:, :], h_sb[:, i, t * P:(t + 1) * P],
                                    wdb[:, w, il, :],
                                    start=(i == 0), stop=(i == 63)),
                                mm if (ts == 3 and icl == 15) else None)
                    wd_rel[wsl] = (mm, q4last)
                    if hg == 3 and q4 == 3 and tg + 1 < NQG:
                        gu_ld[0] = load_gu(0, 0, 0, wd_rel[0])
                        gu_prefetched[0] = True
                if hg == 3 and tg + 1 < NQG:
                    emit_hs(tg + 1)
                dn_mm = mm.n
                for ts in range(4):
                    pd = (pg + pu)[ts]
                    osl = ts % 2
                    xsl = ts % 2
                    tt = tg * 4 + ts
                    rwait("sync", x1b_rel[xsl])
                    xv = dma("sync", x1b[:, xsl, :],
                             x1_scr[tt * P:(tt + 1) * P, hg * 512:(hg + 1) * 512],
                             ldx1b[xsl])
                    wait("vector", mm, dn_mm)
                    wait("vector", ev, hs2c_ev)
                    rwait("vector", outst_rel[osl])
                    emit("vector", lambda e, pd=pd, t=ts, hg=hg, o=osl: e.scalar_tensor_tensor(
                        outst[:, o, :], pd[:, :], hs2c[:, t:t + 1],
                        ds_sb[:, hg * 512:(hg + 1) * 512], ALU.add, ALU.mult), None)
                    wait("vector", ldx1b[xsl], xv)
                    ov = emit("vector", lambda e, o=osl, x=xsl: e.tensor_add(
                        outst[:, o, :], outst[:, o, :], x1b[:, x, :]), ev)
                    x1b_rel[xsl] = (ev, ov)
                    psum_rel[ts] = (ev, ov)
                    wait("scalar", ev, ov)
                    sv = dma("scalar",
                             dout[tt * P:(tt + 1) * P, hg * 512:(hg + 1) * 512],
                             outst[:, osl, :], stout[osl])
                    outst_rel[osl] = (stout[osl], sv)
                if tg + 1 < NQG:
                    if hg == 0:
                        ln2_tp(tg + 1, 0)
                        ln2_stats(tg + 1, 2)
                    elif hg == 1:
                        ln2_tp(tg + 1, 1)
                        ln2_stats(tg + 1, 3)
                        ln2_tp(tg + 1, 2)
                    elif hg == 2:
                        ln2_tp(tg + 1, 3)
            h_pe_done = mm.n
            gu_rel = [(mm, h_pe_done), (mm, h_pe_done)]

        wait("sync", stout[0], stout[0].n)
        wait("sync", stout[1], stout[1].n)
        wait("gpsimd", stout[0], stout[0].n)
        wait("gpsimd", stout[1], stout[1].n)
        flush()

    stack.close()
    return nc


_CACHE = {}


def kernel(L0_attn_x, L0_attn_Wqkv, L0_attn_b_qkv, L0_attn_Wout, L0_attn_b_out,
           L0_ffn_gate_w, L0_ffn_gate_scale, L0_ffn_up_w, L0_ffn_up_scale,
           L0_ffn_down_w, L0_ffn_down_scale):
    x = np.ascontiguousarray(np.asarray(L0_attn_x, dtype=np.float32))
    wqkv = np.ascontiguousarray(np.asarray(L0_attn_Wqkv, dtype=np.float32))
    bqkv = np.ascontiguousarray(np.asarray(L0_attn_b_qkv, dtype=np.float32)).reshape(1, OF)
    wout = np.ascontiguousarray(np.asarray(L0_attn_Wout, dtype=np.float32))
    bout = np.ascontiguousarray(np.asarray(L0_attn_b_out, dtype=np.float32)).reshape(1, DM)
    wg_t = np.ascontiguousarray(np.asarray(L0_ffn_gate_w).T).astype(np.uint8)
    wu_t = np.ascontiguousarray(np.asarray(L0_ffn_up_w).T).astype(np.uint8)
    wd_t = np.ascontiguousarray(np.asarray(L0_ffn_down_w).T).astype(np.uint8)
    gs = np.ascontiguousarray(
        np.asarray(L0_ffn_gate_scale, dtype=np.float32).reshape(64, P).T)
    us = np.ascontiguousarray(
        np.asarray(L0_ffn_up_scale, dtype=np.float32).reshape(64, P).T)
    ds_rep = np.ascontiguousarray(
        np.tile(np.asarray(L0_ffn_down_scale, dtype=np.float32).reshape(1, DM), (P, 1)))
    inv = (1.0 / 10000.0 ** (np.arange(0, 32, 2, dtype=np.float32) / np.float32(32.0)))
    tpos = np.arange(T, dtype=np.float32)
    fr = np.outer(tpos, inv).astype(np.float32)
    cos_t = np.cos(fr).astype(np.float32).reshape(TT, P, 16)
    sin_t = np.sin(fr).astype(np.float32).reshape(TT, P, 16)
    ident = np.eye(P, dtype=np.float32)
    kk = np.arange(P)[:, None]
    qq = np.arange(P)[None, :]
    maskd = np.where(kk > qq, np.float32(-10000.0), np.float32(0.0)).astype(np.float32)

    if "nc" not in _CACHE:
        _CACHE["nc"] = _build()
    nc = _CACHE["nc"]

    shared = dict(wqkv=wqkv, bqkv=bqkv, wout=wout, bout=bout,
                  wg_t=wg_t, wu_t=wu_t, wd_t=wd_t, gs=gs, us=us, ds_rep=ds_rep,
                  cos_t=cos_t, sin_t=sin_t, ident=ident, maskd=maskd)
    in_maps = [dict(x=np.ascontiguousarray(x[b]), **shared) for b in range(NCORES)]
    res = run_bass_kernel_spmd(nc, in_maps, core_ids=list(range(NCORES)))
    return np.stack([np.asarray(res.results[b]["out"]) for b in range(NCORES)], axis=0)


# version 21 (final)
# speedup vs baseline: 1.0885x; 1.0120x over previous
import contextlib
import numpy as np

import concourse.bass as bass
import concourse.mybir as mybir
from concourse.bass_utils import run_bass_kernel_spmd

F32 = mybir.dt.float32
BF16 = mybir.dt.bfloat16
U8 = mybir.dt.uint8
AF = mybir.ActivationFunctionType
ALU = mybir.AluOpType
AX = mybir.AxisListType

P = 128
T = 2048
DM = 2048
OF = 6144
IFF = 8192
TT = T // P
DC = DM // P
FC = 16
NQG = 4
NCORES = 8

ENGS = ("tensor", "vector", "scalar", "gpsimd", "sync")


def _build(phases=3.0, dbg=False):
    nc = bass.Bass("TRN2", target_bir_lowering=False)

    dx = nc.dram_tensor("x", [T, DM], F32, kind="ExternalInput")
    dwqkv = nc.dram_tensor("wqkv", [DM, OF], F32, kind="ExternalInput")
    dbqkv = nc.dram_tensor("bqkv", [1, OF], F32, kind="ExternalInput")
    dwout = nc.dram_tensor("wout", [DM, DM], F32, kind="ExternalInput")
    dbout = nc.dram_tensor("bout", [1, DM], F32, kind="ExternalInput")
    dwg = nc.dram_tensor("wg_t", [DM, IFF], U8, kind="ExternalInput")
    dwu = nc.dram_tensor("wu_t", [DM, IFF], U8, kind="ExternalInput")
    dwd = nc.dram_tensor("wd_t", [IFF, DM], U8, kind="ExternalInput")
    dgs = nc.dram_tensor("gs", [P, 64], F32, kind="ExternalInput")
    dus = nc.dram_tensor("us", [P, 64], F32, kind="ExternalInput")
    dds = nc.dram_tensor("ds_rep", [P, DM], F32, kind="ExternalInput")
    dcos = nc.dram_tensor("cos_t", [TT, P, 16], F32, kind="ExternalInput")
    dsin = nc.dram_tensor("sin_t", [TT, P, 16], F32, kind="ExternalInput")
    dident = nc.dram_tensor("ident", [P, P], F32, kind="ExternalInput")
    dmask = nc.dram_tensor("maskd", [P, P], F32, kind="ExternalInput")
    _skind = dict(kind="ExternalOutput") if dbg else {}
    q_scr = nc.dram_tensor("q_scr", [FC, P, T], BF16, **_skind)
    k_scr = nc.dram_tensor("k_scr", [FC, P, T], BF16, **_skind)
    v_scr = nc.dram_tensor("v_scr", [FC, TT, P, 130], BF16, **_skind)
    x1_scr = nc.dram_tensor("x1_scr", [T, DM], F32, **_skind)
    rb_scr = nc.dram_tensor("rb_scr", [2, 1024], BF16)
    if dbg:
        dxn2_dbg = nc.dram_tensor("xn2_dbg", [P, DC, 512], BF16, kind="ExternalOutput")
        dhsr_dbg = nc.dram_tensor("hsr_dbg", [P, 512], F32, kind="ExternalOutput")
        dh_dbg = nc.dram_tensor("h_dbg", [P, 64, 512], BF16, kind="ExternalOutput")
    dout = nc.dram_tensor("out", [T, DM], F32, kind="ExternalOutput")

    ops = {e: [] for e in ENGS}
    waited = {e: {} for e in ENGS}
    stack = contextlib.ExitStack()

    class Ctr:
        __slots__ = ("sem", "step", "n")

        def __init__(self, name, step):
            self.sem = stack.enter_context(nc.semaphore(name))
            self.step = step
            self.n = 0

    def wait(e, ctr, val=None):
        v = ctr.n if val is None else val
        if v <= 0 or waited[e].get(ctr, -1) >= v:
            return
        waited[e][ctr] = v
        sem = ctr.sem
        ops[e].append(lambda eng, sem=sem, v=v: eng.wait_ge(sem, v))

    def rwait(e, rel):
        if rel is not None:
            wait(e, rel[0], rel[1])

    def emit(e, fn, inc=None):
        if e == "vector":
            wait("vector", ev, ev.n)
            if inc is None:
                inc = ev
        if inc is None:
            ops[e].append(fn)
            return None
        inc.n += inc.step
        sem, st = inc.sem, inc.step
        ops[e].append(lambda eng, fn=fn, sem=sem, st=st: fn(eng).then_inc(sem, st))
        return inc.n

    def dma(e, out_ap, in_ap, ctr):
        return emit(e, lambda eng, o=out_ap, i=in_ap: eng.dma_start(o, i), ctr)

    def flush():
        with nc.Block() as blk:
            for e in ENGS:
                lst = ops[e]
                if not lst:
                    continue

                def runner(eng, lst=lst):
                    for f in lst:
                        f(eng)

                getattr(blk, e)(runner)
                ops[e] = []

    def barrier(ctrs):
        for e in ENGS:
            for c in ctrs:
                wait(e, c)

    mm = Ctr("mm", 1)
    ev = Ctr("ev", 1)
    es = Ctr("es", 1)
    pre = Ctr("pre", 16)
    preg = Ctr("preg", 16)
    ldx = [Ctr(f"ldx{i}", 16) for i in range(2)]
    ldw = [Ctr(f"ldw{i}", 16) for i in range(2)]
    stqk = [Ctr(f"stqk{i}", 16) for i in range(4)]
    stv = [Ctr(f"stv{i}", 16) for i in range(2)]
    ldq = [Ctr(f"ldq{i}", 16) for i in range(3)]
    ldk = [Ctr(f"ldk{i}", 16) for i in range(3)]
    ldv = [Ctr(f"ldv{i}", 16) for i in range(3)]
    ldwo = [Ctr(f"ldwo{i}", 16) for i in range(2)]
    ldxr = [Ctr(f"ldxr{i}", 16) for i in range(2)]
    sth2 = [Ctr(f"sth2{i}", 16) for i in range(2)]
    stx1 = [Ctr(f"stx1{i}", 16) for i in range(2)]
    ldx1 = [Ctr(f"ldx1{i}", 16) for i in range(2)]
    ldgu = [Ctr(f"ldgu{i}", 16) for i in range(2)]
    ldwd = [Ctr(f"ldwd{i}", 16) for i in range(2)]
    ldx1b = [Ctr(f"ldx1b{i}", 16) for i in range(2)]
    stout = [Ctr(f"stout{i}", 16) for i in range(2)]
    strb = [Ctr(f"strb{i}", 16) for i in range(2)]
    ldrb = [Ctr(f"ldrb{i}", 16) for i in range(2)]

    ones_sb = stack.enter_context(nc.sbuf_tensor("ones_sb", [P, P], BF16))
    identb = stack.enter_context(nc.sbuf_tensor("identb", [P, P], BF16))
    identf = stack.enter_context(nc.sbuf_tensor("identf", [P, P], F32))
    maskb = stack.enter_context(nc.sbuf_tensor("maskb", [P, P], BF16))
    bqkv_sb = stack.enter_context(nc.sbuf_tensor("bqkv_sb", [1, OF], BF16))
    bout_sb = stack.enter_context(nc.sbuf_tensor("bout_sb", [1, DM], BF16))
    cos_sb = stack.enter_context(nc.sbuf_tensor("cos_sb", [P, TT, 16], F32))
    sin_sb = stack.enter_context(nc.sbuf_tensor("sin_sb", [P, TT, 16], F32))
    gs_sb = stack.enter_context(nc.sbuf_tensor("gs_sb", [P, 64], F32))
    us_sb = stack.enter_context(nc.sbuf_tensor("us_sb", [P, 64], F32))
    ds_sb = stack.enter_context(nc.sbuf_tensor("ds_sb", [P, DM], F32))
    consts = stack.enter_context(nc.sbuf_tensor("consts", [P, 8], F32))
    C_INVD = consts[:, 0:1]
    C_EPS = consts[:, 1:2]
    C_NEG1 = consts[:, 2:3]
    C_8TH = consts[:, 3:4]
    C_NEG4 = consts[:, 4:5]
    C_NEG128 = consts[:, 5:6]
    stats = stack.enter_context(nc.sbuf_tensor("stats", [P, 32], F32))
    x1buf = stack.enter_context(nc.sbuf_tensor("x1buf", [P, 2, DM], F32))
    sq2 = stack.enter_context(nc.sbuf_tensor("sq2", [P, DM], F32))
    xn2tok = stack.enter_context(nc.sbuf_tensor("xn2tok", [P, 2, DM], BF16))

    emit("vector", lambda e: e.memset(ones_sb[:, :], 1.0), ev)
    for col, val in ((0, 1.0 / DM), (1, 1e-5), (2, -1.0), (3, 0.125),
                     (4, -4.0), (5, -128.0)):
        emit("vector", lambda e, c=col, v=val: e.memset(consts[:, c:c + 1], v), ev)
    EV_CONST = ev.n
    dma("gpsimd", identb[:, :], dident[:, :], preg)
    dma("gpsimd", maskb[:, :], dmask[:, :], preg)
    dma("gpsimd", bqkv_sb[:, :], dbqkv[:, :], preg)
    dma("gpsimd", bout_sb[:, :], dbout[:, :], preg)
    dma("sync", identf[:, :], dident[:, :], pre)
    dma("sync", cos_sb[:, :, :], dcos[:, :, :].transpose([1, 0, 2]), pre)
    dma("sync", sin_sb[:, :, :], dsin[:, :, :].transpose([1, 0, 2]), pre)
    dma("sync", gs_sb[:, :], dgs[:, :], pre)
    dma("sync", us_sb[:, :], dus[:, :], pre)
    dma("sync", ds_sb[:, :], dds[:, :], pre)
    PRE_TOTAL = pre.n
    PREG_TOTAL = preg.n
    for e in ("tensor", "vector", "scalar"):
        wait(e, pre, PRE_TOTAL)
        wait(e, preg, PREG_TOTAL)
        wait(e, ev, EV_CONST)

    def emit_ln(xb, sq, par, xn_out):
        b = 16 * par
        S1 = stats[:, b + 0:b + 1]
        S2 = stats[:, b + 1:b + 2]
        M = stats[:, b + 2:b + 3]
        MSQ = stats[:, b + 3:b + 4]
        MM2 = stats[:, b + 4:b + 5]
        VAR = stats[:, b + 5:b + 6]
        STD = stats[:, b + 6:b + 7]
        ISTD = stats[:, b + 7:b + 8]
        NMI = stats[:, b + 8:b + 9]
        emit("vector", lambda e: e.reduce_sum(S1, xb, axis=AX.X), ev)
        sq_done = emit("scalar", lambda e: e.activation(sq, xb, AF.Square), es)
        wait("vector", es, sq_done)
        emit("vector", lambda e: e.reduce_sum(S2, sq, axis=AX.X), ev)
        emit("vector", lambda e: e.tensor_single_scalar(M, S1, C_INVD, ALU.mult), ev)
        emit("vector", lambda e: e.tensor_single_scalar(MSQ, S2, C_INVD, ALU.mult), ev)
        emit("vector", lambda e: e.tensor_mul(MM2, M, M), ev)
        v_done = emit("vector", lambda e: e.tensor_sub(VAR, MSQ, MM2), ev)
        wait("scalar", ev, v_done)
        std_done = emit("scalar", lambda e: e.activation(STD, VAR, AF.Sqrt, bias=C_EPS), es)
        wait("vector", es, std_done)
        emit("vector", lambda e: e.reciprocal(ISTD, STD), ev)
        r_done = emit("vector", lambda e: e.scalar_tensor_tensor(
            NMI, M, C_NEG1, ISTD, ALU.mult, ALU.mult), ev)
        wait("scalar", ev, r_done)
        xn_done = emit("scalar", lambda e: e.activation(
            xn_out, xb, AF.Identity, bias=NMI, scale=ISTD), es)
        return xn_done

    x1_rel = [None, None]
    ln_state = {}
    xn2tok_rel = [None, None]

    def ln2_stats(tg, ts, st_wait=None):
        par = ts % 2
        tt = tg * 4 + ts
        rwait("sync", x1_rel[par])
        if st_wait is not None:
            wait("sync", stx1[0], st_wait[0])
            wait("sync", stx1[1], st_wait[1])
        lv = dma("sync", x1buf[:, par, :], x1_scr[tt * P:(tt + 1) * P, :],
                 ldx1[par])
        wait("vector", ldx1[par], lv)
        wait("scalar", ldx1[par], lv)
        rwait("scalar", xn2tok_rel[par])
        ln_state[(tg, ts)] = emit_ln(
            x1buf[:, par, :], sq2[:, :], par, xn2tok[:, par, :])
        x1_rel[par] = (es, ln_state[(tg, ts)])

    with contextlib.ExitStack() as sa:
        xnf = sa.enter_context(nc.sbuf_tensor("xnf", [P, DC, T], BF16))
        xbuf = sa.enter_context(nc.sbuf_tensor("xbuf", [P, 2, DM], F32))
        sqscr = sa.enter_context(nc.sbuf_tensor("sqscr", [P, DM], F32))
        xntok = sa.enter_context(nc.sbuf_tensor("xntok", [P, 2, DM], BF16))
        wbuf = sa.enter_context(nc.sbuf_tensor("wbuf", [P, 2, DC, 512], BF16))
        ropest = sa.enter_context(nc.sbuf_tensor("ropest", [P, 4, 512], BF16))
        scrA = sa.enter_context(nc.sbuf_tensor("scrA", [P, 8, 16], F32))
        scrB = sa.enter_context(nc.sbuf_tensor("scrB", [P, 8, 16], F32))
        qk2 = sa.enter_context(nc.sbuf_tensor("qk2", [P, 4, P], BF16))
        vstage = sa.enter_context(nc.sbuf_tensor("vstage", [P, 2, 4, 130], BF16))
        pq = [sa.enter_context(nc.psum_tensor(f"pq{i}", [P, 512], F32)) for i in range(4)]
        psT = [sa.enter_context(nc.psum_tensor(f"psT{i}", [P, 1024], BF16)) for i in range(2)]

        for sl in range(2):
            emit("vector", lambda e, s=sl: e.memset(vstage[:, s, :, :], 1.0), ev)

        def load_wgrp(og, slot, rel):
            rwait("gpsimd", rel)
            for dcc in range(DC):
                dma("gpsimd", wbuf[:, slot, dcc, :],
                    dwqkv[dcc * P:(dcc + 1) * P, og * 512:(og + 1) * 512], ldw[slot])
            return ldw[slot].n

        wld = [0, 0]
        wrel = [None, None]
        wld[0] = load_wgrp(0, 0, None)
        wld[1] = load_wgrp(1, 1, None)
        pq_rel = [None, None, None, None]
        qk2_rel = [None] * 4
        ropest_rel = [None, None, None, None]
        vstage_rel = [None, None]
        xbuf_rel = [None, None]
        psT_rel = [None, None]
        xntok_rel = [None, None]
        xnf_cp = [0] * TT
        pend = [None]
        jidx = [0]

        def emit_ln_tile(tt):
            par = tt % 2
            rwait("sync", xbuf_rel[par])
            lv = dma("sync", xbuf[:, par, :], dx[tt * P:(tt + 1) * P, :], ldx[par])
            wait("vector", ldx[par], lv)
            wait("scalar", ldx[par], lv)
            rwait("scalar", xntok_rel[par])
            xn_done = emit_ln(xbuf[:, par, :], sqscr[:, :], par, xntok[:, par, :])
            xbuf_rel[par] = (es, xn_done)
            return xn_done

        def emit_ln_tp(tt, xn_done):
            par = tt % 2
            wait("tensor", es, xn_done)
            for g in range(2):
                sp = g % 2
                rwait("tensor", psT_rel[sp])
                mv = 0
                for c in range(8):
                    dc = g * 8 + c
                    mv = emit("tensor", lambda e, d=dc, c=c, s=sp, p=par: e.transpose(
                        psT[s][:, c * P:(c + 1) * P],
                        xntok[:, p, d * P:(d + 1) * P], identb[:, :]), mm)
                wait("vector", mm, mv)
                dst = xnf[:, g * 8:(g + 1) * 8, tt * P:(tt + 1) * P]
                cv = emit("vector", lambda e, s=sp, dst=dst: e.tensor_copy(
                    dst, psT[s][:, :].rearrange("p (c f) -> p c f", c=8)), ev)
                psT_rel[sp] = (ev, cv)
            xntok_rel[par] = (mm, mm.n)
            xnf_cp[tt] = ev.n

        def mm_job(og, tt):
            par = jidx[0] % 4
            jidx[0] += 1
            wslot = og % 2
            wait("tensor", ldw[wslot], wld[wslot])
            wait("tensor", ev, xnf_cp[tt])
            for _r in (pq_rel[par] or []):
                rwait("tensor", _r)
            for dcc in range(DC):
                emit("tensor", lambda e, d=dcc, t=tt, w=wslot, p=par: e.matmul(
                    pq[p][:, :], xnf[:, d, t * P:(t + 1) * P], wbuf[:, w, d, :],
                    start=(d == 0), stop=False), None)
            mm_tile = emit("tensor", lambda e, o=og, p=par: e.matmul(
                pq[p][:, :], ones_sb[0:1, :], bqkv_sb[0:1, o * 512:(o + 1) * 512],
                start=False, stop=True), mm)
            if tt == TT - 1:
                wrel[wslot] = (mm, mm_tile)
            if pend[0] is not None:
                pend[0]()
                pend[0] = None
            if og < 8:
                isq = og < 4
                scr = q_scr if isq else k_scr
                fcbase = (og % 4) * 4
                pr = pq[par].rearrange("p (h d) -> p h d", h=8)
                st = ropest[:, par, :].rearrange("p (h d) -> p h d", h=8)
                cosb = cos_sb[:, tt, :][:, None, :].broadcast_to((P, 8, 16))
                sinb = sin_sb[:, tt, :][:, None, :].broadcast_to((P, 8, 16))
                wait("vector", mm, mm_tile)
                rwait("vector", ropest_rel[par])
                emit("vector", lambda e, pr=pr, c=cosb: e.tensor_mul(
                    scrA[:, :, :], pr[:, :, 0:16], c), None)
                emit("vector", lambda e, pr=pr, s=sinb: e.tensor_mul(
                    scrB[:, :, :], pr[:, :, 16:32], s), None)
                emit("vector", lambda e, st=st: e.tensor_sub(
                    st[:, :, 0:16], scrA[:, :, :], scrB[:, :, :]), None)
                emit("vector", lambda e, pr=pr, s=sinb: e.tensor_mul(
                    scrA[:, :, :], pr[:, :, 0:16], s), None)
                emit("vector", lambda e, pr=pr, c=cosb: e.tensor_mul(
                    scrB[:, :, :], pr[:, :, 16:32], c), None)
                emit("vector", lambda e, st=st: e.tensor_add(
                    st[:, :, 16:32], scrA[:, :, :], scrB[:, :, :]), None)
                rope_done = emit("vector", lambda e, st=st, pr=pr: e.tensor_copy(
                    st[:, :, 32:64], pr[:, :, 32:64]), ev)
                sp_cp = None
                pq_rel[par] = [(ev, rope_done)]

                def tp(par=par, tt=tt, scr=scr, fcbase=fcbase,
                       rope_done=rope_done):
                    sp = jidx[0] % 2
                    wait("tensor", ev, rope_done)
                    rwait("tensor", psT_rel[sp])
                    mv = 0
                    for c in range(4):
                        mv = emit("tensor", lambda e, c=c, s=sp, p=par: e.transpose(
                            psT[s][:, c * P:(c + 1) * P],
                            ropest[:, p, c * P:(c + 1) * P], identb[:, :]), mm)
                    wait("vector", mm, mv)
                    for c in range(4):
                        rwait("vector", qk2_rel[c])
                    cv = emit("vector", lambda e, s=sp: e.tensor_copy(
                        qk2[:, :, :], psT[s][:, 0:512].rearrange(
                            "p (c f) -> p c f", c=4)), ev)
                    psT_rel[sp] = (ev, cv)
                    wait("sync", ev, cv)
                    for c in range(4):
                        sv = dma("sync", scr[fcbase + c, :, tt * P:(tt + 1) * P],
                                 qk2[:, c, :], stqk[c])
                        qk2_rel[c] = (stqk[c], sv)
                    ropest_rel[par] = (mm, mm.n)
                pend[0] = tp
            else:
                hpbase = (og - 8) * 4
                vsl = jidx[0] % 2
                pr = pq[par].rearrange("p (h x d) -> p h x d", h=4, x=2)
                dstv = vstage[:, vsl, :, :].rearrange(
                    "p h (x e) -> p h x e", x=2)[:, :, :, 0:64]
                wait("vector", mm, mm_tile)
                rwait("vector", vstage_rel[vsl])
                cv = emit("vector", lambda e, pr=pr, d=dstv: e.tensor_copy(
                    d, pr[:, :, :, 0:64]), ev)
                pq_rel[par] = [(ev, cv)]
                wait("sync", ev, cv)
                last = 0
                for c in range(4):
                    last = dma("sync", v_scr[hpbase + c, tt, :, :],
                               vstage[:, vsl, c, :], stv[vsl])
                vstage_rel[vsl] = (stv[vsl], last)
            if tt == TT - 1 and og + 2 < 12:
                wld[wslot] = load_wgrp(og + 2, wslot, wrel[wslot])

        xn_pend = emit_ln_tile(0)
        for tt in range(TT):
            emit_ln_tp(tt, xn_pend)
            if tt + 1 < TT:
                xn_pend = emit_ln_tile(tt + 1)
            mm_job(0, tt)
            mm_job(1, tt)
        for og in range(2, 12):
            for tt in range(TT):
                mm_job(og, tt)
        if pend[0] is not None:
            pend[0]()
            pend[0] = None
        A_CTRS = [mm, ev, es] + stqk + stv
        flush()

    if phases < 2:
        stack.close()
        return nc
    with contextlib.ExitStack() as sb:
        qbuf = sb.enter_context(nc.sbuf_tensor("qbuf", [P, 3, 512], BF16))
        kbuf = sb.enter_context(nc.sbuf_tensor("kbuf", [P, 3, T], BF16))
        vbuf = sb.enter_context(nc.sbuf_tensor("vbuf", [P, 3, TT, 130], BF16))
        ebAB = sb.enter_context(nc.sbuf_tensor("ebAB", [P, 4, 1024], BF16))
        rdbuf = sb.enter_context(nc.sbuf_tensor("rdbuf", [P, 2, 512], BF16))
        repsb = sb.enter_context(nc.sbuf_tensor("repsb", [P, 2, 512], BF16))
        stg2 = sb.enter_context(nc.sbuf_tensor("stg2", [P, 2, 512], BF16))
        oc = sb.enter_context(nc.sbuf_tensor("oc", [P, 2, 2, 512], F32))
        attnout = sb.enter_context(nc.sbuf_tensor("attnout", [P, 2, FC, 512], BF16))
        wobuf = sb.enter_context(nc.sbuf_tensor("wobuf", [P, 2, DC, 512], BF16))
        xresb = sb.enter_context(nc.sbuf_tensor("xresb", [P, 2, 512], F32))
        o1st = sb.enter_context(nc.sbuf_tensor("o1st", [P, 2, 512], F32))
        sAB = [sb.enter_context(nc.psum_tensor(f"sAB{i}", [P, 1024], F32)) for i in range(2)]
        oA = sb.enter_context(nc.psum_tensor("oA", [P, 512], F32))
        oB = sb.enter_context(nc.psum_tensor("oB", [P, 512], F32))
        pw = [sb.enter_context(nc.psum_tensor(f"pw{i}", [P, 512], F32)) for i in range(2)]

        barrier(A_CTRS)

        q_rel = [None, None, None]
        k_rel = [None, None, None]
        v_rel = [None, None, None]
        kld = [0, 0, 0]
        vld = [0, 0, 0]
        qld = [0, 0, 0]
        sAB_rel = [None, None]
        eb_rel = [None, None, None, None]
        o_rel = [None]
        rb_rel = [None, None]
        rd_rel = [None]
        oc_rel = [None, None]
        att_pend = [None]
        stg2_rel = [None, None]
        wout_psum_rel = [None, None]
        wo_rel = [None, None]
        wo_pref = [0, 0]
        xres_rel = [None, None]
        o1st_rel = [None, None]
        units_done = {-2: 0, -1: 0}
        ux1_done = {}
        wout_units = []
        wo_ld = [0, 0]

        def load_job(g):
            jqg, jfc = g // FC, g % FC
            jnkt = 4 * (jqg + 1)
            s = g % 3
            rwait("sync", q_rel[s])
            qld[s] = dma("sync", qbuf[:, s, :],
                         q_scr[jfc, :, jqg * 512:(jqg + 1) * 512], ldq[s])
            rwait("sync", k_rel[s])
            kld[s] = dma("sync", kbuf[:, s, 0:jnkt * P],
                         k_scr[jfc, :, 0:jnkt * P], ldk[s])
            rwait("sync", v_rel[s])
            h1 = (jnkt + 1) // 2
            dma("sync", vbuf[:, s, 0:h1, :],
                v_scr[jfc, 0:h1, :, :].transpose([1, 0, 2]), ldv[s])
            vld[s] = dma("sync", vbuf[:, s, h1:jnkt, :],
                         v_scr[jfc, h1:jnkt, :, :].transpose([1, 0, 2]), ldv[s])

        load_job(0)
        load_job(1)
        for qg in range(NQG):
            nkt = 4 * (qg + 1)
            for fc in range(FC):
                g = qg * FC + fc
                qsl = g % 3
                if g + 2 < NQG * FC:
                    load_job(g + 2)
                qv = qld[qsl]
                qflat = qbuf[:, qsl, :]
                av_q = []
                sm = {}
                for kt in range(nkt):
                    ks = qsl
                    ssl = kt % 2
                    diag = kt >= 4 * qg
                    off = P * (kt - 4 * qg) if diag else 0
                    wait("tensor", ldk[ks], kld[ks])
                    wait("tensor", ldq[qsl], qv)
                    rwait("tensor", sAB_rel[ssl])
                    kap = kbuf[:, ks, kt * P:(kt + 1) * P]
                    if diag:
                        emit("tensor", lambda e, k=kap, s=ssl, o=off, q=qflat: e.matmul(
                            sAB[s][:, o:512], k[0:64, :], q[0:64, o:512],
                            start=True, stop=False, skip_group_check=True), None)
                        emit("tensor", lambda e, s=ssl, o=off: e.matmul(
                            sAB[s][:, o:o + P], identb[:, :], maskb[:, :],
                            start=False, stop=True, skip_group_check=True), None)
                        emit("tensor", lambda e, k=kap, s=ssl, o=off, q=qflat: e.matmul(
                            sAB[s][:, 512 + o:1024], k[64:128, :],
                            q[64:128, o:512],
                            start=True, stop=False, skip_group_check=True), None)
                        sm[kt] = emit("tensor", lambda e, s=ssl, o=off: e.matmul(
                            sAB[s][:, 512 + o:512 + o + P], identb[:, :], maskb[:, :],
                            start=False, stop=True, skip_group_check=True), mm)
                    else:
                        emit("tensor", lambda e, k=kap, s=ssl, q=qflat: e.matmul(
                            sAB[s][:, 0:512], k[0:64, :], q[0:64, :],
                            start=True, stop=True, skip_group_check=True), None)
                        sm[kt] = emit("tensor", lambda e, k=kap, s=ssl, q=qflat: e.matmul(
                            sAB[s][:, 512:1024], k[64:128, :], q[64:128, :],
                            start=True, stop=True, skip_group_check=True), mm)
                    if kt % 2 == 0:
                        continue
                    for k2 in (kt - 1, kt):
                        s2 = k2 % 2
                        d2 = k2 >= 4 * qg
                        o2 = P * (k2 - 4 * qg) if d2 else 0
                        e4 = k2 % 4
                        wait("scalar", mm, sm[k2])
                        rwait("scalar", eb_rel[e4])
                        if d2:
                            emit("scalar", lambda e, s=s2, b=e4, o=o2: e.activation(
                                ebAB[:, b, o:512], sAB[s][:, o:512], AF.Exp,
                                bias=C_NEG4, scale=C_8TH), es)
                            exAB = emit("scalar", lambda e, s=s2, b=e4, o=o2: e.activation(
                                ebAB[:, b, 512 + o:1024], sAB[s][:, 512 + o:1024],
                                AF.Exp, bias=C_NEG4, scale=C_8TH), es)
                        else:
                            exAB = emit("scalar", lambda e, s=s2, b=e4: e.activation(
                                ebAB[:, b, 0:1024], sAB[s][:, 0:1024], AF.Exp,
                                bias=C_NEG4, scale=C_8TH), es)
                        sAB_rel[s2] = (es, exAB)

                        def av_emit(kt=k2, off=o2, exAB=exAB, vs=qsl,
                                    first=(k2 == 0), last=(k2 == nkt - 1)):
                            ebs = kt % 4
                            wait("tensor", es, exAB)
                            wait("tensor", ldv[vs], vld[vs])
                            if first:
                                rwait("tensor", o_rel[0])
                            emit("tensor", lambda e, kt=kt, vs=vs, ebs=ebs, off=off,
                                 first=first, last=last: e.matmul(
                                oA[0:65, off:512], vbuf[:, vs, kt, 0:65],
                                ebAB[:, ebs, off:512],
                                start=first, stop=last, skip_group_check=True), None)
                            r2 = emit("tensor", lambda e, kt=kt, vs=vs, ebs=ebs, off=off,
                                      first=first, last=last: e.matmul(
                                oB[0:65, off:512], vbuf[:, vs, kt, 65:130],
                                ebAB[:, ebs, 512 + off:1024],
                                start=first, stop=last, skip_group_check=True), mm)
                            eb_rel[ebs] = (mm, r2)
                            return r2
                        av_q.append(av_emit)
                    while len(av_q) > 2:
                        av_q.pop(0)()
                    if kt == 1 and att_pend[0] is not None:
                        att_pend[0]()
                        att_pend[0] = None
                    if kt == 3 and wout_units and fc >= 4:
                        wout_units.pop(0)()
                        if fc in (6, 8, 10, 12) and wout_units:
                            wout_units.pop(0)()
                while av_q:
                    av_q.pop(0)()
                av_last = mm.n
                q_rel[qsl] = (mm, av_last)
                k_rel[qsl] = (mm, av_last)
                v_rel[qsl] = (mm, av_last)
                ocs = fc % 2
                wait("scalar", mm, av_last)
                rwait("scalar", oc_rel[ocs])
                emit("scalar", lambda e, s=ocs: e.activation(
                    oc[0:65, s, 0, :], oA[0:65, :], AF.Identity), None)
                oc_done = emit("scalar", lambda e, s=ocs: e.activation(
                    oc[0:65, s, 1, :], oB[0:65, :], AF.Identity), es)
                o_rel[0] = (es, oc_done)

                def attnorm(fc=fc, ocs=ocs, oc_done=oc_done, aslot=qg % 2,
                            slot_free=units_done[qg - 2]):
                    rsl = fc % 2
                    wait("vector", es, oc_done)
                    rwait("vector", rb_rel[rsl])
                    rwait("vector", rd_rel[0])

                    def _recip(e, dst, src):
                        with nc.allow_low_precision(
                                reason="softmax denominator reciprocal stored "
                                       "bf16 as a broadcast operand"):
                            return e.reciprocal(dst, src)
                    emit("vector", lambda e, s=ocs: _recip(
                        e, rdbuf[64:65, 0, :], oc[64:65, s, 0, :]), ev)
                    rb = emit("vector", lambda e, s=ocs: _recip(
                        e, rdbuf[64:65, 1, :], oc[64:65, s, 1, :]), ev)
                    wait("gpsimd", ev, rb)
                    rwait("gpsimd", rb_rel[rsl])
                    sv1 = dma("gpsimd", rb_scr[rsl:rsl + 1, :],
                              rdbuf[64:65, 0:2, :].rearrange("p a b -> p (a b)"),
                              strb[rsl])
                    rd_rel[0] = (strb[rsl], sv1)
                    wait("gpsimd", strb[rsl], sv1)
                    bv = dma("gpsimd", repsb[0:64, :, :],
                             rb_scr[rsl:rsl + 1, :].broadcast_to((64, 1024)),
                             ldrb[rsl])
                    rb_rel[rsl] = (ldrb[rsl], bv)
                    wait("vector", ldrb[rsl], bv)
                    wait("vector", mm, slot_free)
                    emit("vector", lambda e, f=fc, a=aslot, s=ocs: e.tensor_mul(
                        attnout[0:64, a, f, :], oc[0:64, s, 0, :],
                        repsb[0:64, 0, :]), ev)
                    h2sl = fc % 2
                    rwait("vector", stg2_rel[h2sl])
                    mub = emit("vector", lambda e, s=h2sl, c=ocs: e.tensor_mul(
                        stg2[0:64, s, :], oc[0:64, c, 1, :], repsb[0:64, 1, :]), ev)
                    oc_rel[ocs] = (ev, mub)
                    wait("sync", ev, mub)
                    wait("sync", mm, slot_free)
                    sv = dma("sync", attnout[64:128, aslot, fc, :],
                             stg2[0:64, h2sl, :], sth2[h2sl])
                    stg2_rel[h2sl] = (sth2[h2sl], sv)
                att_pend[0] = attnorm
            for og_p in range(2):
                rwait("gpsimd", wo_rel[og_p])
                for dcc in range(DC):
                    dma("gpsimd", wobuf[:, og_p, dcc, :],
                        dwout[dcc * P:(dcc + 1) * P,
                              og_p * 512:(og_p + 1) * 512], ldwo[og_p])
                wo_pref[og_p] = ldwo[og_p].n
            if att_pend[0] is not None:
                att_pend[0]()
                att_pend[0] = None
            att_ev = ev.n
            att_h2 = [sth2[0].n, sth2[1].n]

            def wout_unit(qg=qg, u=None, att_ev=att_ev, att_h2=att_h2,
                          aslot=qg % 2):
                og, ts = u // 4, u % 4
                wsl = og % 2
                if og < 2:
                    wov = wo_pref[wsl]
                else:
                    wov = wo_ld[wsl]
                psl = ts % 2
                pwt = pw[psl][:, :]
                wait("tensor", ldwo[wsl], wov)
                wait("tensor", ev, att_ev)
                wait("tensor", sth2[0], att_h2[0])
                wait("tensor", sth2[1], att_h2[1])
                rwait("tensor", wout_psum_rel[psl])
                xsl = u % 2
                rwait("gpsimd", xres_rel[xsl])
                xr = dma("gpsimd", xresb[:, xsl, :],
                         dx[qg * 512 + ts * P: qg * 512 + (ts + 1) * P,
                            og * 512:(og + 1) * 512], ldxr[xsl])
                for dcc in range(DC):
                    emit("tensor", lambda e, d=dcc, t=ts, w=wsl, ps=pwt, a=aslot:
                         e.matmul(ps, attnout[:, a, d, t * P:(t + 1) * P],
                                  wobuf[:, w, d, :],
                                  start=(d == 0), stop=False), None)
                wo_mm = emit("tensor", lambda e, o=og, ps=pwt: e.matmul(
                    ps, ones_sb[0:1, :], bout_sb[0:1, o * 512:(o + 1) * 512],
                    start=False, stop=True), mm)
                osl = u % 2
                wait("vector", mm, wo_mm)
                wait("vector", ldxr[xsl], xr)
                rwait("vector", o1st_rel[osl])
                o1v = emit("vector", lambda e, ps=pwt, x=xsl, s=osl: e.tensor_add(
                    o1st[:, s, :], ps, xresb[:, x, :]), ev)
                wout_psum_rel[psl] = (ev, o1v)
                xres_rel[xsl] = (ev, o1v)
                wait("gpsimd", ev, o1v)
                sv = dma("gpsimd",
                         x1_scr[qg * 512 + ts * P: qg * 512 + (ts + 1) * P,
                                og * 512:(og + 1) * 512],
                         o1st[:, osl, :], stx1[osl])
                o1st_rel[osl] = (stx1[osl], sv)
                if u == 3 or u == 7:
                    og2 = 2 if u == 3 else 3
                    wsl2 = og2 % 2
                    wo_rel[wsl2] = (mm, mm.n)
                    rwait("gpsimd", wo_rel[wsl2])
                    for dcc in range(DC):
                        dma("gpsimd", wobuf[:, wsl2, dcc, :],
                            dwout[dcc * P:(dcc + 1) * P, og2 * 512:(og2 + 1) * 512],
                            ldwo[wsl2])
                    wo_ld[wsl2] = ldwo[wsl2].n
                if u == 11:
                    wo_rel[0] = (mm, mm.n)
                if u == 15:
                    wo_rel[1] = (mm, mm.n)
                    units_done[qg] = mm.n
                    ux1_done[qg] = (stx1[0].n, stx1[1].n)

            wout_units = [
                (lambda u=u, f=wout_unit: f(u=u)) for u in range(16)]
            if qg == NQG - 1:
                while wout_units:
                    wout_units.pop(0)()
        ln2_stats(0, 0, st_wait=ux1_done[0])
        ln2_stats(0, 1, st_wait=ux1_done[0])
        B_CTRS = [mm, ev, es] + stx1 + sth2
        flush()

    if phases < 2.1:
        stack.close()
        return nc
    with contextlib.ExitStack() as sc:
        xn2f = sc.enter_context(nc.sbuf_tensor("xn2f", [P, DC, 512], BF16))
        wgub = sc.enter_context(nc.sbuf_tensor("wgub", [P, 2, DC, 512], BF16))
        wdb = wgub
        h_sb = sc.enter_context(nc.sbuf_tensor("h_sb", [P, 64, 512], BF16))
        hsr = sc.enter_context(nc.sbuf_tensor("hsr", [P, 512], F32))
        hs2r = sc.enter_context(nc.sbuf_tensor("hs2r", [P, 512], F32))
        hs2c = sc.enter_context(nc.sbuf_tensor("hs2c", [P, 4], F32))
        gl = sc.enter_context(nc.sbuf_tensor("gl", [P, 2, 512], F32))
        sil = sc.enter_context(nc.sbuf_tensor("sil", [P, 4, 512], F32))
        usb = sc.enter_context(nc.sbuf_tensor("usb", [P, 2, 512], F32))
        x1b = sc.enter_context(nc.sbuf_tensor("x1b", [P, 2, 512], F32))
        outst = sc.enter_context(nc.sbuf_tensor("outst", [P, 2, 512], F32))
        pg = [sc.enter_context(nc.psum_tensor(f"pg{i}", [P, 512], F32)) for i in range(2)]
        pu = [sc.enter_context(nc.psum_tensor(f"pu{i}", [P, 512], F32)) for i in range(2)]
        psh = sc.enter_context(nc.psum_tensor("psh", [P, 512], F32))
        ptp = [sc.enter_context(nc.psum_tensor(f"ptp{i}", [P, 1024], BF16)) for i in range(2)]
        pht = sc.enter_context(nc.psum_tensor("pht", [P, 512], F32))

        barrier(B_CTRS)

        psT_rel2 = [None, None]
        gu_rel = [None, None]
        gu_ld = [0, 0]
        wd_rel = [None, None]
        psum_rel = [None, None, None, None]
        gl_rel = [None, None]
        sil_rel = [None, None, None, None]
        usb_rel = [None, None]
        h_pe_done = 0
        pht_rel = None
        outst_rel = [None, None]
        x1b_rel = [None, None]
        hs2_es_prev = [0]

        guf_prev = [0]
        lnhs = {}

        def ln2_tp(tg, ts):
            par = ts % 2
            wait("vector", mm, guf_prev[0])
            wait("tensor", es, ln_state[(tg, ts)])
            for g in range(2):
                sp = g % 2
                rwait("tensor", psT_rel2[sp])
                mv = 0
                for c in range(8):
                    dcc = g * 8 + c
                    mv = emit("tensor", lambda e, d=dcc, c=c, s=sp, p=par: e.transpose(
                        ptp[s][:, c * P:(c + 1) * P],
                        xn2tok[:, p, d * P:(d + 1) * P], identb[:, :]), mm)
                wait("vector", mm, mv)
                dst = xn2f[:, g * 8:(g + 1) * 8, ts * P:(ts + 1) * P]
                cv = emit("vector", lambda e, s=sp, dst=dst: e.tensor_copy(
                    dst, ptp[s][:, :].rearrange("p (c f) -> p c f", c=8)), ev)
                psT_rel2[sp] = (ev, cv)
            xn2tok_rel[par] = (mm, mm.n)

        def emit_hs(tg):
            xn2_ev = ev.n
            wait("tensor", ev, xn2_ev)
            wait("tensor", es, hs2_es_prev[0])
            for dcc in range(DC):
                hs_mm = emit("tensor", lambda e, d=dcc: e.matmul(
                    psh[:, :], ones_sb[:, :], xn2f[:, d, :],
                    start=(d == 0), stop=(d == DC - 1)),
                    mm if dcc == DC - 1 else None)
            wait("scalar", mm, hs_mm)
            wait("scalar", ev, ev.n)
            hs_es = emit("scalar", lambda e: e.activation(
                hsr[:, :], psh[:, :], AF.Identity, scale=C_NEG128), es)
            lnhs[tg] = (xn2_ev, hs_es)

        ln2_tp(0, 0)
        ln2_stats(0, 2)
        ln2_tp(0, 1)
        ln2_stats(0, 3)
        ln2_tp(0, 2)
        ln2_tp(0, 3)
        emit_hs(0)
        gu_prefetched = [False]
        for tg in range(NQG):
            xn2_ev, hs_es = lnhs[tg]
            hs2_es_prev[0] = hs_es
            wait("vector", mm, h_pe_done)

            def load_gu(kind, ib, slot, rel):
                rwait("gpsimd", rel)
                src = dwg if kind == 0 else dwu
                for dcc in range(DC):
                    dma("gpsimd", wgub[:, slot, dcc, :],
                        src[dcc * P:(dcc + 1) * P, ib * 512:(ib + 1) * 512], ldgu[slot])
                return ldgu[slot].n

            seq = [(k, ib) for ib in range(16) for k in (0, 1)]
            sil_done = [0, 0, 0, 0]
            if not gu_prefetched[0]:
                gu_ld[0] = load_gu(*seq[0], 0, gu_rel[0])
            gu_prefetched[0] = False
            gu_ld[1] = load_gu(*seq[1], 1, gu_rel[1])
            gu_use = [0, 0]
            for si, (kind, ib) in enumerate(seq):
                slot = si % 2
                wait("tensor", ldgu[slot], gu_ld[slot])
                for icl in range(4):
                    ic = ib * 4 + icl
                    ppar = icl % 2
                    pslot = ppar if kind == 0 else 2 + ppar
                    ps = pg[ppar] if kind == 0 else pu[ppar]
                    rwait("tensor", psum_rel[pslot])
                    for dcc in range(DC):
                        mmv = emit("tensor", lambda e, d=dcc, i=icl, s=slot, ps=ps: e.matmul(
                            ps[:, :], wgub[:, s, d, i * P:(i + 1) * P], xn2f[:, d, :],
                            start=(d == 0), stop=(d == DC - 1)),
                            mm if dcc == DC - 1 else None)
                    if kind == 0:
                        wait("vector", mm, mmv)
                        wait("vector", es, hs_es)
                        rwait("vector", gl_rel[ppar])
                        gv = emit("vector", lambda e, ps=ps, p=ppar: e.tensor_add(
                            gl[:, p, :], ps[:, :], hsr[:, :]), ev)
                        psum_rel[pslot] = (ev, gv)
                        wait("scalar", ev, gv)
                        rwait("scalar", sil_rel[icl])
                        sv = emit("scalar", lambda e, p=ppar, il=icl, ic=ic: e.activation(
                            sil[:, il, :], gl[:, p, :], AF.Silu,
                            scale=gs_sb[:, ic:ic + 1]), es)
                        gl_rel[ppar] = (es, sv)
                        sil_done[icl] = sv
                    else:
                        wait("vector", mm, mmv)
                        rwait("vector", usb_rel[ppar])
                        uv = emit("vector", lambda e, ps=ps, p=ppar: e.tensor_add(
                            usb[:, p, :], ps[:, :], hsr[:, :]), ev)
                        psum_rel[pslot] = (ev, uv)
                        wait("vector", es, sil_done[icl])
                        hv = emit("vector", lambda e, p=ppar, il=icl, ic=ic: e.scalar_tensor_tensor(
                            h_sb[:, ic, :], sil[:, il, :], us_sb[:, ic:ic + 1],
                            usb[:, p, :], ALU.mult, ALU.mult), ev)
                        sil_rel[icl] = (ev, hv)
                        usb_rel[ppar] = (ev, hv)
                gu_use[slot] = mm.n
                if si + 2 < len(seq):
                    gu_rel[slot] = (mm, mm.n)
                    gu_ld[slot] = load_gu(*seq[si + 2], slot, gu_rel[slot])
            h_ev = ev.n
            guf_mm = mm.n
            guf_prev[0] = guf_mm
            if dbg and tg == 0:
                wait("sync", ev, h_ev)
                wait("sync", es, hs_es)
                dma("sync", dxn2_dbg[:, :, :], xn2f[:, :, :], pre)
                dma("sync", dhsr_dbg[:, :], hsr[:, :], pre)
                dma("sync", dh_dbg[:, :, :], h_sb[:, :, :], pre)
                wait("vector", pre, pre.n)
                wait("tensor", pre, pre.n)
                wait("scalar", pre, pre.n)
            wait("tensor", ev, h_ev)
            wait("tensor", es, hs_es)
            for ic in range(64):
                hs2_mm = emit("tensor", lambda e, i=ic: e.matmul(
                    psh[:, :], ones_sb[:, :], h_sb[:, i, :],
                    start=(i == 0), stop=(i == 63)),
                    mm if ic == 63 else None)
            wait("scalar", mm, hs2_mm)
            wait("scalar", ev, ev.n)
            hs2_es = emit("scalar", lambda e: e.activation(
                hs2r[:, :], psh[:, :], AF.Identity, scale=C_NEG128), es)
            hs2_es_prev[0] = hs2_es
            wait("tensor", es, hs2_es)
            rwait("tensor", pht_rel)
            mv = 0
            for ts in range(4):
                mv = emit("tensor", lambda e, t=ts: e.transpose(
                    pht[:, t * P:(t + 1) * P], hs2r[:, t * P:(t + 1) * P],
                    identf[:, :]), mm)
            wait("vector", mm, mv)
            cpv = emit("vector", lambda e: e.tensor_copy(
                hs2c[:, 0:4],
                bass.AP(pht.tensor if hasattr(pht, 'tensor') else pht, 0,
                        [[512, P], [P, 4], [1, 1]])), ev)
            pht_rel = (ev, cpv)
            hs2c_ev = ev.n
            if tg + 1 < NQG:
                ln2_stats(tg + 1, 0)
                ln2_stats(tg + 1, 1)
            for hg in range(4):
                for q4 in range(4):
                    wsl = (hg * 4 + q4) % 2
                    rwait("gpsimd", wd_rel[wsl])
                    wait("gpsimd", mm, gu_use[wsl])
                    for icl in range(16):
                        i = q4 * 16 + icl
                        dma("gpsimd", wdb[:, wsl, icl, :],
                            dwd[i * P:(i + 1) * P, hg * 512:(hg + 1) * 512], ldwd[wsl])
                    wv = ldwd[wsl].n
                    wait("tensor", ldwd[wsl], wv)
                    q4last = 0
                    for ts in range(4):
                        pd = (pg + pu)[ts]
                        if q4 == 0:
                            rwait("tensor", psum_rel[ts])
                        for icl in range(16):
                            i = q4 * 16 + icl
                            q4last = emit("tensor",
                                lambda e, i=i, il=icl, w=wsl, t=ts, pd=pd: e.matmul(
                                    pd[:, :], h_sb[:, i, t * P:(t + 1) * P],
                                    wdb[:, w, il, :],
                                    start=(i == 0), stop=(i == 63)),
                                mm if (ts == 3 and icl == 15) else None)
                    wd_rel[wsl] = (mm, q4last)
                    if hg == 3 and q4 == 3 and tg + 1 < NQG:
                        gu_ld[0] = load_gu(0, 0, 0, wd_rel[0])
                        gu_prefetched[0] = True
                if hg == 3 and tg + 1 < NQG:
                    emit_hs(tg + 1)
                dn_mm = mm.n
                for ts in range(4):
                    pd = (pg + pu)[ts]
                    osl = ts % 2
                    xsl = ts % 2
                    tt = tg * 4 + ts
                    rwait("sync", x1b_rel[xsl])
                    xv = dma("sync", x1b[:, xsl, :],
                             x1_scr[tt * P:(tt + 1) * P, hg * 512:(hg + 1) * 512],
                             ldx1b[xsl])
                    wait("vector", mm, dn_mm)
                    wait("vector", ev, hs2c_ev)
                    rwait("vector", outst_rel[osl])
                    emit("vector", lambda e, pd=pd, t=ts, hg=hg, o=osl: e.scalar_tensor_tensor(
                        outst[:, o, :], pd[:, :], hs2c[:, t:t + 1],
                        ds_sb[:, hg * 512:(hg + 1) * 512], ALU.add, ALU.mult), None)
                    wait("vector", ldx1b[xsl], xv)
                    ov = emit("vector", lambda e, o=osl, x=xsl: e.tensor_add(
                        outst[:, o, :], outst[:, o, :], x1b[:, x, :]), ev)
                    x1b_rel[xsl] = (ev, ov)
                    psum_rel[ts] = (ev, ov)
                    wait("scalar", ev, ov)
                    sv = dma("scalar",
                             dout[tt * P:(tt + 1) * P, hg * 512:(hg + 1) * 512],
                             outst[:, osl, :], stout[osl])
                    outst_rel[osl] = (stout[osl], sv)
                if tg + 1 < NQG:
                    if hg == 0:
                        ln2_tp(tg + 1, 0)
                        ln2_stats(tg + 1, 2)
                    elif hg == 1:
                        ln2_tp(tg + 1, 1)
                        ln2_stats(tg + 1, 3)
                        ln2_tp(tg + 1, 2)
                    elif hg == 2:
                        ln2_tp(tg + 1, 3)
            h_pe_done = mm.n
            gu_rel = [(mm, h_pe_done), (mm, h_pe_done)]

        wait("sync", stout[0], stout[0].n)
        wait("sync", stout[1], stout[1].n)
        wait("gpsimd", stout[0], stout[0].n)
        wait("gpsimd", stout[1], stout[1].n)
        flush()

    stack.close()
    return nc


_CACHE = {}


def kernel(L0_attn_x, L0_attn_Wqkv, L0_attn_b_qkv, L0_attn_Wout, L0_attn_b_out,
           L0_ffn_gate_w, L0_ffn_gate_scale, L0_ffn_up_w, L0_ffn_up_scale,
           L0_ffn_down_w, L0_ffn_down_scale):
    x = np.ascontiguousarray(np.asarray(L0_attn_x, dtype=np.float32))
    wqkv = np.ascontiguousarray(np.asarray(L0_attn_Wqkv, dtype=np.float32))
    bqkv = np.ascontiguousarray(np.asarray(L0_attn_b_qkv, dtype=np.float32)).reshape(1, OF)
    wout = np.ascontiguousarray(np.asarray(L0_attn_Wout, dtype=np.float32))
    bout = np.ascontiguousarray(np.asarray(L0_attn_b_out, dtype=np.float32)).reshape(1, DM)
    wg_t = np.ascontiguousarray(np.asarray(L0_ffn_gate_w).T).astype(np.uint8)
    wu_t = np.ascontiguousarray(np.asarray(L0_ffn_up_w).T).astype(np.uint8)
    wd_t = np.ascontiguousarray(np.asarray(L0_ffn_down_w).T).astype(np.uint8)
    gs = np.ascontiguousarray(
        np.asarray(L0_ffn_gate_scale, dtype=np.float32).reshape(64, P).T)
    us = np.ascontiguousarray(
        np.asarray(L0_ffn_up_scale, dtype=np.float32).reshape(64, P).T)
    ds_rep = np.ascontiguousarray(
        np.tile(np.asarray(L0_ffn_down_scale, dtype=np.float32).reshape(1, DM), (P, 1)))
    inv = (1.0 / 10000.0 ** (np.arange(0, 32, 2, dtype=np.float32) / np.float32(32.0)))
    tpos = np.arange(T, dtype=np.float32)
    fr = np.outer(tpos, inv).astype(np.float32)
    cos_t = np.cos(fr).astype(np.float32).reshape(TT, P, 16)
    sin_t = np.sin(fr).astype(np.float32).reshape(TT, P, 16)
    ident = np.eye(P, dtype=np.float32)
    kk = np.arange(P)[:, None]
    qq = np.arange(P)[None, :]
    maskd = np.where(kk > qq, np.float32(-10000.0), np.float32(0.0)).astype(np.float32)

    if "nc" not in _CACHE:
        _CACHE["nc"] = _build()
    nc = _CACHE["nc"]

    shared = dict(wqkv=wqkv, bqkv=bqkv, wout=wout, bout=bout,
                  wg_t=wg_t, wu_t=wu_t, wd_t=wd_t, gs=gs, us=us, ds_rep=ds_rep,
                  cos_t=cos_t, sin_t=sin_t, ident=ident, maskd=maskd)
    in_maps = [dict(x=np.ascontiguousarray(x[b]), **shared) for b in range(NCORES)]
    res = run_bass_kernel_spmd(nc, in_maps, core_ids=list(range(NCORES)))
    return np.stack([np.asarray(res.results[b]["out"]) for b in range(NCORES)], axis=0)
